# version 01
# baseline (speedup 1.0000x reference)
import numpy as np
from contextlib import ExitStack

import concourse.bass as bass
import concourse.mybir as mybir
from concourse.bass_utils import run_bass_kernel_spmd
from concourse.alu_op_type import AluOpType

F32 = mybir.dt.float32
BF16 = mybir.dt.bfloat16
AF = mybir.ActivationFunctionType
OP = AluOpType

NCORES = 8
B, S, D = 4, 4096, 1024
TOK = 2048
NC_D = D // 128
DH = 64
EPS_LN = 1e-5

ENGS = ("tensor", "vector", "scalar", "gpsimd", "sync")
RACE_DETECT = False


class Ctx:

    def __init__(self, E, eng, sems):
        self.E = E
        self.eng = eng
        self.sems = sems
        self.cnt = {k: 0 for k in sems}

    def on(self, E):
        return self.E == E

    def inc(self, name, n=1):
        self.cnt[name] += n

    def tag(self, inst, name, n=1):
        if inst is not None:
            inst.then_inc(self.sems[name], n)
        self.cnt[name] += n

    def wait(self, E, name, val=None):
        if self.E == E:
            self.eng.wait_ge(self.sems[name], self.cnt[name] if val is None else val)

    def dma(self, E, sem, fn, n=16):
        if self.E == E:
            fn(self.eng).then_inc(self.sems[sem], n)
        self.cnt[sem] += n

    def op(self, E, sem, fn, n=1):
        if self.E == E:
            fn(self.eng).then_inc(self.sems[sem], n)
        self.cnt[sem] += n

    def run(self, E, fn):
        if self.E == E:
            fn(self.eng)

    def tag_fenced(self, E, inst, name, n=1):
        self.fence(E, inst)
        if self.E == E:
            i = self.eng.engine_nop()
            i.then_inc(self.sems[name], n)
        self.cnt[name] += n

    def fence(self, E, inst):
        name = "f" + E[0]
        if inst is not None:
            inst.then_inc(self.sems[name], 1)
        self.cnt[name] += 1
        if self.E == E:
            self.eng.wait_ge(self.sems[name], self.cnt[name])


def run_program(nc, names, prog):
    with ExitStack() as es:
        sems = {n: es.enter_context(nc.semaphore(n)) for n in names}
        block = es.enter_context(nc.Block())
        for E in ENGS:
            def mk(E):
                def _(eng):
                    prog(Ctx(E, eng, sems))
                return _
            getattr(block, E)(mk(E))



def alloc_ln_scratch(es, nc, NC, T, tag=""):
    t = {}
    t["x2t"] = es.enter_context(nc.sbuf_tensor(f"ln_x2t{tag}", [128, NC, 512], F32))
    t["srow"] = es.enter_context(nc.sbuf_tensor(f"ln_srow{tag}", [1, T], F32))
    t["qrow"] = es.enter_context(nc.sbuf_tensor(f"ln_qrow{tag}", [1, T], F32))
    t["rrow"] = es.enter_context(nc.sbuf_tensor(f"ln_rrow{tag}", [1, T], F32))
    t["mrow"] = es.enter_context(nc.sbuf_tensor(f"ln_mrow{tag}", [1, T], F32))
    t["ps_s"] = es.enter_context(nc.psum_tensor(f"ln_ps_s{tag}", [1, 512], F32))
    t["ps_q"] = es.enter_context(nc.psum_tensor(f"ln_ps_q{tag}", [1, 512], F32))
    t["ps_R"] = es.enter_context(nc.psum_tensor(f"ln_ps_R{tag}", [128, 512], F32))
    t["ps_M"] = es.enter_context(nc.psum_tensor(f"ln_ps_M{tag}", [128, 512], F32))
    t["onesf"] = es.enter_context(nc.sbuf_tensor(f"ln_onesf{tag}", [128, 128], F32))
    t["onesb"] = es.enter_context(nc.sbuf_tensor(f"ln_onesb{tag}", [1, 128], BF16))
    t["epsr"] = es.enter_context(nc.sbuf_tensor(f"ln_epsr{tag}", [1, 1], F32))
    return t


def emit_ln_const_init(C, t, eps=EPS_LN):
    if C.on("vector"):
        C.eng.memset(t["onesf"][:, :], 1.0)
        C.eng.memset(t["onesb"][:, :], 1.0)
        i = C.eng.memset(t["epsr"][:, :], eps)
        C.tag(i, "v")
    else:
        C.inc("v")


def emit_ln_fm(C, t, X, T, apply, rms=False, eps=EPS_LN, shift=None):
    NC = X.shape[1]
    NT = T // 512
    C.wait("tensor", "v")
    C.wait("scalar", "v")
    for ti in range(NT):
        sl = slice(ti * 512, (ti + 1) * 512)
        C.wait("scalar", "ld")
        if C.on("scalar"):
            i = C.eng.activation(out=t["x2t"][:, :, :], in_=X[:, :, sl],
                                 func=AF.Square)
            C.tag(i, "s")
        else:
            C.inc("s")
        C.wait("tensor", "s")
        C.wait("tensor", "ld")
        if C.on("tensor"):
            if not rms:
                for c in range(NC):
                    C.eng.matmul(t["ps_s"][:, :], t["onesf"][:, 0:1], X[:, c, sl],
                                 start=(c == 0), stop=(c == NC - 1))
            for c in range(NC):
                i = C.eng.matmul(t["ps_q"][:, :], t["onesf"][:, 0:1],
                                 t["x2t"][:, c, :], start=(c == 0),
                                 stop=(c == NC - 1))
            C.tag(i, "m")
        else:
            C.inc("m")
        C.wait("vector", "m")
        if C.on("vector"):
            if not rms:
                C.eng.tensor_copy(t["srow"][:, sl], t["ps_s"][:, :])
            i = C.eng.tensor_copy(t["qrow"][:, sl], t["ps_q"][:, :])
        else:
            i = None
        C.tag_fenced("vector", i, "v")
        C.wait("tensor", "v")
        C.wait("scalar", "v")
    Dn = float(X.shape[0] * NC)
    if not rms:
        if C.on("vector"):
            C.eng.tensor_scalar(t["srow"][:, 0:T], t["srow"][:, 0:T], 1.0 / Dn,
                                None, OP.mult)
            C.eng.tensor_tensor(t["mrow"][:, 0:T], t["srow"][:, 0:T],
                                t["srow"][:, 0:T], OP.mult)
            i = C.eng.scalar_tensor_tensor(t["qrow"][:, 0:T], t["qrow"][:, 0:T],
                                           1.0 / Dn, t["mrow"][:, 0:T],
                                           OP.mult, OP.subtract)
            C.tag(i, "v")
        else:
            C.inc("v")
        C.wait("scalar", "v")
        if C.on("scalar"):
            C.eng.activation(out=t["rrow"][:, 0:T], in_=t["qrow"][:, 0:T],
                             func=AF.Ln, bias=t["epsr"][0:1, 0:1])
            i = C.eng.activation(out=t["rrow"][:, 0:T], in_=t["rrow"][:, 0:T],
                                 func=AF.Exp, scale=-0.5)
            C.tag(i, "s")
        else:
            C.inc("s")
        C.wait("vector", "s")
        if C.on("vector"):
            i = C.eng.tensor_tensor(t["mrow"][:, 0:T], t["srow"][:, 0:T],
                                    t["rrow"][:, 0:T], OP.mult)
            C.tag(i, "v")
        else:
            C.inc("v")
    else:
        if C.on("scalar"):
            C.eng.activation(out=t["rrow"][:, 0:T], in_=t["qrow"][:, 0:T],
                             func=AF.Ln, scale=1.0 / Dn, bias=t["epsr"][0:1, 0:1])
            i = C.eng.activation(out=t["rrow"][:, 0:T], in_=t["rrow"][:, 0:T],
                                 func=AF.Exp, scale=-0.5)
            C.tag(i, "s")
        else:
            C.inc("s")
    C.wait("tensor", "v")
    C.wait("tensor", "s")
    for ti in range(NT):
        sl = slice(ti * 512, (ti + 1) * 512)
        if C.on("tensor"):
            i = C.eng.matmul(t["ps_R"][:, :], t["onesf"][0:1, 0:128],
                             t["rrow"][:, sl], start=True, stop=True)
            if not rms:
                i = C.eng.matmul(t["ps_M"][:, :], t["onesf"][0:1, 0:128],
                                 t["mrow"][:, sl], start=True, stop=True)
            C.tag(i, "m")
        else:
            C.inc("m")
        C.wait("vector", "m")
        apply(C, ti)
        C.wait("tensor", "v")


def bcN(ps, NC):
    return ps[:, :].rearrange("p (o t) -> p o t", o=1).broadcast_to((128, NC, 512))



def build_l0a_proj():
    nc = bass.Bass("TRN2", target_bir_lowering=False, debug=False,
                   detect_race_conditions=RACE_DETECT)
    xo = nc.dram_tensor("xo", [D, TOK], F32, kind="ExternalInput").ap()
    wc = nc.dram_tensor("wc", [D + 1, 3 * D], F32, kind="ExternalInput").ap()
    qkv = nc.dram_tensor("qkv", [TOK, 3 * D], BF16, kind="ExternalOutput").ap()
    es = ExitStack()
    with es:
        X = es.enter_context(nc.sbuf_tensor("X", [128, NC_D, TOK], F32))
        zb = es.enter_context(nc.sbuf_tensor("zb", [128, NC_D, TOK], BF16))
        W = es.enter_context(nc.sbuf_tensor("W", [128, NC_D, 3 * D // 2], BF16))
        wb = es.enter_context(nc.sbuf_tensor("wb", [1, 3 * D], BF16))
        ob = es.enter_context(nc.sbuf_tensor("ob", [128, 2, 512], BF16))
        ps = [es.enter_context(nc.psum_tensor(f"ps{i}", [128, 512], F32))
              for i in range(2)]
        lnt = alloc_ln_scratch(es, nc, NC_D, TOK)

        def prog(C):
            if C.on("sync"):
                i = C.eng.dma_start(out=X[:, :, :],
                                    in_=xo.rearrange("(c p) t -> p c t", p=128))
                C.tag(i, "ld", 16)
            else:
                C.inc("ld", 16)
            if C.on("gpsimd"):
                i = C.eng.dma_start(
                    out=W[:, :, :],
                    in_=wc[0:D, 0:3 * D // 2].rearrange("(c p) o -> p c o", p=128))
                C.tag(i, "lw", 16)
                i = C.eng.dma_start(out=wb[:, :], in_=wc[D:D + 1, :])
                C.tag(i, "lw", 16)
            else:
                C.inc("lw", 32)
            emit_ln_const_init(C, lnt)

            def apply(C, ti):
                sl = slice(ti * 512, (ti + 1) * 512)
                if C.on("vector"):
                    Rb = bcN(lnt["ps_R"], NC_D)
                    Mb = bcN(lnt["ps_M"], NC_D)
                    C.eng.tensor_tensor(lnt["x2t"][:, :, :], X[:, :, sl], Rb, OP.mult)
                    i = C.eng.tensor_tensor(zb[:, :, sl], lnt["x2t"][:, :, :], Mb,
                                            OP.subtract)
                    C.tag(i, "v")
                else:
                    C.inc("v")

            emit_ln_fm(C, lnt, X, TOK, apply)

            it = 0
            for og in range(2):
                if og == 1:
                    C.wait("gpsimd", "m")
                    if C.on("gpsimd"):
                        i = C.eng.dma_start(
                            out=W[:, :, :],
                            in_=wc[0:D, 3 * D // 2:3 * D].rearrange(
                                "(c p) o -> p c o", p=128))
                        C.tag(i, "lw", 16)
                    else:
                        C.inc("lw", 16)
                C.wait("tensor", "lw")
                C.wait("tensor", "v")
                for tc in range(TOK // 128):
                    for oh in range(3):
                        o = og * 3 + oh
                        bank = it % 2
                        osl = slice(o * 512, (o + 1) * 512)
                        wsl = slice(oh * 512, (oh + 1) * 512)
                        if it >= 2:
                            C.wait("tensor", f"st{bank}")
                        if C.on("tensor"):
                            for c in range(NC_D):
                                C.eng.matmul(ps[bank][:, :],
                                             zb[:, c, tc * 128:(tc + 1) * 128],
                                             W[:, c, wsl], start=(c == 0),
                                             stop=False)
                            i = C.eng.matmul(ps[bank][:, :], lnt["onesb"][:, 0:128],
                                             wb[:, osl], start=False, stop=True)
                            C.tag(i, "m")
                        else:
                            C.inc("m")
                        C.wait("scalar", "m")
                        if C.on("scalar"):
                            fn = AF.Exp if o < 4 else AF.Identity
                            i = C.eng.activation(out=ob[:, bank, :],
                                                 in_=ps[bank][:, :], func=fn)
                            C.tag(i, "s")
                        else:
                            C.inc("s")
                        C.wait("sync", "s")
                        if C.on("sync"):
                            i = C.eng.dma_start(out=qkv[tc * 128:(tc + 1) * 128, osl],
                                                in_=ob[:, bank, :])
                            C.tag(i, f"st{bank}", 16)
                        else:
                            C.inc(f"st{bank}", 16)
                        it += 1
            C.wait("sync", "st0")
            C.wait("sync", "st1")

        run_program(nc, ["ld", "lw", "v", "s", "m", "fv", "st0", "st1"], prog)
    return nc



def build_l0a_attn():
    nc = bass.Bass("TRN2", target_bir_lowering=False, debug=False,
                   detect_race_conditions=RACE_DETECT)
    NH = 8
    CH = S // 128
    qf = nc.dram_tensor("qf", [NH, DH, S], BF16, kind="ExternalInput").ap()
    kf = nc.dram_tensor("kf", [NH, DH, S], BF16, kind="ExternalInput").ap()
    kt = nc.dram_tensor("kt", [NH, S, DH], BF16, kind="ExternalInput").ap()
    vt = nc.dram_tensor("vt", [NH, S, DH + 1], BF16, kind="ExternalInput").ap()
    tri = nc.dram_tensor("tri", [128, 128], BF16, kind="ExternalInput").ap()
    at = nc.dram_tensor("at", [NH, S, DH], BF16, kind="ExternalOutput").ap()
    es = ExitStack()
    with es:
        Q = es.enter_context(nc.sbuf_tensor("Q", [64, 2, S], BF16))
        K = es.enter_context(nc.sbuf_tensor("K", [64, 2, S], BF16))
        KT = es.enter_context(nc.sbuf_tensor("KT", [128, 2, CH, DH], BF16))
        VT = es.enter_context(nc.sbuf_tensor("VT", [128, 2, CH, DH + 1], BF16))
        AT = es.enter_context(nc.sbuf_tensor("AT", [128, 2, CH, DH], BF16))
        TRI = es.enter_context(nc.sbuf_tensor("TRI", [128, 128], BF16))
        M = es.enter_context(nc.sbuf_tensor("M", [128, 128], BF16))
        KV = es.enter_context(nc.sbuf_tensor("KV", [64, 2, DH + 1], BF16))
        KVA = es.enter_context(nc.sbuf_tensor("KVA", [64, DH + 1], F32))
        dn = es.enter_context(nc.sbuf_tensor("dn", [128, 1], F32))
        psA = es.enter_context(nc.psum_tensor("psA", [128, 128], F32))
        psN = es.enter_context(nc.psum_tensor("psN", [128, DH + 1], F32))
        psK = es.enter_context(nc.psum_tensor("psK", [64, DH + 1], F32))

        def prog(C):
            if C.on("sync"):
                i = C.eng.dma_start(out=TRI[:, :], in_=tri[:, :])
                C.tag(i, "lt", 16)
            else:
                C.inc("lt", 16)

            def load_head(h, buf):
                sem = f"ld{buf}"
                for (dst, src) in ((Q[:, buf, :], qf[h, :, :]),
                                   (K[:, buf, :], kf[h, :, :]),
                                   (KT[:, buf, :, :],
                                    kt[h, :, :].rearrange("(c p) d -> p c d", p=128)),
                                   (VT[:, buf, :, :],
                                    vt[h, :, :].rearrange("(c p) d -> p c d", p=128))):
                    if C.on("sync"):
                        i = C.eng.dma_start(out=dst, in_=src)
                        C.tag(i, sem, 16)
                    else:
                        C.inc(sem, 16)

            load_head(0, 0)
            C.wait("tensor", "lt")
            for h in range(NH):
                buf = h % 2
                if h + 1 < NH:
                    load_head(h + 1, 1 - buf)
                C.wait("tensor", f"ld{buf}", 64 * (h // 2 + 1))
                if h >= 2:
                    C.wait("tensor", f"st{buf}")
                for ci in range(CH):
                    csl = slice(ci * 128, (ci + 1) * 128)
                    if C.on("tensor"):
                        C.eng.matmul(psA[:, :], K[:, buf, csl], Q[:, buf, csl],
                                     start=True, stop=True)
                        i = C.eng.matmul(psK[:, :], KT[:, buf, ci, :],
                                         VT[:, buf, ci, :], start=True, stop=True)
                        C.tag(i, "m")
                    else:
                        C.inc("m")
                    C.wait("vector", "m")
                    if C.on("vector"):
                        C.eng.tensor_tensor(M[:, :], psA[:, :], TRI[:, :], OP.mult)
                        if ci == 0:
                            i = C.eng.tensor_copy(KVA[:, :], psK[:, :])
                        else:
                            i = C.eng.tensor_tensor(KVA[:, :], KVA[:, :], psK[:, :],
                                                    OP.add)
                    else:
                        i = None
                    C.fence("vector", i)
                    if C.on("vector"):
                        i = C.eng.tensor_copy(KV[:, ci % 2, :], KVA[:, :])
                        C.tag(i, "v")
                    else:
                        C.inc("v")
                    C.wait("tensor", "v")
                    if C.on("tensor"):
                        i = C.eng.matmul(psN[:, :], M[:, :], VT[:, buf, ci, :],
                                         start=True, stop=(ci == 0))
                        if ci > 0:
                            i = C.eng.matmul(psN[:, :], Q[:, buf, csl],
                                             KV[:, (ci - 1) % 2, :], start=False,
                                             stop=True)
                        C.tag(i, "m")
                    else:
                        C.inc("m")
                    C.wait("vector", "m")
                    if C.on("vector"):
                        i = C.eng.reciprocal(dn[:, :], psN[:, DH:DH + 1])
                    else:
                        i = None
                    C.fence("vector", i)
                    if C.on("vector"):
                        i = C.eng.tensor_scalar(AT[:, buf, ci, :], psN[:, 0:DH],
                                                dn[:, 0:1], None, OP.mult)
                        C.tag(i, "v")
                    else:
                        C.inc("v")
                    C.wait("tensor", "v")
                C.wait("sync", "v")
                if C.on("sync"):
                    i = C.eng.dma_start(
                        out=at[h, :, :].rearrange("(c p) d -> p c d", p=128),
                        in_=AT[:, buf, :, :])
                    C.tag(i, f"st{buf}", 16)
                else:
                    C.inc(f"st{buf}", 16)
            C.wait("sync", "st0")
            C.wait("sync", "st1")

        run_program(nc, ["lt", "ld0", "ld1", "v", "m", "fv", "st0", "st1"], prog)
    return nc



def build_l0a_out():
    nc = bass.Bass("TRN2", target_bir_lowering=False, debug=False,
                   detect_race_conditions=RACE_DETECT)
    TG = 1024
    af = nc.dram_tensor("af", [D, TOK], BF16, kind="ExternalInput").ap()
    wo = nc.dram_tensor("wo", [D, D], F32, kind="ExternalInput").ap()
    bo = nc.dram_tensor("bo", [128, NC_D], F32, kind="ExternalInput").ap()
    xo = nc.dram_tensor("xo", [D, TOK], F32, kind="ExternalInput").ap()
    out = nc.dram_tensor("out", [D, TOK], F32, kind="ExternalOutput").ap()
    es = ExitStack()
    with es:
        A = es.enter_context(nc.sbuf_tensor("A", [128, NC_D, TG], BF16))
        W = es.enter_context(nc.sbuf_tensor("W", [128, NC_D, D], BF16))
        BO = es.enter_context(nc.sbuf_tensor("BO", [128, NC_D], F32))
        X = es.enter_context(nc.sbuf_tensor("X", [128, NC_D, TG], F32))
        Y = es.enter_context(nc.sbuf_tensor("Y", [128, NC_D, TG], F32))
        OB = es.enter_context(nc.sbuf_tensor("OB", [128, NC_D, 512], F32))
        ps = [es.enter_context(nc.psum_tensor(f"ps{i}", [128, 512], F32))
              for i in range(2)]
        lnt = alloc_ln_scratch(es, nc, NC_D, TG)

        def prog(C):
            if C.on("gpsimd"):
                C.eng.dma_start(out=W[:, :, :],
                                in_=wo.rearrange("(c p) o -> p c o", p=128)
                                ).then_inc(C.sems["lw"], 16)
                i = C.eng.dma_start(out=BO[:, :], in_=bo)
                C.tag(i, "lw", 16)
                C.inc("lw", 16)
            else:
                C.inc("lw", 32)
            emit_ln_const_init(C, lnt)
            C.wait("tensor", "lw")
            C.wait("scalar", "lw")

            evac_s = []
            for g in range(TOK // TG):
                gsl = slice(g * TG, (g + 1) * TG)
                if C.on("sync"):
                    C.eng.dma_start(
                        out=A[:, :, :],
                        in_=af.rearrange("(c p) t -> p c t", p=128)[:, :, gsl]
                        ).then_inc(C.sems["ld"], 16)
                    i = C.eng.dma_start(
                        out=X[:, :, :],
                        in_=xo.rearrange("(c p) t -> p c t", p=128)[:, :, gsl])
                    C.tag(i, "ld", 16)
                    C.inc("ld", 16)
                else:
                    C.inc("ld", 32)
                C.wait("tensor", "ld")
                it = 0
                for tc in range(TG // 512):
                    tsl = slice(tc * 512, (tc + 1) * 512)
                    for oc in range(NC_D):
                        bank = it % 2
                        if len(evac_s) >= 2:
                            C.wait("tensor", "s", evac_s[-2])
                        if C.on("tensor"):
                            for c in range(NC_D):
                                i = C.eng.matmul(ps[bank][:, :],
                                                 W[:, c, oc * 128:(oc + 1) * 128],
                                                 A[:, c, tsl], start=(c == 0),
                                                 stop=(c == NC_D - 1))
                            C.tag(i, "m")
                        else:
                            C.inc("m")
                        C.wait("scalar", "m")
                        if C.on("scalar"):
                            i = C.eng.activation(out=Y[:, oc, tsl],
                                                 in_=ps[bank][:, :],
                                                 func=AF.Identity,
                                                 bias=BO[:, oc:oc + 1])
                            C.tag(i, "s")
                        else:
                            C.inc("s")
                        evac_s.append(C.cnt["s"])
                        it += 1
                C.wait("vector", "s")
                C.wait("scalar", "s")
                C.wait("tensor", "s")

                def apply(C, ti):
                    sl = slice(ti * 512, (ti + 1) * 512)
                    if C.on("vector"):
                        Rb = bcN(lnt["ps_R"], NC_D)
                        Mb = bcN(lnt["ps_M"], NC_D)
                        C.eng.tensor_tensor(OB[:, :, :], Y[:, :, sl], Rb, OP.mult)
                        C.eng.tensor_tensor(OB[:, :, :], OB[:, :, :], Mb,
                                            OP.subtract)
                        i = C.eng.tensor_tensor(OB[:, :, :], OB[:, :, :],
                                                X[:, :, sl], OP.add)
                        C.tag(i, "v")
                    else:
                        C.inc("v")
                    C.wait("sync", "v")
                    if C.on("sync"):
                        i = C.eng.dma_start(
                            out=out.rearrange("(c p) t -> p c t",
                                              p=128)[:, :, gsl][:, :, sl],
                            in_=OB[:, :, :])
                        C.tag(i, "st", 16)
                    else:
                        C.inc("st", 16)
                    C.wait("vector", "st")

                emit_ln_fm(C, lnt, Y, TG, apply)
                C.wait("sync", "v")
                C.wait("sync", "s")
                C.wait("sync", "m")
            C.wait("sync", "st")

        run_program(nc, ["ld", "lw", "v", "s", "m", "fv", "st"], prog)
    return nc



EXT = 2432
L0F_SEGS = [(3, 9), (9, 15), (15, 19)]


def _tiles(n, w=512):
    out = []
    t = 0
    while t < n:
        out.append((t, min(w, n - t)))
        t += w
    return out


def build_l0f_attn():
    nc = bass.Bass("TRN2", target_bir_lowering=False, debug=False,
                   detect_race_conditions=RACE_DETECT)
    xin = nc.dram_tensor("xin", [D, EXT], F32, kind="ExternalInput").ap()
    wqkv = nc.dram_tensor("wqkv", [D, 3 * D], F32, kind="ExternalInput").ap()
    wo = nc.dram_tensor("wo", [D, D], F32, kind="ExternalInput").ap()
    tq1 = nc.dram_tensor("tq1", [128, EXT], F32, kind="ExternalInput").ap()
    tq2 = nc.dram_tensor("tq2", [128, EXT], F32, kind="ExternalInput").ap()
    tk1 = nc.dram_tensor("tk1", [128, EXT], F32, kind="ExternalInput").ap()
    tk2 = nc.dram_tensor("tk2", [128, EXT], F32, kind="ExternalInput").ap()
    rotm = nc.dram_tensor("rotm", [128, 128], F32, kind="ExternalInput").ap()
    blk = nc.dram_tensor("blk", [128, 128], F32, kind="ExternalInput").ap()
    idm = nc.dram_tensor("idm", [128, 128], F32, kind="ExternalInput").ap()
    mneg = nc.dram_tensor("mneg", [3, 128, 384], F32, kind="ExternalInput").ap()
    xout = nc.dram_tensor("xout", [D, TOK], F32, kind="ExternalOutput").ap()

    es = ExitStack()
    with es:
        X = es.enter_context(nc.sbuf_tensor("X", [128, NC_D, EXT], F32))
        XBS = es.enter_context(nc.sbuf_tensor("XBS", [128, NC_D, 1025], BF16))
        W = es.enter_context(nc.sbuf_tensor("W", [128, NC_D, 2 * D], BF16))
        WOC = es.enter_context(nc.sbuf_tensor("WOC", [128, 2, NC_D, 128], BF16))
        TQ1 = es.enter_context(nc.sbuf_tensor("TQ1", [128, 1024], BF16))
        TQ2 = es.enter_context(nc.sbuf_tensor("TQ2", [128, 1024], BF16))
        TK1 = es.enter_context(nc.sbuf_tensor("TK1", [128, 1024], BF16))
        TK2 = es.enter_context(nc.sbuf_tensor("TK2", [128, 1024], BF16))
        ROTM = es.enter_context(nc.sbuf_tensor("ROTM", [128, 128], BF16))
        BLKM = es.enter_context(nc.sbuf_tensor("BLKM", [128, 128], BF16))
        IDM = es.enter_context(nc.sbuf_tensor("IDM", [128, 128], BF16))
        MNEG = es.enter_context(nc.sbuf_tensor("MNEG", [128, 3, 384], BF16))
        KH = es.enter_context(nc.sbuf_tensor("KH", [128, NC_D, 1024], BF16))
        QH = es.enter_context(nc.sbuf_tensor("QH", [128, NC_D, 768], BF16))
        VT = es.enter_context(nc.sbuf_tensor("VT", [128, 8, 16, DH + 1], BF16))
        ATM = es.enter_context(nc.sbuf_tensor("ATM", [128, 16, DH], BF16))
        AFM = es.enter_context(nc.sbuf_tensor("AFM", [128, NC_D, 512], BF16))
        CAR = es.enter_context(nc.sbuf_tensor("CAR", [128, NC_D, 256], F32))
        K2T = es.enter_context(nc.sbuf_tensor("K2T", [128, 512], BF16))
        RSB = es.enter_context(nc.sbuf_tensor("RSB", [128, 512], F32))
        ET = es.enter_context(nc.sbuf_tensor("ET", [128, 512], BF16))
        PB = es.enter_context(nc.sbuf_tensor("PB", [128, 384], BF16))
        PJI = es.enter_context(nc.sbuf_tensor("PJI", [128, 3, 128], BF16))
        NMX = es.enter_context(nc.sbuf_tensor("NMX", [128, 1], F32))
        RDN = es.enter_context(nc.sbuf_tensor("RDN", [128, 1], F32))
        EPSR = es.enter_context(nc.sbuf_tensor("EPSR", [128, 1], F32))
        psP = es.enter_context(nc.psum_tensor("psP", [128, 512], F32))
        psB = es.enter_context(nc.psum_tensor("psB", [128, 512], F32))
        psR = es.enter_context(nc.psum_tensor("psR", [128, 512], F32))
        psS = es.enter_context(nc.psum_tensor("psS", [128, 384], F32))
        psT = es.enter_context(nc.psum_tensor("psT", [128, 384], BF16))
        psN = es.enter_context(nc.psum_tensor("psN", [128, DH + 1], F32))
        psO = es.enter_context(nc.psum_tensor("psO", [128, 512], F32))

        def prog(C):
            if C.on("sync"):
                i = C.eng.dma_start(out=X[:, :, :],
                                    in_=xin.rearrange("(c p) t -> p c t", p=128))
                C.tag(i, "ld", 16)
            else:
                C.inc("ld", 16)
            if C.on("gpsimd"):
                for dst, srcap in ((ROTM, rotm), (BLKM, blk), (IDM, idm)):
                    i = C.eng.dma_start(out=dst[:, :], in_=srcap)
                    C.tag(i, "lw", 16)
                i = C.eng.dma_start(out=MNEG[:, :, :],
                                    in_=mneg.rearrange("m p t -> p m t"))
                C.tag(i, "lw", 16)
                i = C.eng.dma_start(
                    out=W[:, :, D:2 * D],
                    in_=wqkv[:, 2 * D:3 * D].rearrange("(c p) o -> p c o", p=128))
                C.tag(i, "lw", 16)
            else:
                C.inc("lw", 80)
            if C.on("vector"):
                C.eng.memset(EPSR[:, :], 1e-24)
                i = C.eng.memset(VT[:, :, :, DH:DH + 1], 1.0)
                C.tag(i, "v")
            else:
                C.inc("v")

            def pcn(dst, oc, dst0, src0, tw, wofs, t1, t2):
                if C.on("tensor"):
                    for c in range(NC_D):
                        b = 1 if c < 4 else 0
                        i = C.eng.matmul(psP[:, 0:tw],
                                         W[:, c, wofs + oc * 128:wofs + oc * 128 + 128],
                                         XBS[:, c, b + src0:b + src0 + tw],
                                         start=(c == 0), stop=(c == NC_D - 1))
                    C.tag(i, "m")
                else:
                    C.inc("m")
                dsl = slice(dst0, dst0 + tw)
                C.wait("scalar", "m")
                if C.on("scalar"):
                    C.eng.activation(out=dst[:, oc, dsl], in_=psP[:, 0:tw],
                                     func=AF.Identity)
                    i = C.eng.activation(out=K2T[:, 0:tw], in_=psP[:, 0:tw],
                                         func=AF.Square)
                    C.tag(i, "s")
                else:
                    C.inc("s")
                C.wait("tensor", "s")
                if C.on("tensor"):
                    i = C.eng.matmul(psB[:, 0:tw], BLKM[:, :], K2T[:, 0:tw],
                                     start=True, stop=True)
                    C.tag(i, "m")
                else:
                    C.inc("m")
                C.wait("scalar", "m")
                if C.on("scalar"):
                    C.eng.activation(out=RSB[:, 0:tw], in_=psB[:, 0:tw],
                                     func=AF.Ln, bias=EPSR[:, 0:1])
                    i = C.eng.activation(out=RSB[:, 0:tw], in_=RSB[:, 0:tw],
                                         func=AF.Exp, scale=-0.5)
                    C.tag(i, "s")
                else:
                    C.inc("s")
                C.wait("vector", "s")
                if C.on("vector"):
                    i = C.eng.tensor_tensor(dst[:, oc, dsl], dst[:, oc, dsl],
                                            RSB[:, 0:tw], OP.mult)
                else:
                    i = None
                C.tag_fenced("vector", i, "v")
                C.wait("tensor", "v")
                if C.on("tensor"):
                    i = C.eng.matmul(psR[:, 0:tw], ROTM[:, :], dst[:, oc, dsl],
                                     start=True, stop=True)
                    C.tag(i, "m")
                else:
                    C.inc("m")
                C.wait("vector", "m")
                tsl = slice(src0, src0 + tw)
                if C.on("vector"):
                    C.eng.tensor_tensor(dst[:, oc, dsl], dst[:, oc, dsl],
                                        t1[:, tsl], OP.mult)
                    C.eng.tensor_tensor(ET[:, 0:tw], psR[:, 0:tw], t2[:, tsl],
                                        OP.mult)
                    i = C.eng.tensor_tensor(dst[:, oc, dsl], dst[:, oc, dsl],
                                            ET[:, 0:tw], OP.add)
                else:
                    i = None
                C.tag_fenced("vector", i, "v")
                C.wait("tensor", "v")
                C.wait("scalar", "v")

            carry = []
            for si, (q0, q1) in enumerate(L0F_SEGS):
                kr0, kr1 = q0 - 2, q1
                e0 = kr0 * 128
                kw = (kr1 - kr0) * 128
                C.wait("vector", "ld")
                if C.on("vector"):
                    i = C.eng.tensor_copy(XBS[:, :, 0:kw + 1],
                                          X[:, :, e0 - 1:e0 + kw])
                    C.tag(i, "v")
                else:
                    C.inc("v")
                if carry:
                    cc0 = carry.pop()
                    if C.on("vector"):
                        i = C.eng.tensor_tensor(
                            X[:, :, cc0 * 128:(cc0 + 2) * 128],
                            X[:, :, cc0 * 128:(cc0 + 2) * 128],
                            CAR[:, :, :], OP.add)
                        C.tag(i, "v")
                    else:
                        C.inc("v")
                C.wait("gpsimd", "m")
                if C.on("gpsimd"):
                    for dst, srcap in ((TQ1, tq1), (TQ2, tq2), (TK1, tk1),
                                       (TK2, tk2)):
                        i = C.eng.dma_start(out=dst[:, 0:kw],
                                            in_=srcap[:, e0:e0 + kw])
                        C.tag(i, "lw", 16)
                    i = C.eng.dma_start(
                        out=W[:, :, 0:D],
                        in_=wqkv[:, D:2 * D].rearrange("(c p) o -> p c o", p=128))
                    C.tag(i, "lw", 16)
                else:
                    C.inc("lw", 80)
                C.wait("tensor", "v")
                C.wait("tensor", "lw")
                C.wait("scalar", "lw")
                for oc in range(NC_D):
                    for (tt0, tw) in _tiles(kw):
                        pcn(KH, oc, tt0, tt0, tw, 0, TK1, TK2)
                for kc in range(kr0, kr1):
                    src0 = (kc - kr0) * 128
                    for half in range(2):
                        if C.on("tensor"):
                            for c in range(NC_D):
                                b = 1 if c < 4 else 0
                                i = C.eng.matmul(
                                    psP[:, :],
                                    XBS[:, c, b + src0:b + src0 + 128],
                                    W[:, c, D + half * 512:D + half * 512 + 512],
                                    start=(c == 0), stop=(c == NC_D - 1))
                            C.tag(i, "m")
                        else:
                            C.inc("m")
                        C.wait("vector", "m")
                        if C.on("vector"):
                            i = C.eng.tensor_copy(
                                VT[:, kc - kr0, half * 8:half * 8 + 8, 0:DH],
                                psP[:, :].rearrange("p (h d) -> p h d", h=8))
                            C.tag(i, "v")
                        else:
                            C.inc("v")
                        C.wait("tensor", "v")
                C.wait("gpsimd", "m")
                if C.on("gpsimd"):
                    i = C.eng.dma_start(
                        out=W[:, :, 0:D],
                        in_=wqkv[:, 0:D].rearrange("(c p) o -> p c o", p=128))
                    C.tag(i, "lw", 16)
                else:
                    C.inc("lw", 16)
                C.wait("tensor", "lw")
                C.wait("scalar", "lw")
                qw = (q1 - q0) * 128
                qsrc = (q0 - kr0) * 128
                for oc in range(NC_D):
                    for (tt0, tw) in _tiles(qw):
                        pcn(QH, oc, tt0, qsrc + tt0, tw, 0, TQ1, TQ2)
                for qi, qc in enumerate(range(q0, q1)):
                    for h in range(16):
                        hc, hr = h // 2, (h % 2) * 64
                        rows = slice(hr, hr + 64)
                        qsl = slice(qi * 128, (qi + 1) * 128)
                        ksl = slice((qc - 2 - kr0) * 128, (qc + 1 - kr0) * 128)
                        if C.on("tensor"):
                            C.eng.matmul(psS[:, :], QH[rows, hc, qsl],
                                         KH[rows, hc, ksl], start=True,
                                         stop=False)
                            mi = qc - 3 if qc <= 4 else 2
                            i = C.eng.matmul(psS[:, :], IDM[:, :],
                                             MNEG[:, mi, :], start=False,
                                             stop=True)
                            C.tag(i, "m")
                        else:
                            C.inc("m")
                        C.wait("vector", "m")
                        if C.on("vector"):
                            i = C.eng.tensor_reduce(NMX[:, :], psS[:, :],
                                                    mybir.AxisListType.X,
                                                    OP.max, negate=True)
                        else:
                            i = None
                        C.tag_fenced("vector", i, "v")
                        C.wait("scalar", "v")
                        if C.on("scalar"):
                            i = C.eng.activation(out=PB[:, :], in_=psS[:, :],
                                                 func=AF.Exp, bias=NMX[:, 0:1])
                            C.tag(i, "s")
                        else:
                            C.inc("s")
                        C.wait("tensor", "s")
                        if C.on("tensor"):
                            for b3 in range(3):
                                i = C.eng.transpose(
                                    psT[:, b3 * 128:(b3 + 1) * 128],
                                    PB[:, b3 * 128:(b3 + 1) * 128], IDM[:, :])
                            C.tag(i, "m")
                        else:
                            C.inc("m")
                        C.wait("vector", "m")
                        if C.on("vector"):
                            i = C.eng.tensor_copy(
                                PJI[:, :, :],
                                psT[:, :].rearrange("p (b t) -> p b t", b=3))
                            C.tag(i, "v")
                        else:
                            C.inc("v")
                        C.wait("tensor", "v")
                        if C.on("tensor"):
                            for b3 in range(3):
                                i = C.eng.matmul(psN[:, :], PJI[:, b3, :],
                                                 VT[:, qc - 2 - kr0 + b3, h, :],
                                                 start=(b3 == 0), stop=(b3 == 2))
                            C.tag(i, "m")
                        else:
                            C.inc("m")
                        C.wait("vector", "m")
                        if C.on("vector"):
                            i = C.eng.reciprocal(RDN[:, :], psN[:, DH:DH + 1])
                        else:
                            i = None
                        C.tag_fenced("vector", i, "v")
                        if C.on("vector"):
                            i = C.eng.tensor_scalar(ATM[:, h, :], psN[:, 0:DH],
                                                    RDN[:, 0:1], None, OP.mult)
                            C.tag(i, "v")
                        else:
                            C.inc("v")
                        C.wait("tensor", "v")
                        C.wait("scalar", "v")
                    fcol = (qi * 128) % 512
                    ATMflat = ATM[:, :, :].rearrange("p h d -> p (h d)")
                    for fb in range(NC_D):
                        if C.on("tensor"):
                            i = C.eng.transpose(psT[:, 0:128],
                                                ATMflat[:, fb * 128:(fb + 1) * 128],
                                                IDM[:, :])
                            C.tag(i, "m")
                        else:
                            C.inc("m")
                        C.wait("vector", "m")
                        if C.on("vector"):
                            i = C.eng.tensor_copy(AFM[:, fb, fcol:fcol + 128],
                                                  psT[:, 0:128])
                            C.tag(i, "v")
                        else:
                            C.inc("v")
                        C.wait("tensor", "v")
                    if fcol == 384 or qc == q1 - 1:
                        nfl = fcol + 128
                        t00 = (qc + 1) * 128 - nfl
                        last = (qc == q1 - 1)
                        for oc in range(NC_D):
                            wbuf = oc % 2
                            if oc == 0:
                                C.wait("gpsimd", "m")
                                if C.on("gpsimd"):
                                    i = C.eng.dma_start(
                                        out=WOC[:, 0, :, :],
                                        in_=wo.rearrange("(c p) o -> p c o",
                                                         p=128)[:, :, 0:128])
                                    C.tag(i, "lw", 16)
                                else:
                                    C.inc("lw", 16)
                            if oc + 1 < NC_D:
                                C.wait("gpsimd", "m")
                                if C.on("gpsimd"):
                                    i = C.eng.dma_start(
                                        out=WOC[:, 1 - wbuf, :, :],
                                        in_=wo.rearrange(
                                            "(c p) o -> p c o", p=128)[
                                            :, :, (oc + 1) * 128:(oc + 2) * 128])
                                    C.tag(i, "lw", 16)
                                else:
                                    C.inc("lw", 16)
                            C.wait("tensor", "lw")
                            if C.on("tensor"):
                                for c in range(NC_D):
                                    i = C.eng.matmul(
                                        psO[:, 0:nfl],
                                        WOC[:, wbuf, c, :],
                                        AFM[:, c, 0:nfl],
                                        start=(c == 0), stop=(c == NC_D - 1))
                                C.tag(i, "m")
                            else:
                                C.inc("m")
                            C.wait("vector", "m")
                            ncar = 256 if last else 0
                            ndir = nfl - ncar
                            if C.on("vector"):
                                if ndir > 0:
                                    i = C.eng.tensor_tensor(
                                        X[:, oc, t00:t00 + ndir],
                                        X[:, oc, t00:t00 + ndir],
                                        psO[:, 0:ndir], OP.add)
                                if ncar:
                                    i = C.eng.tensor_copy(CAR[:, oc, :],
                                                          psO[:, ndir:nfl])
                                C.tag(i, "v")
                            else:
                                C.inc("v")
                            C.wait("tensor", "v")
                        if last:
                            carry.append(q1 - 2)
            if carry:
                cc0 = carry.pop()
                if C.on("vector"):
                    i = C.eng.tensor_tensor(
                        X[:, :, cc0 * 128:(cc0 + 2) * 128],
                        X[:, :, cc0 * 128:(cc0 + 2) * 128],
                        CAR[:, :, :], OP.add)
                    C.tag(i, "v")
                else:
                    C.inc("v")
            C.wait("sync", "v")
            if C.on("sync"):
                i = C.eng.dma_start(
                    out=xout.rearrange("(c p) t -> p c t", p=128),
                    in_=X[:, :, 384:EXT])
                C.tag(i, "st", 16)
            else:
                C.inc("st", 16)
            C.wait("sync", "st")

        run_program(nc, ["ld", "lw", "v", "s", "m", "fv", "st"], prog)
    return nc



NHC = 22


def build_l0f_ffn():
    nc = bass.Bass("TRN2", target_bir_lowering=False, debug=False,
                   detect_race_conditions=RACE_DETECT)
    xin = nc.dram_tensor("xin", [D, TOK + 1], F32, kind="ExternalInput").ap()
    w1i = nc.dram_tensor("w1i", [D, NHC * 256], F32, kind="ExternalInput").ap()
    b1c = nc.dram_tensor("b1c", [128, NHC * 2], F32, kind="ExternalInput").ap()
    w2 = nc.dram_tensor("w2", [NHC * 128, D], F32, kind="ExternalInput").ap()
    b2c = nc.dram_tensor("b2c", [128, NC_D], F32, kind="ExternalInput").ap()
    gln = nc.dram_tensor("gln", [128, NC_D], F32, kind="ExternalInput").ap()
    bln = nc.dram_tensor("bln", [128, NC_D], F32, kind="ExternalInput").ap()
    xout = nc.dram_tensor("xout", [D, TOK], F32, kind="ExternalOutput").ap()

    es = ExitStack()
    with es:
        XT = es.enter_context(nc.sbuf_tensor("XT", [128, NC_D, 513], F32))
        ZNB = es.enter_context(nc.sbuf_tensor("ZNB", [128, NC_D, 512], BF16))
        W1 = es.enter_context(nc.sbuf_tensor("W1", [128, NC_D, NHC * 256], BF16))
        B1C = es.enter_context(nc.sbuf_tensor("B1C", [128, NHC, 2], F32))
        W2C = es.enter_context(nc.sbuf_tensor("W2C", [128, 2, NHC, 128], BF16))
        B2C = es.enter_context(nc.sbuf_tensor("B2C", [128, NC_D], F32))
        GLN = es.enter_context(nc.sbuf_tensor("GLN", [128, NC_D], F32))
        BLN = es.enter_context(nc.sbuf_tensor("BLN", [128, NC_D], F32))
        HB = es.enter_context(nc.sbuf_tensor("HB", [128, NHC, 512], BF16))
        GEL = es.enter_context(nc.sbuf_tensor("GEL", [128, 512], BF16))
        OB = es.enter_context(nc.sbuf_tensor("OB", [128, NC_D, 512], F32))
        lnt = alloc_ln_scratch(es, nc, NC_D, 512, tag="f")
        psA = es.enter_context(nc.psum_tensor("psA", [128, 512], F32))
        psG = es.enter_context(nc.psum_tensor("psG", [128, 512], F32))
        psY = es.enter_context(nc.psum_tensor("psY", [128, 512], F32))

        def prog(C):
            if C.on("gpsimd"):
                i = C.eng.dma_start(out=W1[:, :, :],
                                    in_=w1i.rearrange("(c p) o -> p c o", p=128))
                C.tag(i, "lw", 16)
                i = C.eng.dma_start(out=B1C[:, :, :],
                                    in_=b1c.rearrange("p (h two) -> p h two", two=2))
                C.tag(i, "lw", 16)
                i = C.eng.dma_start(out=B2C[:, :], in_=b2c)
                C.tag(i, "lw", 16)
                i = C.eng.dma_start(out=GLN[:, :], in_=gln)
                C.tag(i, "lw", 16)
                i = C.eng.dma_start(out=BLN[:, :], in_=bln)
                C.tag(i, "lw", 16)
            else:
                C.inc("lw", 80)
            emit_ln_const_init(C, lnt)
            C.wait("tensor", "lw")
            C.wait("scalar", "lw")
            C.wait("vector", "lw")

            for ti in range(TOK // 512):
                if C.on("sync"):
                    i = C.eng.dma_start(
                        out=XT[:, :, :],
                        in_=xin.rearrange("(c p) t -> p c t", p=128)[
                            :, :, ti * 512:ti * 512 + 513])
                    C.tag(i, "ld", 16)
                else:
                    C.inc("ld", 16)
                C.wait("tensor", "ld")
                C.wait("scalar", "ld")
                C.wait("vector", "ld")

                def sx(c):
                    b = 1 if c < 4 else 0
                    return XT[:, c, b:b + 512]

                if C.on("scalar"):
                    for c in range(NC_D):
                        i = C.eng.activation(out=lnt["x2t"][:, c, :], in_=sx(c),
                                             func=AF.Square)
                    C.tag(i, "s")
                else:
                    C.inc("s")
                C.wait("tensor", "s")
                if C.on("tensor"):
                    for c in range(NC_D):
                        C.eng.matmul(lnt["ps_s"][:, :], lnt["onesf"][:, 0:1],
                                     sx(c), start=(c == 0), stop=(c == NC_D - 1))
                    for c in range(NC_D):
                        i = C.eng.matmul(lnt["ps_q"][:, :], lnt["onesf"][:, 0:1],
                                         lnt["x2t"][:, c, :], start=(c == 0),
                                         stop=(c == NC_D - 1))
                    C.tag(i, "m")
                else:
                    C.inc("m")
                C.wait("vector", "m")
                if C.on("vector"):
                    C.eng.tensor_copy(lnt["srow"][:, :], lnt["ps_s"][:, :])
                    i = C.eng.tensor_copy(lnt["qrow"][:, :], lnt["ps_q"][:, :])
                else:
                    i = None
                C.tag_fenced("vector", i, "v")
                Dn = 1024.0
                if C.on("vector"):
                    C.eng.tensor_scalar(lnt["srow"][:, :], lnt["srow"][:, :],
                                        1.0 / Dn, None, OP.mult)
                    C.eng.tensor_tensor(lnt["mrow"][:, :], lnt["srow"][:, :],
                                        lnt["srow"][:, :], OP.mult)
                    i = C.eng.scalar_tensor_tensor(lnt["qrow"][:, :],
                                                   lnt["qrow"][:, :], 1.0 / Dn,
                                                   lnt["mrow"][:, :], OP.mult,
                                                   OP.subtract)
                else:
                    i = None
                C.tag_fenced("vector", i, "v")
                C.wait("scalar", "v")
                if C.on("scalar"):
                    C.eng.activation(out=lnt["rrow"][:, :], in_=lnt["qrow"][:, :],
                                     func=AF.Ln, bias=lnt["epsr"][0:1, 0:1])
                    i = C.eng.activation(out=lnt["rrow"][:, :],
                                         in_=lnt["rrow"][:, :], func=AF.Exp,
                                         scale=-0.5)
                    C.tag(i, "s")
                else:
                    C.inc("s")
                C.wait("vector", "s")
                if C.on("vector"):
                    i = C.eng.tensor_tensor(lnt["mrow"][:, :], lnt["srow"][:, :],
                                            lnt["rrow"][:, :], OP.mult)
                else:
                    i = None
                C.tag_fenced("vector", i, "v")
                C.wait("tensor", "v")
                if C.on("tensor"):
                    C.eng.matmul(lnt["ps_R"][:, :], lnt["onesf"][0:1, 0:128],
                                 lnt["rrow"][:, :], start=True, stop=True)
                    i = C.eng.matmul(lnt["ps_M"][:, :], lnt["onesf"][0:1, 0:128],
                                     lnt["mrow"][:, :], start=True, stop=True)
                    C.tag(i, "m")
                else:
                    C.inc("m")
                C.wait("vector", "m")
                if C.on("vector"):
                    for c in range(NC_D):
                        C.eng.tensor_tensor(OB[:, c, :], sx(c), lnt["ps_R"][:, :],
                                            OP.mult)
                    C.eng.tensor_tensor(
                        OB[:, :, :], OB[:, :, :],
                        lnt["ps_M"][:, :].rearrange("p (o t) -> p o t",
                                                    o=1).broadcast_to(
                            (128, NC_D, 512)), OP.subtract)
                    for c in range(NC_D):
                        i = C.eng.tensor_scalar(ZNB[:, c, :], OB[:, c, :],
                                                GLN[:, c:c + 1], BLN[:, c:c + 1],
                                                OP.mult, OP.add)
                    C.tag(i, "v")
                else:
                    C.inc("v")
                C.wait("tensor", "v")

                for hc in range(NHC):
                    if C.on("tensor"):
                        for c in range(NC_D):
                            C.eng.matmul(psA[:, :], W1[:, c, hc * 256:hc * 256 + 128],
                                         ZNB[:, c, :], start=(c == 0),
                                         stop=(c == NC_D - 1))
                        for c in range(NC_D):
                            i = C.eng.matmul(psG[:, :],
                                             W1[:, c, hc * 256 + 128:hc * 256 + 256],
                                             ZNB[:, c, :], start=(c == 0),
                                             stop=(c == NC_D - 1))
                        C.tag(i, "m")
                    else:
                        C.inc("m")
                    C.wait("scalar", "m")
                    if C.on("scalar"):
                        i = C.eng.activation(out=GEL[:, :], in_=psG[:, :],
                                             func=AF.Gelu, bias=B1C[:, hc, 1:2])
                        C.tag(i, "s")
                    else:
                        C.inc("s")
                    C.wait("vector", "s")
                    if C.on("vector"):
                        i = C.eng.scalar_tensor_tensor(HB[:, hc, :], psA[:, :],
                                                       B1C[:, hc, 0:1], GEL[:, :],
                                                       OP.add, OP.mult)
                        C.tag(i, "v")
                    else:
                        C.inc("v")
                    C.wait("tensor", "v")
                    C.wait("scalar", "v")
                for oc in range(NC_D):
                    wbuf = oc % 2
                    if ti == 0 and oc == 0:
                        if C.on("gpsimd"):
                            i = C.eng.dma_start(
                                out=W2C[:, 0, :, :],
                                in_=w2.rearrange("(c p) o -> p c o", p=128)[
                                    :, :, 0:128])
                            C.tag(i, "lw", 16)
                        else:
                            C.inc("lw", 16)
                    noc = oc + 1
                    if noc < NC_D or ti + 1 < TOK // 512:
                        noc2 = noc % NC_D
                        C.wait("gpsimd", "m")
                        if C.on("gpsimd"):
                            i = C.eng.dma_start(
                                out=W2C[:, 1 - wbuf, :, :],
                                in_=w2.rearrange("(c p) o -> p c o", p=128)[
                                    :, :, noc2 * 128:(noc2 + 1) * 128])
                            C.tag(i, "lw", 16)
                        else:
                            C.inc("lw", 16)
                    C.wait("tensor", "lw")
                    if C.on("tensor"):
                        for hc in range(NHC):
                            i = C.eng.matmul(psY[:, :],
                                             W2C[:, wbuf, hc, :],
                                             HB[:, hc, :], start=(hc == 0),
                                             stop=(hc == NHC - 1))
                        C.tag(i, "m")
                    else:
                        C.inc("m")
                    C.wait("vector", "m")
                    if C.on("vector"):
                        i = C.eng.scalar_tensor_tensor(OB[:, oc, :], psY[:, :],
                                                       B2C[:, oc:oc + 1],
                                                       XT[:, oc, 1:513],
                                                       OP.add, OP.add)
                        C.tag(i, "v")
                    else:
                        C.inc("v")
                    C.wait("tensor", "v")
                C.wait("sync", "v")
                if C.on("sync"):
                    i = C.eng.dma_start(
                        out=xout.rearrange("(c p) t -> p c t", p=128)[
                            :, :, ti * 512:(ti + 1) * 512],
                        in_=OB[:, :, :])
                    C.tag(i, "st", 16)
                else:
                    C.inc("st", 16)
                C.wait("sync", "st")

        run_program(nc, ["ld", "lw", "v", "s", "m", "fv", "st"], prog)
    return nc



def build_pre_ln():
    nc = bass.Bass("TRN2", target_bir_lowering=False, debug=False,
                   detect_race_conditions=RACE_DETECT)
    xo = nc.dram_tensor("xo", [D, TOK], F32, kind="ExternalInput").ap()
    out = nc.dram_tensor("out", [D, TOK], F32, kind="ExternalOutput").ap()
    TG = 1024
    es = ExitStack()
    with es:
        X = es.enter_context(nc.sbuf_tensor("X", [128, NC_D, TG], F32))
        OB = es.enter_context(nc.sbuf_tensor("OB", [128, NC_D, 512], F32))
        lnt = alloc_ln_scratch(es, nc, NC_D, TG)

        def prog(C):
            emit_ln_const_init(C, lnt)
            xor = xo.rearrange("(c p) t -> p c t", p=128)
            outr = out.rearrange("(c p) t -> p c t", p=128)
            for g in range(TOK // TG):
                gsl = slice(g * TG, (g + 1) * TG)
                C.dma("sync", "ld", lambda e: e.dma_start(
                    out=X[:, :, :], in_=xor[:, :, gsl]))
                C.wait("tensor", "ld")
                C.wait("scalar", "ld")
                C.wait("vector", "ld")

                def apply(C, ti):
                    sl = slice(ti * 512, (ti + 1) * 512)
                    if C.on("vector"):
                        C.eng.tensor_tensor(OB[:, :, :], X[:, :, sl],
                                            bcN(lnt["ps_R"], NC_D), OP.mult)
                        i = C.eng.tensor_tensor(OB[:, :, :], OB[:, :, :],
                                                bcN(lnt["ps_M"], NC_D),
                                                OP.subtract)
                        C.tag(i, "v")
                    else:
                        C.inc("v")
                    C.wait("sync", "v")
                    C.dma("sync", "st", lambda e: e.dma_start(
                        out=outr[:, :, gsl][:, :, sl], in_=OB[:, :, :]))
                    C.wait("vector", "st")

                emit_ln_fm(C, lnt, X, TG, apply)
                C.wait("sync", "v")
                C.wait("sync", "s")
                C.wait("sync", "m")
            C.wait("sync", "st")

        run_program(nc, ["ld", "lw", "v", "s", "m", "fv", "st"], prog)
    return nc


def build_post_ln():
    nc = bass.Bass("TRN2", target_bir_lowering=False, debug=False,
                   detect_race_conditions=RACE_DETECT)
    yo = nc.dram_tensor("yo", [D, TOK], F32, kind="ExternalInput").ap()
    xo = nc.dram_tensor("xo", [D, TOK], F32, kind="ExternalInput").ap()
    out = nc.dram_tensor("out", [D, TOK], F32, kind="ExternalOutput").ap()
    TG = 1024
    es = ExitStack()
    with es:
        Y = es.enter_context(nc.sbuf_tensor("Y", [128, NC_D, TG], F32))
        X = es.enter_context(nc.sbuf_tensor("X", [128, NC_D, TG], F32))
        OB = es.enter_context(nc.sbuf_tensor("OB", [128, NC_D, 512], F32))
        lnt = alloc_ln_scratch(es, nc, NC_D, TG)

        def prog(C):
            emit_ln_const_init(C, lnt)
            yor = yo.rearrange("(c p) t -> p c t", p=128)
            xor = xo.rearrange("(c p) t -> p c t", p=128)
            outr = out.rearrange("(c p) t -> p c t", p=128)
            for g in range(TOK // TG):
                gsl = slice(g * TG, (g + 1) * TG)
                C.dma("sync", "ld", lambda e: e.dma_start(
                    out=Y[:, :, :], in_=yor[:, :, gsl]))
                C.dma("sync", "ld", lambda e: e.dma_start(
                    out=X[:, :, :], in_=xor[:, :, gsl]))
                C.wait("tensor", "ld")
                C.wait("scalar", "ld")
                C.wait("vector", "ld")

                def apply(C, ti):
                    sl = slice(ti * 512, (ti + 1) * 512)
                    if C.on("vector"):
                        C.eng.tensor_tensor(OB[:, :, :], Y[:, :, sl],
                                            bcN(lnt["ps_R"], NC_D), OP.mult)
                        C.eng.tensor_tensor(OB[:, :, :], OB[:, :, :],
                                            bcN(lnt["ps_M"], NC_D), OP.subtract)
                        i = C.eng.tensor_tensor(OB[:, :, :], OB[:, :, :],
                                                X[:, :, sl], OP.add)
                        C.tag(i, "v")
                    else:
                        C.inc("v")
                    C.wait("sync", "v")
                    C.dma("sync", "st", lambda e: e.dma_start(
                        out=outr[:, :, gsl][:, :, sl], in_=OB[:, :, :]))
                    C.wait("vector", "st")

                emit_ln_fm(C, lnt, Y, TG, apply)
                C.wait("sync", "v")
                C.wait("sync", "s")
                C.wait("sync", "m")
            C.wait("sync", "st")

        run_program(nc, ["ld", "lw", "v", "s", "m", "fv", "st"], prog)
    return nc



def build_l1a_A():
    nc = bass.Bass("TRN2", target_bir_lowering=False, debug=False,
                   detect_race_conditions=RACE_DETECT)
    TE = TOK + 2
    xo = nc.dram_tensor("xo", [D, TE], F32, kind="ExternalInput").ap()
    nsc = nc.dram_tensor("nsc", [128, NC_D], F32, kind="ExternalInput").ap()
    inw = nc.dram_tensor("inw", [D, 2 * D], F32, kind="ExternalInput").ap()
    inb = nc.dram_tensor("inb", [128, 16], F32, kind="ExternalInput").ap()
    dww = nc.dram_tensor("dww", [128, NC_D * 3], F32, kind="ExternalInput").ap()
    dwb = nc.dram_tensor("dwb", [128, NC_D], F32, kind="ExternalInput").ap()
    pww = nc.dram_tensor("pww", [D, D], F32, kind="ExternalInput").ap()
    pwb = nc.dram_tensor("pwb", [128, NC_D], F32, kind="ExternalInput").ap()
    winf = nc.dram_tensor("winf", [D, 64], F32, kind="ExternalInput").ap()
    winb = nc.dram_tensor("winb", [D, 64], F32, kind="ExternalInput").ap()
    sspar = nc.dram_tensor("sspar", [64, 6], F32, kind="ExternalInput").ap()
    edge = nc.dram_tensor("edge", [128, 2], F32, kind="ExternalInput").ap()
    t_out = nc.dram_tensor("t_out", [D, TOK], F32, kind="ExternalOutput").ap()
    g_out = nc.dram_tensor("g_out", [D, TOK], BF16, kind="ExternalOutput").ap()
    hf_out = nc.dram_tensor("hf_out", [64, 2 * TOK], F32,
                            kind="ExternalOutput").ap()

    es = ExitStack()
    with es:
        XT = es.enter_context(nc.sbuf_tensor("XT", [128, NC_D, 514], F32))
        TT = es.enter_context(nc.sbuf_tensor("TT", [128, NC_D, 514], F32))
        TNB = es.enter_context(nc.sbuf_tensor("TNB", [128, NC_D, 514], BF16))
        XC = es.enter_context(nc.sbuf_tensor("XC", [128, NC_D, 514], F32))
        CVB = es.enter_context(nc.sbuf_tensor("CVB", [128, NC_D, 512], BF16))
        GAT = es.enter_context(nc.sbuf_tensor("GAT", [128, NC_D, 512], BF16))
        CVT = es.enter_context(nc.sbuf_tensor("CVT", [128, 512], F32))
        W = es.enter_context(nc.sbuf_tensor("W", [128, NC_D, 2 * D], BF16))
        WIN = es.enter_context(nc.sbuf_tensor("WIN", [128, NC_D, 128], BF16))
        NSC = es.enter_context(nc.sbuf_tensor("NSC", [128, NC_D], F32))
        INB = es.enter_context(nc.sbuf_tensor("INB", [128, 16], F32))
        DWW = es.enter_context(nc.sbuf_tensor("DWW", [128, NC_D, 3], F32))
        DWB = es.enter_context(nc.sbuf_tensor("DWB", [128, NC_D], F32))
        PWB = es.enter_context(nc.sbuf_tensor("PWB", [128, NC_D], F32))
        SSP = es.enter_context(nc.sbuf_tensor("SSP", [64, 6], F32))
        EDG = es.enter_context(nc.sbuf_tensor("EDG", [128, 2], F32))
        PRM = es.enter_context(nc.sbuf_tensor("PRM", [64, 8], F32))
        ABC = es.enter_context(nc.sbuf_tensor("ABC", [64, 4], F32))
        ABT = es.enter_context(nc.sbuf_tensor("ABT", [64, 2, TOK], F32))
        UF = es.enter_context(nc.sbuf_tensor("UF", [64, 2, TOK], F32))
        HF = es.enter_context(nc.sbuf_tensor("HF", [64, 2, TOK], F32))
        Z0 = es.enter_context(nc.sbuf_tensor("Z0", [64, 1], F32))
        MSK = es.enter_context(nc.sbuf_tensor("MSK", [64, 1], mybir.dt.int32))
        lnt = alloc_ln_scratch(es, nc, NC_D, 514)
        ps = [es.enter_context(nc.psum_tensor(f"ps{i}", [128, 512], F32))
              for i in range(2)]
        psU = es.enter_context(nc.psum_tensor("psU", [64, 512], F32))

        def prog(C):
            xor = xo.rearrange("(c p) t -> p c t", p=128)
            tor = t_out.rearrange("(c p) t -> p c t", p=128)
            gor = g_out.rearrange("(c p) t -> p c t", p=128)
            C.dma("gpsimd", "lw", lambda e: e.dma_start(out=NSC[:, :], in_=nsc))
            C.dma("gpsimd", "lw", lambda e: e.dma_start(out=INB[:, :], in_=inb))
            C.dma("gpsimd", "lw", lambda e: e.dma_start(
                out=DWW[:, :, :], in_=dww.rearrange("p (c k) -> p c k", k=3)))
            C.dma("gpsimd", "lw", lambda e: e.dma_start(out=DWB[:, :], in_=dwb))
            C.dma("gpsimd", "lw", lambda e: e.dma_start(out=PWB[:, :], in_=pwb))
            C.dma("gpsimd", "lw", lambda e: e.dma_start(out=SSP[:, :], in_=sspar))
            C.dma("gpsimd", "lw", lambda e: e.dma_start(out=EDG[:, :], in_=edge))
            C.dma("gpsimd", "lw", lambda e: e.dma_start(
                out=WIN[:, :, 0:64],
                in_=winf.rearrange("(c p) n -> p c n", p=128)))
            C.dma("gpsimd", "lw", lambda e: e.dma_start(
                out=WIN[:, :, 64:128],
                in_=winb.rearrange("(c p) n -> p c n", p=128)))
            emit_ln_const_init(C, lnt)

            C.wait("scalar", "lw")
            for di in range(2):
                la, lb, ldt = 3 * di, 3 * di + 1, 3 * di + 2
                C.op("scalar", "s", lambda e: e.activation(
                    out=PRM[:, 0:1], in_=SSP[:, la:la + 1], func=AF.Exp))
                C.op("scalar", "s", lambda e: e.activation(
                    out=PRM[:, 1:2], in_=SSP[:, ldt:ldt + 1], func=AF.Exp))
                C.wait("vector", "s")
                if C.on("vector"):
                    i = C.eng.tensor_tensor(PRM[:, 2:3], PRM[:, 0:1],
                                            PRM[:, 1:2], OP.mult)
                else:
                    i = None
                C.tag_fenced("vector", i, "v")
                C.wait("scalar", "v")
                C.op("scalar", "s", lambda e: e.activation(
                    out=PRM[:, 3:4], in_=PRM[:, 2:3], func=AF.Exp, scale=-1.0))
                C.wait("vector", "s")
                if C.on("vector"):
                    i = C.eng.reciprocal(PRM[:, 4:5], PRM[:, 0:1])
                else:
                    i = None
                C.tag_fenced("vector", i, "v")
                if C.on("vector"):
                    i = C.eng.tensor_scalar(PRM[:, 5:6], PRM[:, 3:4], -1.0, 1.0,
                                            OP.mult, OP.add)
                else:
                    i = None
                C.tag_fenced("vector", i, "v")
                if C.on("vector"):
                    i = C.eng.tensor_tensor(PRM[:, 7:8], PRM[:, 5:6],
                                            PRM[:, 4:5], OP.mult)
                else:
                    i = None
                C.tag_fenced("vector", i, "v")
                if C.on("vector"):
                    i = C.eng.tensor_scalar(MSK[:, 0:1], SSP[:, la:la + 1],
                                            float(np.log(1e-6)), None, OP.is_lt)
                else:
                    i = None
                C.tag_fenced("vector", i, "v")
                if C.on("vector"):
                    i = C.eng.select(Z0[:, 0:1], MSK[:, 0:1], PRM[:, 1:2],
                                     PRM[:, 7:8])
                else:
                    i = None
                C.tag_fenced("vector", i, "v")
                if C.on("vector"):
                    C.eng.tensor_tensor(ABC[:, 2 * di + 1:2 * di + 2],
                                        Z0[:, 0:1], SSP[:, lb:lb + 1], OP.mult)
                    i = C.eng.tensor_copy(ABC[:, 2 * di:2 * di + 1], PRM[:, 3:4])
                else:
                    i = None
                C.tag_fenced("vector", i, "v")
            if C.on("vector"):
                C.eng.tensor_copy(ABT[:, 0, :],
                                  ABC[:, 0:1].broadcast_to((64, TOK)))
                i = C.eng.tensor_copy(ABT[:, 1, :],
                                      ABC[:, 2:3].broadcast_to((64, TOK)))
                C.tag(i, "v")
            else:
                C.inc("v")

            for g in range(TOK // 512):
                t0 = g * 512
                C.wait("sync", "s")
                C.wait("sync", "m")
                C.wait("sync", "v")
                C.dma("sync", "ld", lambda e: e.dma_start(
                    out=XT[:, :, :], in_=xor[:, :, t0:t0 + 514]))
                if g > 0:
                    C.wait("gpsimd", "m")
                    C.dma("gpsimd", "lw", lambda e: e.dma_start(
                        out=W[:, :, 0:D],
                        in_=inw[:, 0:D].rearrange("(c p) o -> p c o", p=128)))
                else:
                    C.dma("gpsimd", "lw", lambda e: e.dma_start(
                        out=W[:, :, :],
                        in_=inw.rearrange("(c p) o -> p c o", p=128)))
                C.wait("tensor", "ld")
                C.wait("scalar", "ld")
                C.wait("vector", "ld")
                C.wait("tensor", "lw")
                C.wait("scalar", "lw")

                def applyT(C, ti, base=0):
                    sl = slice(base + ti * 512, base + (ti + 1) * 512)
                    if C.on("vector"):
                        C.eng.tensor_tensor(TT[:, :, sl], XT[:, :, sl],
                                            bcN(lnt["ps_R"], NC_D), OP.mult)
                        i = C.eng.tensor_tensor(TT[:, :, sl], TT[:, :, sl],
                                                bcN(lnt["ps_M"], NC_D),
                                                OP.subtract)
                        C.tag(i, "v")
                    else:
                        C.inc("v")
                    C.wait("tensor", "v")

                emit_ln_fm(C, lnt, XT[:, :, 0:512], 512, applyT)
                emit_ln_fm(C, lnt, XT[:, :, 2:514], 512,
                           lambda C, ti: applyT(C, ti, base=2))
                C.wait("sync", "v")
                C.dma("sync", "st", lambda e: e.dma_start(
                    out=tor[:, :, t0:t0 + 512], in_=TT[:, :, 1:513]))

                def applyR(C, ti, base=0):
                    sl = slice(base + ti * 512, base + (ti + 1) * 512)
                    if C.on("vector"):
                        C.eng.tensor_tensor(lnt["x2t"][:, :, :], TT[:, :, sl],
                                            bcN(lnt["ps_R"], NC_D), OP.mult)
                        for c in range(NC_D):
                            i = C.eng.tensor_scalar(TNB[:, c, sl],
                                                    lnt["x2t"][:, c, :],
                                                    NSC[:, c:c + 1], None,
                                                    OP.mult)
                        C.tag(i, "v")
                    else:
                        C.inc("v")
                    C.wait("tensor", "v")
                    C.wait("scalar", "v")

                emit_ln_fm(C, lnt, TT[:, :, 0:512], 512, applyR, rms=True,
                           eps=1e-6)
                emit_ln_fm(C, lnt, TT[:, :, 2:514], 512,
                           lambda C, ti: applyR(C, ti, base=2), rms=True,
                           eps=1e-6)

                sev = []
                for oc in range(NC_D):
                    for (a0, aw) in ((0, 512), (512, 2)):
                        bank = len(sev) % 2
                        if len(sev) >= 2:
                            C.wait("tensor", "s", sev[-2])
                        if C.on("tensor"):
                            for c in range(NC_D):
                                i = C.eng.matmul(ps[bank][:, 0:aw],
                                                 W[:, c, oc * 128:(oc + 1) * 128],
                                                 TNB[:, c, a0:a0 + aw],
                                                 start=(c == 0),
                                                 stop=(c == NC_D - 1))
                            C.tag(i, "m")
                        else:
                            C.inc("m")
                        C.wait("scalar", "m")
                        C.op("scalar", "s", lambda e, oc=oc, a0=a0, aw=aw,
                             bank=bank: e.activation(
                                 out=XC[:, oc, a0:a0 + aw], in_=ps[bank][:, 0:aw],
                                 func=AF.Identity, bias=INB[:, oc:oc + 1]))
                        sev.append(C.cnt["s"])
                for oc in range(NC_D):
                    bank = len(sev) % 2
                    if len(sev) >= 2:
                        C.wait("tensor", "s", sev[-2])
                    if C.on("tensor"):
                        for c in range(NC_D):
                            i = C.eng.matmul(ps[bank][:, :],
                                             W[:, c, D + oc * 128:D + (oc + 1) * 128],
                                             TNB[:, c, 1:513], start=(c == 0),
                                             stop=(c == NC_D - 1))
                        C.tag(i, "m")
                    else:
                        C.inc("m")
                    C.wait("scalar", "m")
                    C.op("scalar", "s", lambda e, oc=oc, bank=bank: e.activation(
                        out=GAT[:, oc, :], in_=ps[bank][:, :], func=AF.Sigmoid,
                        bias=INB[:, 8 + oc:9 + oc]))
                    sev.append(C.cnt["s"])
                C.wait("sync", "s")
                C.dma("sync", "st", lambda e: e.dma_start(
                    out=gor[:, :, t0:t0 + 512], in_=GAT[:, :, :]))

                C.wait("vector", "s")
                C.wait("vector", "lw")
                if g == 0:
                    if C.on("vector"):
                        i = C.eng.tensor_scalar(XC[:, :, 0:1], XC[:, :, 0:1],
                                                EDG[:, 0:1], None, OP.mult)
                        C.tag(i, "v")
                    else:
                        C.inc("v")
                if g == TOK // 512 - 1:
                    if C.on("vector"):
                        i = C.eng.tensor_scalar(XC[:, :, 513:514],
                                                XC[:, :, 513:514],
                                                EDG[:, 1:2], None, OP.mult)
                        C.tag(i, "v")
                    else:
                        C.inc("v")
                for c in range(NC_D):
                    if C.on("vector"):
                        C.eng.tensor_scalar(CVT[:, :], XC[:, c, 0:512],
                                            DWW[:, c, 0:1], None, OP.mult)
                        C.eng.scalar_tensor_tensor(CVT[:, :], XC[:, c, 1:513],
                                                   DWW[:, c, 1:2], CVT[:, :],
                                                   OP.mult, OP.add)
                        C.eng.scalar_tensor_tensor(CVT[:, :], XC[:, c, 2:514],
                                                   DWW[:, c, 2:3], CVT[:, :],
                                                   OP.mult, OP.add)
                        i = C.eng.tensor_scalar(CVB[:, c, :], CVT[:, :],
                                                DWB[:, c:c + 1], None, OP.add)
                    else:
                        i = None
                    C.tag_fenced("vector", i, "v")
                C.wait("gpsimd", "m")
                C.dma("gpsimd", "lw", lambda e: e.dma_start(
                    out=W[:, :, 0:D],
                    in_=pww.rearrange("(c p) o -> p c o", p=128)))
                C.wait("tensor", "lw")
                C.wait("tensor", "v")
                sev2 = []
                for oc in range(NC_D):
                    bank = len(sev2) % 2
                    if len(sev2) >= 2:
                        C.wait("tensor", "s", sev2[-2])
                    if C.on("tensor"):
                        for c in range(NC_D):
                            i = C.eng.matmul(ps[bank][:, :],
                                             W[:, c, oc * 128:(oc + 1) * 128],
                                             CVB[:, c, :], start=(c == 0),
                                             stop=(c == NC_D - 1))
                        C.tag(i, "m")
                    else:
                        C.inc("m")
                    C.wait("scalar", "m")
                    C.op("scalar", "s", lambda e, oc=oc, bank=bank: e.activation(
                        out=XC[:, oc, 0:512], in_=ps[bank][:, :],
                        func=AF.Identity, bias=PWB[:, oc:oc + 1]))
                    sev2.append(C.cnt["s"])
                C.wait("vector", "s")
                if C.on("vector"):
                    i = C.eng.tensor_copy(CVB[:, :, :], XC[:, :, 0:512])
                    C.tag(i, "v")
                else:
                    C.inc("v")
                C.wait("tensor", "v")
                for di in range(2):
                    if C.on("tensor"):
                        for c in range(NC_D):
                            i = C.eng.matmul(psU[:, :],
                                             WIN[:, c, di * 64:(di + 1) * 64],
                                             CVB[:, :, :][:, c, :],
                                             start=(c == 0),
                                             stop=(c == NC_D - 1))
                        C.tag(i, "m")
                    else:
                        C.inc("m")
                    C.wait("vector", "m")
                    if C.on("vector"):
                        i = C.eng.tensor_scalar(UF[:, di, t0:t0 + 512],
                                                psU[:, :],
                                                ABC[:, 2 * di + 1:2 * di + 2],
                                                None, OP.mult)
                    else:
                        i = None
                    C.tag_fenced("vector", i, "v")
                    C.wait("tensor", "v")

            if C.on("vector"):
                i = C.eng.engine_nop()
            else:
                i = None
            C.fence("vector", i)
            if C.on("vector"):
                C.eng.tensor_tensor_scan(HF[:, 0, :], ABT[:, 0, :], UF[:, 0, :],
                                         0.0, OP.mult, OP.add)
                i = C.eng.tensor_tensor_scan(HF[:, 1, ::-1], ABT[:, 1, ::-1],
                                             UF[:, 1, ::-1], 0.0,
                                             OP.mult, OP.add)
                C.tag(i, "v")
            else:
                C.inc("v")
            C.wait("sync", "v")
            C.dma("sync", "st", lambda e: e.dma_start(
                out=hf_out, in_=HF[:, :, :].rearrange("p a t -> p (a t)")))
            C.wait("sync", "st")

        run_program(nc, ["ld", "lw", "v", "s", "m", "fv", "st"], prog)
    return nc



def build_l1a_B(dbg=False):
    nc = bass.Bass("TRN2", target_bir_lowering=False, debug=False,
                   detect_race_conditions=RACE_DETECT)
    hfin = nc.dram_tensor("hfin", [64, 2 * TOK], F32, kind="ExternalInput").ap()
    carry = nc.dram_tensor("carry", [64, 2], F32, kind="ExternalInput").ap()
    sspar = nc.dram_tensor("sspar", [64, 6], F32, kind="ExternalInput").ap()
    cpar = nc.dram_tensor("cpar", [64, 2], F32, kind="ExternalInput").ap()
    woutT = nc.dram_tensor("woutT", [64, 2 * D], F32, kind="ExternalInput").ap()
    outwT = nc.dram_tensor("outwT", [D, D], F32, kind="ExternalInput").ap()
    outb = nc.dram_tensor("outb", [128, NC_D], F32, kind="ExternalInput").ap()
    gin = nc.dram_tensor("gin", [D, TOK], BF16, kind="ExternalInput").ap()
    tin = nc.dram_tensor("tin", [D, TOK], F32, kind="ExternalInput").ap()
    xin = nc.dram_tensor("xin", [D, TOK], F32, kind="ExternalInput").ap()
    out = nc.dram_tensor("out", [D, TOK], F32, kind="ExternalOutput").ap()
    if dbg:
        d_pow = nc.dram_tensor("d_pow", [64, 2 * TOK], F32,
                               kind="ExternalOutput").ap()
        d_hf = nc.dram_tensor("d_hf", [64, 2 * TOK], F32,
                              kind="ExternalOutput").ap()
        d_y = nc.dram_tensor("d_y", [D, TOK], F32, kind="ExternalOutput").ap()
    TG = 512

    es = ExitStack()
    with es:
        HF = es.enter_context(nc.sbuf_tensor("HF", [64, 2, TOK], F32))
        HTB = es.enter_context(nc.sbuf_tensor("HTB", [64, 2, TOK], BF16))
        IOT = es.enter_context(nc.sbuf_tensor("IOT", [64, 2, TOK], F32))
        POW = es.enter_context(nc.sbuf_tensor("POW", [64, TOK], F32))
        CAR = es.enter_context(nc.sbuf_tensor("CAR", [64, 2], F32))
        SSP = es.enter_context(nc.sbuf_tensor("SSP", [64, 6], F32))
        CPA = es.enter_context(nc.sbuf_tensor("CPA", [64, 2], F32))
        PRM = es.enter_context(nc.sbuf_tensor("PRM", [64, 4], F32))
        WOT = es.enter_context(nc.sbuf_tensor("WOT", [64, 2, D], BF16))
        OWT = es.enter_context(nc.sbuf_tensor("OWT", [128, NC_D, D], BF16))
        OBC = es.enter_context(nc.sbuf_tensor("OBC", [128, NC_D], F32))
        GATE = es.enter_context(nc.sbuf_tensor("GATE", [128, NC_D, TG], BF16))
        TIN = es.enter_context(nc.sbuf_tensor("TIN", [128, NC_D, TG], F32))
        XIN = es.enter_context(nc.sbuf_tensor("XIN", [128, NC_D, TG], F32))
        Y = es.enter_context(nc.sbuf_tensor("Y", [128, NC_D, TG], F32))
        YGB = es.enter_context(nc.sbuf_tensor("YGB", [128, NC_D, TG], BF16))
        OB = es.enter_context(nc.sbuf_tensor("OB", [128, NC_D, 512], F32))
        lnt = alloc_ln_scratch(es, nc, NC_D, TG)
        psY = es.enter_context(nc.psum_tensor("psY", [128, 512], F32))
        psO = es.enter_context(nc.psum_tensor("psO", [128, 512], F32))

        def prog(C):
            gir = gin.rearrange("(c p) t -> p c t", p=128)
            tir = tin.rearrange("(c p) t -> p c t", p=128)
            xir = xin.rearrange("(c p) t -> p c t", p=128)
            outr = out.rearrange("(c p) t -> p c t", p=128)
            C.dma("sync", "ld", lambda e: e.dma_start(
                out=HF[:, :, :], in_=hfin.rearrange("p (a t) -> p a t", a=2)))
            C.dma("gpsimd", "lw", lambda e: e.dma_start(out=CAR[:, :], in_=carry))
            C.dma("gpsimd", "lw", lambda e: e.dma_start(out=SSP[:, :], in_=sspar))
            C.dma("gpsimd", "lw", lambda e: e.dma_start(out=CPA[:, :], in_=cpar))
            C.dma("gpsimd", "lw", lambda e: e.dma_start(
                out=WOT[:, :, :], in_=woutT.rearrange("p (a d) -> p a d", a=2)))
            C.dma("gpsimd", "lw", lambda e: e.dma_start(
                out=OWT[:, :, :], in_=outwT.rearrange("(c p) o -> p c o", p=128)))
            C.dma("gpsimd", "lw", lambda e: e.dma_start(out=OBC[:, :], in_=outb))
            emit_ln_const_init(C, lnt)
            C.op("gpsimd", "v", lambda e: e.iota(
                IOT[:, 0, :], pattern=[[-1, TOK]], base=-1, channel_multiplier=0,
                allow_small_or_imprecise_dtypes=True))
            C.op("gpsimd", "v", lambda e: e.iota(
                IOT[:, 1, :], pattern=[[1, TOK]], base=-TOK,
                channel_multiplier=0, allow_small_or_imprecise_dtypes=True))
            C.wait("scalar", "lw")
            C.wait("vector", "lw")
            C.wait("vector", "ld")
            for di in range(2):
                la, ldt = 3 * di, 3 * di + 2
                C.op("scalar", "s", lambda e, la=la: e.activation(
                    out=PRM[:, 0:1], in_=SSP[:, la:la + 1], func=AF.Exp))
                C.op("scalar", "s", lambda e, ldt=ldt: e.activation(
                    out=PRM[:, 1:2], in_=SSP[:, ldt:ldt + 1], func=AF.Exp))
                C.wait("vector", "s")
                if C.on("vector"):
                    i = C.eng.tensor_tensor(PRM[:, 2:3], PRM[:, 0:1],
                                            PRM[:, 1:2], OP.mult)
                else:
                    i = None
                C.tag_fenced("vector", i, "v")
                C.wait("scalar", "v")
                ram = IOT[:, di, :]
                C.op("scalar", "s", lambda e, ram=ram: e.activation(
                    out=POW[:, :], in_=ram, func=AF.Exp, scale=PRM[:, 2:3]))
                C.wait("vector", "s")
                if C.on("vector"):
                    i = C.eng.scalar_tensor_tensor(HF[:, di, :], POW[:, :],
                                                   CAR[:, di:di + 1],
                                                   HF[:, di, :], OP.mult, OP.add)
                else:
                    i = None
                C.tag_fenced("vector", i, "v")
                if C.on("vector"):
                    i = C.eng.tensor_scalar(HTB[:, di, :], HF[:, di, :],
                                            CPA[:, di:di + 1], None, OP.mult)
                    C.tag(i, "v")
                else:
                    C.inc("v")
                if dbg:
                    C.wait("sync", "v")
                    C.dma("sync", "st", lambda e, di=di: e.dma_start(
                        out=d_pow.rearrange("p (a t) -> p a t", a=2)[:, di, :],
                        in_=POW[:, :]))
                    C.dma("sync", "st", lambda e, di=di: e.dma_start(
                        out=d_hf.rearrange("p (a t) -> p a t", a=2)[:, di, :],
                        in_=HF[:, di, :]))
            C.wait("tensor", "lw")
            C.wait("tensor", "v")
            for g in range(TOK // TG):
                gsl = slice(g * TG, (g + 1) * TG)
                C.wait("sync", "s")
                C.wait("sync", "m")
                C.wait("sync", "v")
                C.dma("sync", "ld", lambda e: e.dma_start(
                    out=GATE[:, :, :], in_=gir[:, :, gsl]))
                C.dma("sync", "ld", lambda e: e.dma_start(
                    out=TIN[:, :, :], in_=tir[:, :, gsl]))
                C.dma("sync", "ld", lambda e: e.dma_start(
                    out=XIN[:, :, :], in_=xir[:, :, gsl]))
                C.wait("tensor", "ld")
                C.wait("vector", "ld")
                for oc in range(NC_D):
                    for (tt0, tw) in _tiles(TG):
                        if C.on("tensor"):
                            C.eng.matmul(psY[:, 0:tw],
                                         WOT[:, 0, oc * 128:(oc + 1) * 128],
                                         HTB[:, 0, g * TG + tt0:g * TG + tt0 + tw],
                                         start=True, stop=False)
                            i = C.eng.matmul(psY[:, 0:tw],
                                             WOT[:, 1, oc * 128:(oc + 1) * 128],
                                             HTB[:, 1, g * TG + tt0:g * TG + tt0 + tw],
                                             start=False, stop=True)
                            C.tag(i, "m")
                        else:
                            C.inc("m")
                        C.wait("vector", "m")
                        if C.on("vector"):
                            i = C.eng.tensor_tensor(
                                YGB[:, oc, tt0:tt0 + tw], psY[:, 0:tw],
                                GATE[:, oc, tt0:tt0 + tw], OP.mult)
                            C.tag(i, "v")
                        else:
                            C.inc("v")
                        C.wait("tensor", "v")
                for oc in range(NC_D):
                    for (tt0, tw) in _tiles(TG):
                        if C.on("tensor"):
                            for c in range(NC_D):
                                i = C.eng.matmul(psO[:, 0:tw],
                                                 OWT[:, c, oc * 128:(oc + 1) * 128],
                                                 YGB[:, c, tt0:tt0 + tw],
                                                 start=(c == 0),
                                                 stop=(c == NC_D - 1))
                            C.tag(i, "m")
                        else:
                            C.inc("m")
                        C.wait("vector", "m")
                        if C.on("vector"):
                            i = C.eng.scalar_tensor_tensor(
                                Y[:, oc, tt0:tt0 + tw], psO[:, 0:tw],
                                OBC[:, oc:oc + 1], TIN[:, oc, tt0:tt0 + tw],
                                OP.add, OP.add)
                            C.tag(i, "v")
                        else:
                            C.inc("v")
                        C.wait("tensor", "v")
                        if dbg:
                            C.wait("sync", "v")
                            C.dma("sync", "st", lambda e, oc=oc, tt0=tt0, tw=tw:
                                  e.dma_start(
                                      out=out.rearrange(
                                          "(c p) t -> p c t", p=128)[
                                          :, oc, gsl][:, tt0:tt0 + tw]
                                      if False else
                                      d_y.rearrange("(c p) t -> p c t", p=128)[
                                          :, oc, gsl][:, tt0:tt0 + tw],
                                      in_=Y[:, oc, tt0:tt0 + tw]))
                C.wait("scalar", "v")

                def apply(C, ti):
                    sl = slice(ti * 512, (ti + 1) * 512)
                    if C.on("vector"):
                        C.eng.tensor_tensor(OB[:, :, :], Y[:, :, sl],
                                            bcN(lnt["ps_R"], NC_D), OP.mult)
                        C.eng.tensor_tensor(OB[:, :, :], OB[:, :, :],
                                            bcN(lnt["ps_M"], NC_D), OP.subtract)
                        i = C.eng.tensor_tensor(OB[:, :, :], OB[:, :, :],
                                                XIN[:, :, sl], OP.add)
                        C.tag(i, "v")
                    else:
                        C.inc("v")
                    C.wait("sync", "v")
                    C.dma("sync", "st", lambda e: e.dma_start(
                        out=outr[:, :, gsl][:, :, sl], in_=OB[:, :, :]))
                    C.wait("vector", "st")

                emit_ln_fm(C, lnt, Y, TG, apply)
            C.wait("sync", "st")

        run_program(nc, ["ld", "lw", "v", "s", "m", "fv", "st"], prog)
    return nc



DH1 = 4096
NCH = DH1 // 128


def build_l1f_a():
    nc = bass.Bass("TRN2", target_bir_lowering=False, debug=False,
                   detect_race_conditions=RACE_DETECT)
    xin = nc.dram_tensor("xin", [D, TOK], F32, kind="ExternalInput").ap()
    w1T = nc.dram_tensor("w1T", [D, DH1], F32, kind="ExternalInput").ap()
    w2T = nc.dram_tensor("w2T", [D, DH1], F32, kind="ExternalInput").ap()
    s1c = nc.dram_tensor("s1c", [128, NCH], F32, kind="ExternalInput").ap()
    s2c = nc.dram_tensor("s2c", [128, NCH], F32, kind="ExternalInput").ap()
    y1o = nc.dram_tensor("y1o", [DH1, TOK], BF16, kind="ExternalOutput").ap()
    TG = 512
    es = ExitStack()
    with es:
        XIN = es.enter_context(nc.sbuf_tensor("XIN", [128, NC_D, TG], F32))
        ZB = es.enter_context(nc.sbuf_tensor("ZB", [128, NC_D, TOK], BF16))
        WA = es.enter_context(nc.sbuf_tensor("WA", [128, 2, NCH, 128], BF16))
        SIL = es.enter_context(nc.sbuf_tensor("SIL", [128, 512], BF16))
        OUTB = es.enter_context(nc.sbuf_tensor("OUTB", [128, 2, 512], BF16))
        S1 = es.enter_context(nc.sbuf_tensor("S1", [128, NCH], F32))
        S2 = es.enter_context(nc.sbuf_tensor("S2", [128, NCH], F32))
        lnt = alloc_ln_scratch(es, nc, NC_D, TG)
        psA = es.enter_context(nc.psum_tensor("psA", [128, 512], F32))
        psG = es.enter_context(nc.psum_tensor("psG", [128, 512], F32))

        def prog(C):
            xir = xin.rearrange("(c p) t -> p c t", p=128)
            y1r = y1o.rearrange("(c p) t -> p c t", p=128)
            for dst, srcap in ((S1, s1c), (S2, s2c)):
                C.dma("gpsimd", "lw",
                      lambda e, dst=dst, srcap=srcap: e.dma_start(
                          out=dst[:, :], in_=srcap))
            emit_ln_const_init(C, lnt)
            C.wait("scalar", "lw")
            C.wait("vector", "lw")
            for g in range(TOK // TG):
                gsl = slice(g * TG, (g + 1) * TG)
                C.wait("sync", "v")
                C.wait("sync", "m")
                C.wait("sync", "s")
                C.dma("sync", "ld", lambda e: e.dma_start(
                    out=XIN[:, :, :], in_=xir[:, :, gsl]))
                C.wait("tensor", "ld")
                C.wait("scalar", "ld")
                C.wait("vector", "ld")

                def applyZ(C, ti):
                    if C.on("vector"):
                        C.eng.tensor_tensor(lnt["x2t"][:, :, :], XIN[:, :, :],
                                            bcN(lnt["ps_R"], NC_D), OP.mult)
                        i = C.eng.tensor_tensor(ZB[:, :, gsl],
                                                lnt["x2t"][:, :, :],
                                                bcN(lnt["ps_M"], NC_D),
                                                OP.subtract)
                        C.tag(i, "v")
                    else:
                        C.inc("v")
                    C.wait("tensor", "v")

                emit_ln_fm(C, lnt, XIN, TG, applyZ)

            def stream_w(wap, nk, oc, buf):
                C.wait("gpsimd", "m")
                C.dma("gpsimd", "lw", lambda e: e.dma_start(
                    out=WA[:, buf, 0:nk, :],
                    in_=wap.rearrange("(c p) o -> p c o", p=128)[
                        :, :, oc * 128:(oc + 1) * 128]))

            nst = []
            for oc in range(NCH):
                stream_w(w1T, NC_D, oc, 0)
                stream_w(w2T, NC_D, oc, 1)
                C.wait("tensor", "lw")
                for tj, (tt0, tw) in enumerate(_tiles(TOK)):
                    ob = (oc * len(_tiles(TOK)) + tj) % 2
                    if len(nst) >= 2:
                        C.wait("vector", f"st{ob}", nst[-2])
                    if C.on("tensor"):
                        for c in range(NC_D):
                            C.eng.matmul(psA[:, 0:tw], WA[:, 0, c, :],
                                         ZB[:, c, tt0:tt0 + tw], start=(c == 0),
                                         stop=(c == NC_D - 1))
                        for c in range(NC_D):
                            i = C.eng.matmul(psG[:, 0:tw], WA[:, 1, c, :],
                                             ZB[:, c, tt0:tt0 + tw],
                                             start=(c == 0),
                                             stop=(c == NC_D - 1))
                        C.tag(i, "m")
                    else:
                        C.inc("m")
                    C.wait("scalar", "m")
                    C.op("scalar", "s", lambda e, oc=oc, tw=tw: e.activation(
                        out=SIL[:, 0:tw], in_=psA[:, 0:tw], func=AF.Silu,
                        scale=S1[:, oc:oc + 1]))
                    C.wait("vector", "s")
                    if C.on("vector"):
                        C.eng.tensor_scalar(psG[:, 0:tw], psG[:, 0:tw],
                                            S2[:, oc:oc + 1], None, OP.mult)
                        i = C.eng.tensor_tensor(OUTB[:, ob, 0:tw], psG[:, 0:tw],
                                                SIL[:, 0:tw], OP.mult)
                        C.tag(i, "v")
                    else:
                        C.inc("v")
                    C.wait("tensor", "v")
                    C.wait("scalar", "v")
                    C.wait("sync", "v")
                    C.dma("sync", f"st{ob}", lambda e, oc=oc, tt0=tt0, tw=tw,
                          ob=ob: e.dma_start(out=y1r[:, oc, tt0:tt0 + tw],
                                             in_=OUTB[:, ob, 0:tw]))
                    nst.append(C.cnt[f"st{ob}"])
            C.wait("sync", "st0")
            C.wait("sync", "st1")

        run_program(nc, ["ld", "lw", "v", "s", "m", "fv", "st0", "st1"], prog)
    return nc


def build_l1f_b():
    nc = bass.Bass("TRN2", target_bir_lowering=False, debug=False,
                   detect_race_conditions=RACE_DETECT)
    y1i = nc.dram_tensor("y1i", [DH1, TOK], BF16, kind="ExternalInput").ap()
    h2T = nc.dram_tensor("h2T", [DH1, DH1], F32, kind="ExternalInput").ap()
    s4c = nc.dram_tensor("s4c", [128, NCH], F32, kind="ExternalInput").ap()
    y2o = nc.dram_tensor("y2o", [DH1, TOK], BF16, kind="ExternalOutput").ap()
    es = ExitStack()
    with es:
        HB = es.enter_context(nc.sbuf_tensor("HB", [128, NCH, TOK], BF16))
        WA = es.enter_context(nc.sbuf_tensor("WA", [128, 2, NCH, 128], BF16))
        OUTB = es.enter_context(nc.sbuf_tensor("OUTB", [128, 2, 512], BF16))
        S4 = es.enter_context(nc.sbuf_tensor("S4", [128, NCH], F32))
        psA = es.enter_context(nc.psum_tensor("psA", [128, 512], F32))

        def prog(C):
            y1r = y1i.rearrange("(c p) t -> p c t", p=128)
            y2r = y2o.rearrange("(c p) t -> p c t", p=128)
            for q4 in range(4):
                C.dma("sync", "ld", lambda e, q4=q4: e.dma_start(
                    out=HB[:, q4 * 8:(q4 + 1) * 8, :],
                    in_=y1r[:, q4 * 8:(q4 + 1) * 8, :]))
            C.dma("gpsimd", "lw", lambda e: e.dma_start(out=S4[:, :], in_=s4c))
            C.wait("scalar", "lw")
            C.wait("tensor", "ld")
            nst = []
            for oc in range(NCH):
                buf = oc % 2
                C.wait("gpsimd", "m")
                C.dma("gpsimd", "lw", lambda e, oc=oc, buf=buf: e.dma_start(
                    out=WA[:, buf, :, :],
                    in_=h2T.rearrange("(c p) o -> p c o", p=128)[
                        :, :, oc * 128:(oc + 1) * 128]))
                C.wait("tensor", "lw")
                for tj, (tt0, tw) in enumerate(_tiles(TOK)):
                    ob = (oc * len(_tiles(TOK)) + tj) % 2
                    if len(nst) >= 2:
                        C.wait("scalar", f"st{ob}", nst[-2])
                    C.wait("tensor", "s")
                    if C.on("tensor"):
                        for c in range(NCH):
                            i = C.eng.matmul(psA[:, 0:tw], WA[:, buf, c, :],
                                             HB[:, c, tt0:tt0 + tw],
                                             start=(c == 0), stop=(c == NCH - 1))
                        C.tag(i, "m")
                    else:
                        C.inc("m")
                    C.wait("scalar", "m")
                    C.op("scalar", "s", lambda e, tw=tw, oc=oc, ob=ob:
                         e.activation(out=OUTB[:, ob, 0:tw], in_=psA[:, 0:tw],
                                      func=AF.Gelu, scale=S4[:, oc:oc + 1]))
                    C.wait("sync", "s")
                    C.dma("sync", f"st{ob}", lambda e, oc=oc, tt0=tt0, tw=tw,
                          ob=ob: e.dma_start(
                              out=y2r[:, oc, tt0:tt0 + tw],
                              in_=OUTB[:, ob, 0:tw]))
                    nst.append(C.cnt[f"st{ob}"])
            C.wait("sync", "st0")
            C.wait("sync", "st1")

        run_program(nc, ["ld", "lw", "v", "s", "m", "fv", "st0", "st1"], prog)
    return nc


def build_l1f_c():
    nc = bass.Bass("TRN2", target_bir_lowering=False, debug=False,
                   detect_race_conditions=RACE_DETECT)
    y2i = nc.dram_tensor("y2i", [DH1, TOK], BF16, kind="ExternalInput").ap()
    w3T = nc.dram_tensor("w3T", [DH1, D], F32, kind="ExternalInput").ap()
    s5c = nc.dram_tensor("s5c", [128, NC_D], F32, kind="ExternalInput").ap()
    xin = nc.dram_tensor("xin", [D, TOK], F32, kind="ExternalInput").ap()
    out = nc.dram_tensor("out", [D, TOK], F32, kind="ExternalOutput").ap()
    TG = 512
    es = ExitStack()
    with es:
        HB = es.enter_context(nc.sbuf_tensor("HB", [128, NCH, TOK], BF16))
        WA = es.enter_context(nc.sbuf_tensor("WA", [128, 2, NCH, 128], BF16))
        S5 = es.enter_context(nc.sbuf_tensor("S5", [128, NC_D], F32))
        Y = es.enter_context(nc.sbuf_tensor("Y", [128, NC_D, TG], F32))
        XIN = es.enter_context(nc.sbuf_tensor("XIN", [128, NC_D, TG], F32))
        lnt = alloc_ln_scratch(es, nc, NC_D, TG)
        psA = es.enter_context(nc.psum_tensor("psA", [128, 512], F32))

        def prog(C):
            y2r = y2i.rearrange("(c p) t -> p c t", p=128)
            xir = xin.rearrange("(c p) t -> p c t", p=128)
            outr = out.rearrange("(c p) t -> p c t", p=128)
            for q4 in range(4):
                C.dma("sync", "ld", lambda e, q4=q4: e.dma_start(
                    out=HB[:, q4 * 8:(q4 + 1) * 8, :],
                    in_=y2r[:, q4 * 8:(q4 + 1) * 8, :]))
            C.dma("gpsimd", "lw", lambda e: e.dma_start(out=S5[:, :], in_=s5c))
            emit_ln_const_init(C, lnt)
            C.wait("vector", "lw")
            C.wait("tensor", "ld")
            for g in range(TOK // TG):
                gsl = slice(g * TG, (g + 1) * TG)
                C.wait("sync", "v")
                C.wait("sync", "m")
                C.wait("sync", "s")
                C.dma("sync", "ld", lambda e: e.dma_start(
                    out=XIN[:, :, :], in_=xir[:, :, gsl]))
                C.wait("vector", "ld")
                for oc in range(NC_D):
                    buf = oc % 2
                    C.wait("gpsimd", "m")
                    C.dma("gpsimd", "lw", lambda e, oc=oc, buf=buf: e.dma_start(
                        out=WA[:, buf, :, :],
                        in_=w3T.rearrange("(c p) o -> p c o", p=128)[
                            :, :, oc * 128:(oc + 1) * 128]))
                    C.wait("tensor", "lw")
                    for (tt0, tw) in _tiles(TG):
                        if C.on("tensor"):
                            for c in range(NCH):
                                i = C.eng.matmul(
                                    psA[:, 0:tw], WA[:, buf, c, :],
                                    HB[:, c, g * TG + tt0:g * TG + tt0 + tw],
                                    start=(c == 0), stop=(c == NCH - 1))
                            C.tag(i, "m")
                        else:
                            C.inc("m")
                        C.wait("vector", "m")
                        if C.on("vector"):
                            i = C.eng.tensor_scalar(Y[:, oc, tt0:tt0 + tw],
                                                    psA[:, 0:tw],
                                                    S5[:, oc:oc + 1], None,
                                                    OP.mult)
                            C.tag(i, "v")
                        else:
                            C.inc("v")
                        C.wait("tensor", "v")

                def apply(C, ti):
                    sl = slice(ti * 512, (ti + 1) * 512)
                    if C.on("vector"):
                        C.eng.tensor_tensor(Y[:, :, sl], Y[:, :, sl],
                                            bcN(lnt["ps_R"], NC_D), OP.mult)
                        C.eng.tensor_tensor(Y[:, :, sl], Y[:, :, sl],
                                            bcN(lnt["ps_M"], NC_D), OP.subtract)
                        i = C.eng.tensor_tensor(Y[:, :, sl], Y[:, :, sl],
                                                XIN[:, :, sl], OP.add)
                        C.tag(i, "v")
                    else:
                        C.inc("v")
                    C.wait("sync", "v")
                    C.dma("sync", "st", lambda e: e.dma_start(
                        out=outr[:, :, gsl][:, :, sl], in_=Y[:, :, sl]))
                    C.wait("vector", "st")

                emit_ln_fm(C, lnt, Y, TG, apply)
            C.wait("sync", "st")

        run_program(nc, ["ld", "lw", "v", "s", "m", "fv", "st"], prog)
    return nc



_PROG_CACHE = {}


def _get(name, builder):
    if name not in _PROG_CACHE:
        _PROG_CACHE[name] = builder()
    return _PROG_CACHE[name]


def _bf16(x):
    import ml_dtypes
    return np.asarray(x).astype(ml_dtypes.bfloat16)


def _tri_mask():
    return _bf16(np.tril(np.ones((128, 128), np.float32)).T)


def l0_attn_forward(x, Wq, bq, Wk, bk, Wv, bv, Wo, bo):
    import ml_dtypes
    xT = [np.ascontiguousarray(x[b].T) for b in range(B)]
    wcat = np.concatenate([
        np.concatenate([np.asarray(Wq).T, np.asarray(Wk).T, np.asarray(Wv).T],
                       axis=1),
        np.concatenate([bq, bk, bv])[None, :]], axis=0)
    wcat = np.ascontiguousarray(wcat, np.float32)

    nc1 = _get("l0a_proj", build_l0a_proj)
    in_maps = []
    for c in range(NCORES):
        b, h = divmod(c, 2)
        in_maps.append({
            "xo": np.ascontiguousarray(xT[b][:, h * TOK:(h + 1) * TOK]),
            "wc": wcat,
        })
    res = run_bass_kernel_spmd(nc1, in_maps, core_ids=list(range(NCORES)))
    qkv = [np.concatenate([res.results[2 * b]["qkv"], res.results[2 * b + 1]["qkv"]],
                          axis=0) for b in range(B)]

    nc2 = _get("l0a_attn", build_l0a_attn)
    tri = _tri_mask()
    in_maps = []
    for c in range(NCORES):
        b, hh = divmod(c, 2)
        hs = slice(hh * 8 * DH, (hh + 1) * 8 * DH)
        q = qkv[b][:, 0 * D:1 * D][:, hs].reshape(S, 8, DH)
        k = qkv[b][:, 1 * D:2 * D][:, hs].reshape(S, 8, DH)
        v = qkv[b][:, 2 * D:3 * D][:, hs].reshape(S, 8, DH)
        ones = np.ones((8, S, 1), ml_dtypes.bfloat16)
        in_maps.append({
            "qf": np.ascontiguousarray(q.transpose(1, 2, 0)),
            "kf": np.ascontiguousarray(k.transpose(1, 2, 0)),
            "kt": np.ascontiguousarray(k.transpose(1, 0, 2)),
            "vt": np.ascontiguousarray(
                np.concatenate([v.transpose(1, 0, 2), ones], axis=2)),
            "tri": tri})
    res = run_bass_kernel_spmd(nc2, in_maps, core_ids=list(range(NCORES)))
    at = []
    for b in range(B):
        ab = np.concatenate([res.results[2 * b]["at"], res.results[2 * b + 1]["at"]],
                            axis=0)
        at.append(np.ascontiguousarray(ab.transpose(1, 0, 2).reshape(S, D)))

    nc3 = _get("l0a_out", build_l0a_out)
    woT = np.ascontiguousarray(np.asarray(Wo).T, np.float32)
    bo_r = np.ascontiguousarray(np.asarray(bo).reshape(NC_D, 128).T, np.float32)
    in_maps = []
    for c in range(NCORES):
        b, h = divmod(c, 2)
        in_maps.append({
            "af": np.ascontiguousarray(at[b][h * TOK:(h + 1) * TOK, :].T),
            "wo": woT,
            "bo": bo_r,
            "xo": np.ascontiguousarray(xT[b][:, h * TOK:(h + 1) * TOK]),
        })
    res = run_bass_kernel_spmd(nc3, in_maps, core_ids=list(range(NCORES)))
    out = np.empty((B, S, D), np.float32)
    for c in range(NCORES):
        b, h = divmod(c, 2)
        out[b, h * TOK:(h + 1) * TOK, :] = res.results[c]["out"].T
    return out


INNER_W = 256


def _l0f_tables(o0):
    d = np.arange(128) % 64
    half = (d % 32).astype(np.float64)
    inv_freq = 1.0 / (10000.0 ** (2 * half / 64.0))
    base_scale = (2 * half + 0.4 * 64) / (1.4 * 64)
    g = np.arange(o0 - 384, o0 + TOK, dtype=np.float64)
    freqs = g[None, :] * inv_freq[:, None]
    power = (g - S // 2) / (INNER_W // 2)
    scale = base_scale[:, None] ** power[None, :]
    cos, sin = np.cos(freqs), np.sin(freqs)
    return (cos * scale).astype(np.float32), (sin * scale).astype(np.float32), \
           (cos / scale).astype(np.float32), (sin / scale).astype(np.float32)


def _rho_sigma():
    d = np.arange(128) % 64
    rho = np.where(d < 32, np.arange(128) + 32, np.arange(128) - 32)
    sigma = np.where(d < 32, -1.0, 1.0)
    return rho, sigma


def _l0f_attn_consts():
    rho, sigma = _rho_sigma()
    R = np.zeros((128, 128), np.float32)
    for op in range(128):
        R[rho[op], op] = sigma[op]
    blk = np.zeros((128, 128), np.float32)
    blk[:64, :64] = 1.0
    blk[64:, 64:] = 1.0
    idm = np.eye(128, dtype=np.float32)
    return R, blk, idm


def _l0f_masks(h):
    NEG = -1e38
    il = np.arange(128)[:, None]
    jl = np.arange(384)[None, :] % 128
    blkid = np.arange(384)[None, :] // 128
    interior = np.zeros((128, 384), np.float32)
    interior[(blkid == 0) & (jl < il)] = NEG
    interior[(blkid == 2) & (jl > il)] = NEG
    m3 = interior.copy()
    m4 = interior.copy()
    if h == 0:
        m3[blkid.repeat(128, 0) <= 1] = NEG
        m4[blkid.repeat(128, 0) == 0] = NEG
    return np.stack([m3, m4, interior])


def l0_ffn_forward(x, Wqkv, q_scale, k_scale, Wo, ln_g, ln_b, W1, b1, W2, b2, nlayers=6):
    nc_a = _get("l0f_attn", build_l0f_attn)
    nc_f = _get("l0f_ffn", build_l0f_ffn)
    rho, _ = _rho_sigma()
    R, blkm, idm = _l0f_attn_consts()
    xT = [np.ascontiguousarray(x[b].T, np.float32) for b in range(B)]
    DIM_INNER = 2730
    for li in range(nlayers):
        qs = np.asarray(q_scale[li], np.float32)
        ks = np.asarray(k_scale[li], np.float32)
        d64 = np.arange(128) % 64
        qs_c = qs[d64]
        qs_r = qs[d64[rho]]
        ks_c = ks[d64]
        ks_r = ks[d64[rho]]
        wq = np.ascontiguousarray(Wqkv[li], np.float32)
        woL = np.ascontiguousarray(Wo[li], np.float32)
        in_maps = []
        for c in range(NCORES):
            b, h = divmod(c, 2)
            o0 = h * TOK
            ext = np.zeros((D, EXT), np.float32)
            lo = max(0, o0 - 384)
            ext[:, 384 - (o0 - lo):384] = xT[b][:, lo:o0]
            ext[:, 384:] = xT[b][:, o0:o0 + TOK]
            c1, s1, c2, s2 = _l0f_tables(o0)
            in_maps.append({
                "xin": ext, "wqkv": wq, "wo": woL,
                "tq1": np.ascontiguousarray(8.0 * c1 * qs_c[:, None]),
                "tq2": np.ascontiguousarray(8.0 * s1 * qs_r[:, None]),
                "tk1": np.ascontiguousarray(c2 * ks_c[:, None]),
                "tk2": np.ascontiguousarray(s2 * ks_r[:, None]),
                "rotm": R, "blk": blkm, "idm": idm,
                "mneg": _l0f_masks(h)})
        res = run_bass_kernel_spmd(nc_a, in_maps, core_ids=list(range(NCORES)))
        for c in range(NCORES):
            b, h = divmod(c, 2)
            xT[b][:, h * TOK:(h + 1) * TOK] = res.results[c]["xout"]
        W1a = np.zeros((D, NHC * 128), np.float32)
        W1g = np.zeros((D, NHC * 128), np.float32)
        W1a[:, :DIM_INNER] = W1[li][:, :DIM_INNER]
        W1g[:, :DIM_INNER] = W1[li][:, DIM_INNER:]
        w1iL = np.zeros((D, NHC * 256), np.float32)
        for hc in range(NHC):
            w1iL[:, hc * 256:hc * 256 + 128] = W1a[:, hc * 128:(hc + 1) * 128]
            w1iL[:, hc * 256 + 128:hc * 256 + 256] = W1g[:, hc * 128:(hc + 1) * 128]
        b1aP = np.zeros(NHC * 128, np.float32)
        b1gP = np.zeros(NHC * 128, np.float32)
        b1aP[:DIM_INNER] = b1[li][:DIM_INNER]
        b1gP[:DIM_INNER] = b1[li][DIM_INNER:]
        b1cL = np.zeros((128, NHC, 2), np.float32)
        for hc in range(NHC):
            b1cL[:, hc, 0] = b1aP[hc * 128:(hc + 1) * 128]
            b1cL[:, hc, 1] = b1gP[hc * 128:(hc + 1) * 128]
        b1cL = np.ascontiguousarray(b1cL.reshape(128, NHC * 2))
        w2P = np.zeros((NHC * 128, D), np.float32)
        w2P[:DIM_INNER] = W2[li]
        b2cL = np.ascontiguousarray(np.asarray(b2[li], np.float32).reshape(NC_D, 128).T)
        glnL = np.ascontiguousarray(np.asarray(ln_g[li], np.float32).reshape(NC_D, 128).T)
        blnL = np.ascontiguousarray(np.asarray(ln_b[li], np.float32).reshape(NC_D, 128).T)
        in_maps = []
        for c in range(NCORES):
            b, h = divmod(c, 2)
            o0 = h * TOK
            ext = np.zeros((D, TOK + 1), np.float32)
            if o0 > 0:
                ext[:, 0] = xT[b][:, o0 - 1]
            ext[:, 1:] = xT[b][:, o0:o0 + TOK]
            in_maps.append({
                "xin": ext, "w1i": w1iL, "b1c": b1cL, "w2": w2P, "b2c": b2cL,
                "gln": glnL, "bln": blnL})
        res = run_bass_kernel_spmd(nc_f, in_maps, core_ids=list(range(NCORES)))
        for c in range(NCORES):
            b, h = divmod(c, 2)
            xT[b][:, h * TOK:(h + 1) * TOK] = res.results[c]["xout"]
    out = np.stack([xT[b].T for b in range(B)])
    return np.ascontiguousarray(out)


def sandwich_pre(xT):
    nc = _get("pre_ln", build_pre_ln)
    in_maps = []
    for c in range(NCORES):
        b, h = divmod(c, 2)
        in_maps.append({"xo": np.ascontiguousarray(
            xT[b][:, h * TOK:(h + 1) * TOK])})
    res = run_bass_kernel_spmd(nc, in_maps, core_ids=list(range(NCORES)))
    return [np.concatenate([res.results[2 * b]["out"],
                            res.results[2 * b + 1]["out"]], axis=1)
            for b in range(B)]


def sandwich_post(yT, xT):
    nc = _get("post_ln", build_post_ln)
    in_maps = []
    for c in range(NCORES):
        b, h = divmod(c, 2)
        sl = slice(h * TOK, (h + 1) * TOK)
        in_maps.append({"yo": np.ascontiguousarray(yT[b][:, sl]),
                        "xo": np.ascontiguousarray(xT[b][:, sl])})
    res = run_bass_kernel_spmd(nc, in_maps, core_ids=list(range(NCORES)))
    return [np.concatenate([res.results[2 * b]["out"],
                            res.results[2 * b + 1]["out"]], axis=1)
            for b in range(B)]


def _cols(v, n):
    return np.ascontiguousarray(np.asarray(v, np.float32).reshape(n, 128).T)


def l1_attn_forward(xT, inp):
    nc_a = _get("l1a_A", build_l1a_A)
    nc_b = _get("l1a_B", build_l1a_B)
    nsc = _cols(inp["L1_attn_norm_scale"], NC_D)
    inwT = np.ascontiguousarray(np.asarray(inp["L1_attn_in_w"], np.float32).T)
    inb = _cols(inp["L1_attn_in_b"], 16)
    dww = np.ascontiguousarray(np.asarray(inp["L1_attn_dw_w"], np.float32)
                               .reshape(NC_D, 128, 3).transpose(1, 0, 2)
                               .reshape(128, NC_D * 3))
    dwb = _cols(inp["L1_attn_dw_b"], NC_D)
    pwwT = np.ascontiguousarray(np.asarray(inp["L1_attn_pw_w"], np.float32).T)
    pwb = _cols(inp["L1_attn_pw_b"], NC_D)
    winf = np.ascontiguousarray(np.asarray(inp["L1_attn_f_win"], np.float32).T)
    winb = np.ascontiguousarray(np.asarray(inp["L1_attn_b_win"], np.float32).T)
    sspar = np.stack([
        np.asarray(inp["L1_attn_f_logA"], np.float32),
        np.asarray(inp["L1_attn_f_B"], np.float32),
        np.full(64, float(inp["L1_attn_f_logdt"]), np.float32),
        np.asarray(inp["L1_attn_b_logA"], np.float32),
        np.asarray(inp["L1_attn_b_B"], np.float32),
        np.full(64, float(inp["L1_attn_b_logdt"]), np.float32)], axis=1)
    sspar = np.ascontiguousarray(sspar)
    in_maps = []
    for c in range(NCORES):
        b, h = divmod(c, 2)
        o0 = h * TOK
        ext = np.zeros((D, TOK + 2), np.float32)
        if o0 > 0:
            ext[:, 0] = xT[b][:, o0 - 1]
        ext[:, 1:TOK + 1] = xT[b][:, o0:o0 + TOK]
        if o0 + TOK < S:
            ext[:, TOK + 1] = xT[b][:, o0 + TOK]
        edge = np.ones((128, 2), np.float32)
        if h == 0:
            edge[:, 0] = 0.0
        if h == 1:
            edge[:, 1] = 0.0
        in_maps.append({"xo": ext, "nsc": nsc, "inw": inwT, "inb": inb,
                        "dww": dww, "dwb": dwb, "pww": pwwT, "pwb": pwb,
                        "winf": winf, "winb": winb, "sspar": sspar,
                        "edge": np.ascontiguousarray(edge)})
    res = run_bass_kernel_spmd(nc_a, in_maps, core_ids=list(range(NCORES)))
    rA = [res.results[c] for c in range(NCORES)]

    cpar = np.ascontiguousarray(np.stack(
        [np.asarray(inp["L1_attn_f_C"], np.float32),
         np.asarray(inp["L1_attn_b_C"], np.float32)], axis=1))
    woutT = np.ascontiguousarray(np.concatenate(
        [np.asarray(inp["L1_attn_f_wout"], np.float32).T,
         np.asarray(inp["L1_attn_b_wout"], np.float32).T], axis=1))
    outwT = np.ascontiguousarray(np.asarray(inp["L1_attn_out_w"], np.float32).T)
    outb = _cols(inp["L1_attn_out_b"], NC_D)
    in_maps = []
    for c in range(NCORES):
        b, h = divmod(c, 2)
        carry = np.zeros((64, 2), np.float32)
        if h == 1:
            carry[:, 0] = rA[2 * b]["hf_out"][:, TOK - 1]
        if h == 0:
            carry[:, 1] = rA[2 * b + 1]["hf_out"][:, TOK]
        in_maps.append({
            "hfin": rA[c]["hf_out"], "carry": np.ascontiguousarray(carry),
            "sspar": sspar, "cpar": cpar, "woutT": woutT, "outwT": outwT,
            "outb": outb, "gin": rA[c]["g_out"], "tin": rA[c]["t_out"],
            "xin": np.ascontiguousarray(xT[b][:, h * TOK:(h + 1) * TOK])})
    res = run_bass_kernel_spmd(nc_b, in_maps, core_ids=list(range(NCORES)))
    return [np.concatenate([res.results[2 * b]["out"],
                            res.results[2 * b + 1]["out"]], axis=1)
            for b in range(B)]


def l1_ffn_forward(xT, inp):
    nc_a = _get("l1f_a", build_l1f_a)
    nc_b = _get("l1f_b", build_l1f_b)
    nc_c = _get("l1f_c", build_l1f_c)
    NCH_ = NCH
    w1T = np.ascontiguousarray(np.asarray(inp["L1_ffn_w1_w"], np.float32).T)
    w2T = np.ascontiguousarray(np.asarray(inp["L1_ffn_w2_w"], np.float32).T)
    h1T = np.ascontiguousarray(np.asarray(inp["L1_ffn_h1_w"], np.float32).T)
    h2T = np.ascontiguousarray(np.asarray(inp["L1_ffn_h2_w"], np.float32).T)
    w3T = np.ascontiguousarray(np.asarray(inp["L1_ffn_w3_w"], np.float32).T)
    s1 = _cols(inp["L1_ffn_w1_s"], NCH_)
    s2 = _cols(inp["L1_ffn_w2_s"], NCH_)
    s3 = _cols(inp["L1_ffn_h1_s"], NCH_)
    s4 = _cols(inp["L1_ffn_h2_s"], NCH_)
    s5 = _cols(inp["L1_ffn_w3_s"], NC_D)
    xs = [np.ascontiguousarray(xT[c // 2][:, (c % 2) * TOK:(c % 2 + 1) * TOK])
          for c in range(NCORES)]
    in_maps = [{"xin": xs[c], "w1T": w1T, "w2T": w2T, "s1c": s1, "s2c": s2}
               for c in range(NCORES)]
    res = run_bass_kernel_spmd(nc_a, in_maps, core_ids=list(range(NCORES)))
    y1 = [res.results[c]["y1o"] for c in range(NCORES)]
    in_maps = [{"y1i": y1[c], "h2T": h1T if False else h1T, "s4c": s3}
               for c in range(NCORES)]
    res = run_bass_kernel_spmd(nc_b, in_maps, core_ids=list(range(NCORES)))
    y2 = [res.results[c]["y2o"] for c in range(NCORES)]
    in_maps = [{"y1i": y2[c], "h2T": h2T, "s4c": s4} for c in range(NCORES)]
    res = run_bass_kernel_spmd(nc_b, in_maps, core_ids=list(range(NCORES)))
    y3 = [res.results[c]["y2o"] for c in range(NCORES)]
    in_maps = [{"y2i": y3[c], "w3T": w3T, "s5c": s5, "xin": xs[c]}
               for c in range(NCORES)]
    res = run_bass_kernel_spmd(nc_c, in_maps, core_ids=list(range(NCORES)))
    return [np.concatenate([res.results[2 * b]["out"],
                            res.results[2 * b + 1]["out"]], axis=1)
            for b in range(B)]


def kernel(**inputs):
    x = np.asarray(inputs["L0_attn_x"], np.float32)
    x = l0_attn_forward(x, inputs["L0_attn_Wq"], inputs["L0_attn_bq"],
                        inputs["L0_attn_Wk"], inputs["L0_attn_bk"],
                        inputs["L0_attn_Wv"], inputs["L0_attn_bv"],
                        inputs["L0_attn_Wo"], inputs["L0_attn_bo"])
    xT = [np.ascontiguousarray(x[b].T) for b in range(B)]
    zT = sandwich_pre(xT)
    z = np.stack([zT[b].T for b in range(B)])
    z = l0_ffn_forward(z, inputs["L0_ffn_Wqkv"], inputs["L0_ffn_q_scale"],
                       inputs["L0_ffn_k_scale"], inputs["L0_ffn_Wo"],
                       inputs["L0_ffn_ln_g"], inputs["L0_ffn_ln_b"],
                       inputs["L0_ffn_W1"], inputs["L0_ffn_b1"],
                       inputs["L0_ffn_W2"], inputs["L0_ffn_b2"])
    zT = [np.ascontiguousarray(z[b].T) for b in range(B)]
    xT = sandwich_post(zT, xT)
    xT = l1_attn_forward(xT, inputs)
    xT = l1_ffn_forward(xT, inputs)
    out = np.stack([xT[b].T for b in range(B)])
    return np.ascontiguousarray(out)


# version 02
# speedup vs baseline: 1.1624x; 1.1624x over previous
import numpy as np
from contextlib import ExitStack

import concourse.bass as bass
import concourse.mybir as mybir
from concourse.bass_utils import run_bass_kernel_spmd
from concourse.alu_op_type import AluOpType

F32 = mybir.dt.float32
BF16 = mybir.dt.bfloat16
AF = mybir.ActivationFunctionType
OP = AluOpType

NCORES = 8
B, S, D = 4, 4096, 1024
TOK = 2048
NC_D = D // 128
DH = 64
EPS_LN = 1e-5

ENGS = ("tensor", "vector", "scalar", "gpsimd", "sync")
RACE_DETECT = False


class Ctx:

    def __init__(self, E, eng, sems):
        self.E = E
        self.eng = eng
        self.sems = sems
        self.cnt = {k: 0 for k in sems}

    def on(self, E):
        return self.E == E

    def inc(self, name, n=1):
        self.cnt[name] += n

    def tag(self, inst, name, n=1):
        if inst is not None:
            inst.then_inc(self.sems[name], n)
        self.cnt[name] += n

    def wait(self, E, name, val=None):
        if self.E == E:
            self.eng.wait_ge(self.sems[name], self.cnt[name] if val is None else val)

    def dma(self, E, sem, fn, n=16):
        if self.E == E:
            fn(self.eng).then_inc(self.sems[sem], n)
        self.cnt[sem] += n

    def op(self, E, sem, fn, n=1):
        if self.E == E:
            fn(self.eng).then_inc(self.sems[sem], n)
        self.cnt[sem] += n

    def run(self, E, fn):
        if self.E == E:
            fn(self.eng)

    def tag_fenced(self, E, inst, name, n=1):
        self.fence(E, inst)
        if self.E == E:
            i = self.eng.engine_nop()
            i.then_inc(self.sems[name], n)
        self.cnt[name] += n

    def fence(self, E, inst):
        name = "f" + E[0]
        if inst is not None:
            inst.then_inc(self.sems[name], 1)
        self.cnt[name] += 1
        if self.E == E:
            self.eng.wait_ge(self.sems[name], self.cnt[name])


def run_program(nc, names, prog):
    with ExitStack() as es:
        sems = {n: es.enter_context(nc.semaphore(n)) for n in names}
        block = es.enter_context(nc.Block())
        for E in ENGS:
            def mk(E):
                def _(eng):
                    prog(Ctx(E, eng, sems))
                return _
            getattr(block, E)(mk(E))



def alloc_ln_scratch(es, nc, NC, T, tag=""):
    t = {}
    t["x2t"] = es.enter_context(nc.sbuf_tensor(f"ln_x2t{tag}", [128, NC, 512], F32))
    t["srow"] = es.enter_context(nc.sbuf_tensor(f"ln_srow{tag}", [1, T], F32))
    t["qrow"] = es.enter_context(nc.sbuf_tensor(f"ln_qrow{tag}", [1, T], F32))
    t["rrow"] = es.enter_context(nc.sbuf_tensor(f"ln_rrow{tag}", [1, T], F32))
    t["mrow"] = es.enter_context(nc.sbuf_tensor(f"ln_mrow{tag}", [1, T], F32))
    t["ps_s"] = es.enter_context(nc.psum_tensor(f"ln_ps_s{tag}", [1, 512], F32))
    t["ps_q"] = es.enter_context(nc.psum_tensor(f"ln_ps_q{tag}", [1, 512], F32))
    t["ps_R"] = es.enter_context(nc.psum_tensor(f"ln_ps_R{tag}", [128, 512], F32))
    t["ps_M"] = es.enter_context(nc.psum_tensor(f"ln_ps_M{tag}", [128, 512], F32))
    t["onesf"] = es.enter_context(nc.sbuf_tensor(f"ln_onesf{tag}", [128, 128], F32))
    t["onesb"] = es.enter_context(nc.sbuf_tensor(f"ln_onesb{tag}", [1, 128], BF16))
    t["epsr"] = es.enter_context(nc.sbuf_tensor(f"ln_epsr{tag}", [1, 1], F32))
    return t


def emit_ln_const_init(C, t, eps=EPS_LN):
    if C.on("vector"):
        C.eng.memset(t["onesf"][:, :], 1.0)
        C.eng.memset(t["onesb"][:, :], 1.0)
        i = C.eng.memset(t["epsr"][:, :], eps)
        C.tag(i, "v")
    else:
        C.inc("v")


def emit_ln_fm(C, t, X, T, apply, rms=False, eps=EPS_LN, shift=None):
    NC = X.shape[1]
    NT = T // 512
    C.wait("tensor", "v")
    C.wait("scalar", "v")
    for ti in range(NT):
        sl = slice(ti * 512, (ti + 1) * 512)
        C.wait("scalar", "ld")
        if C.on("scalar"):
            i = C.eng.activation(out=t["x2t"][:, :, :], in_=X[:, :, sl],
                                 func=AF.Square)
            C.tag(i, "s")
        else:
            C.inc("s")
        C.wait("tensor", "s")
        C.wait("tensor", "ld")
        if C.on("tensor"):
            if not rms:
                for c in range(NC):
                    C.eng.matmul(t["ps_s"][:, :], t["onesf"][:, 0:1], X[:, c, sl],
                                 start=(c == 0), stop=(c == NC - 1))
            for c in range(NC):
                i = C.eng.matmul(t["ps_q"][:, :], t["onesf"][:, 0:1],
                                 t["x2t"][:, c, :], start=(c == 0),
                                 stop=(c == NC - 1))
            C.tag(i, "m")
        else:
            C.inc("m")
        C.wait("vector", "m")
        if C.on("vector"):
            if not rms:
                C.eng.tensor_copy(t["srow"][:, sl], t["ps_s"][:, :])
            i = C.eng.tensor_copy(t["qrow"][:, sl], t["ps_q"][:, :])
        else:
            i = None
        C.tag_fenced("vector", i, "v")
        C.wait("tensor", "v")
        C.wait("scalar", "v")
    Dn = float(X.shape[0] * NC)
    if not rms:
        if C.on("vector"):
            C.eng.tensor_scalar(t["srow"][:, 0:T], t["srow"][:, 0:T], 1.0 / Dn,
                                None, OP.mult)
            C.eng.tensor_tensor(t["mrow"][:, 0:T], t["srow"][:, 0:T],
                                t["srow"][:, 0:T], OP.mult)
            i = C.eng.scalar_tensor_tensor(t["qrow"][:, 0:T], t["qrow"][:, 0:T],
                                           1.0 / Dn, t["mrow"][:, 0:T],
                                           OP.mult, OP.subtract)
            C.tag(i, "v")
        else:
            C.inc("v")
        C.wait("scalar", "v")
        if C.on("scalar"):
            C.eng.activation(out=t["rrow"][:, 0:T], in_=t["qrow"][:, 0:T],
                             func=AF.Ln, bias=t["epsr"][0:1, 0:1])
            i = C.eng.activation(out=t["rrow"][:, 0:T], in_=t["rrow"][:, 0:T],
                                 func=AF.Exp, scale=-0.5)
            C.tag(i, "s")
        else:
            C.inc("s")
        C.wait("vector", "s")
        if C.on("vector"):
            i = C.eng.tensor_tensor(t["mrow"][:, 0:T], t["srow"][:, 0:T],
                                    t["rrow"][:, 0:T], OP.mult)
            C.tag(i, "v")
        else:
            C.inc("v")
    else:
        if C.on("scalar"):
            C.eng.activation(out=t["rrow"][:, 0:T], in_=t["qrow"][:, 0:T],
                             func=AF.Ln, scale=1.0 / Dn, bias=t["epsr"][0:1, 0:1])
            i = C.eng.activation(out=t["rrow"][:, 0:T], in_=t["rrow"][:, 0:T],
                                 func=AF.Exp, scale=-0.5)
            C.tag(i, "s")
        else:
            C.inc("s")
    C.wait("tensor", "v")
    C.wait("tensor", "s")
    for ti in range(NT):
        sl = slice(ti * 512, (ti + 1) * 512)
        if C.on("tensor"):
            i = C.eng.matmul(t["ps_R"][:, :], t["onesf"][0:1, 0:128],
                             t["rrow"][:, sl], start=True, stop=True)
            if not rms:
                i = C.eng.matmul(t["ps_M"][:, :], t["onesf"][0:1, 0:128],
                                 t["mrow"][:, sl], start=True, stop=True)
            C.tag(i, "m")
        else:
            C.inc("m")
        C.wait("vector", "m")
        apply(C, ti)
        C.wait("tensor", "v")


def bcN(ps, NC):
    return ps[:, :].rearrange("p (o t) -> p o t", o=1).broadcast_to((128, NC, 512))



def build_l0a_proj():
    nc = bass.Bass("TRN2", target_bir_lowering=False, debug=False,
                   detect_race_conditions=RACE_DETECT)
    xo = nc.dram_tensor("xo", [D, TOK], F32, kind="ExternalInput").ap()
    wc = nc.dram_tensor("wc", [D + 1, 3 * D], F32, kind="ExternalInput").ap()
    qkv = nc.dram_tensor("qkv", [TOK, 3 * D], BF16, kind="ExternalOutput").ap()
    es = ExitStack()
    with es:
        X = es.enter_context(nc.sbuf_tensor("X", [128, NC_D, TOK], F32))
        zb = es.enter_context(nc.sbuf_tensor("zb", [128, NC_D, TOK], BF16))
        W = es.enter_context(nc.sbuf_tensor("W", [128, NC_D, 3 * D // 2], BF16))
        wb = es.enter_context(nc.sbuf_tensor("wb", [1, 3 * D], BF16))
        ob = es.enter_context(nc.sbuf_tensor("ob", [128, 2, 512], BF16))
        ps = [es.enter_context(nc.psum_tensor(f"ps{i}", [128, 512], F32))
              for i in range(2)]
        lnt = alloc_ln_scratch(es, nc, NC_D, TOK)

        def prog(C):
            if C.on("sync"):
                i = C.eng.dma_start(out=X[:, :, :],
                                    in_=xo.rearrange("(c p) t -> p c t", p=128))
                C.tag(i, "ld", 16)
            else:
                C.inc("ld", 16)
            if C.on("gpsimd"):
                i = C.eng.dma_start(
                    out=W[:, :, :],
                    in_=wc[0:D, 0:3 * D // 2].rearrange("(c p) o -> p c o", p=128))
                C.tag(i, "lw", 16)
                i = C.eng.dma_start(out=wb[:, :], in_=wc[D:D + 1, :])
                C.tag(i, "lw", 16)
            else:
                C.inc("lw", 32)
            emit_ln_const_init(C, lnt)

            def apply(C, ti):
                sl = slice(ti * 512, (ti + 1) * 512)
                if C.on("vector"):
                    Rb = bcN(lnt["ps_R"], NC_D)
                    Mb = bcN(lnt["ps_M"], NC_D)
                    C.eng.tensor_tensor(lnt["x2t"][:, :, :], X[:, :, sl], Rb, OP.mult)
                    i = C.eng.tensor_tensor(zb[:, :, sl], lnt["x2t"][:, :, :], Mb,
                                            OP.subtract)
                    C.tag(i, "v")
                else:
                    C.inc("v")

            emit_ln_fm(C, lnt, X, TOK, apply)

            it = 0
            for og in range(2):
                if og == 1:
                    C.wait("gpsimd", "m")
                    if C.on("gpsimd"):
                        i = C.eng.dma_start(
                            out=W[:, :, :],
                            in_=wc[0:D, 3 * D // 2:3 * D].rearrange(
                                "(c p) o -> p c o", p=128))
                        C.tag(i, "lw", 16)
                    else:
                        C.inc("lw", 16)
                C.wait("tensor", "lw")
                C.wait("tensor", "v")
                for tc in range(TOK // 128):
                    for oh in range(3):
                        o = og * 3 + oh
                        bank = it % 2
                        osl = slice(o * 512, (o + 1) * 512)
                        wsl = slice(oh * 512, (oh + 1) * 512)
                        if it >= 2:
                            C.wait("tensor", f"st{bank}")
                        if C.on("tensor"):
                            for c in range(NC_D):
                                C.eng.matmul(ps[bank][:, :],
                                             zb[:, c, tc * 128:(tc + 1) * 128],
                                             W[:, c, wsl], start=(c == 0),
                                             stop=False)
                            i = C.eng.matmul(ps[bank][:, :], lnt["onesb"][:, 0:128],
                                             wb[:, osl], start=False, stop=True)
                            C.tag(i, "m")
                        else:
                            C.inc("m")
                        C.wait("scalar", "m")
                        if C.on("scalar"):
                            fn = AF.Exp if o < 4 else AF.Identity
                            i = C.eng.activation(out=ob[:, bank, :],
                                                 in_=ps[bank][:, :], func=fn)
                            C.tag(i, "s")
                        else:
                            C.inc("s")
                        C.wait("sync", "s")
                        if C.on("sync"):
                            i = C.eng.dma_start(out=qkv[tc * 128:(tc + 1) * 128, osl],
                                                in_=ob[:, bank, :])
                            C.tag(i, f"st{bank}", 16)
                        else:
                            C.inc(f"st{bank}", 16)
                        it += 1
            C.wait("sync", "st0")
            C.wait("sync", "st1")

        run_program(nc, ["ld", "lw", "v", "s", "m", "fv", "st0", "st1"], prog)
    return nc



def build_l0a_attn():
    nc = bass.Bass("TRN2", target_bir_lowering=False, debug=False,
                   detect_race_conditions=RACE_DETECT)
    NH = 8
    CH = S // 128
    qf = nc.dram_tensor("qf", [NH, DH, S], BF16, kind="ExternalInput").ap()
    kf = nc.dram_tensor("kf", [NH, DH, S], BF16, kind="ExternalInput").ap()
    kt = nc.dram_tensor("kt", [NH, S, DH], BF16, kind="ExternalInput").ap()
    vt = nc.dram_tensor("vt", [NH, S, DH + 1], BF16, kind="ExternalInput").ap()
    tri = nc.dram_tensor("tri", [128, 128], BF16, kind="ExternalInput").ap()
    at = nc.dram_tensor("at", [NH, S, DH], BF16, kind="ExternalOutput").ap()
    es = ExitStack()
    with es:
        Q = es.enter_context(nc.sbuf_tensor("Q", [64, 2, S], BF16))
        K = es.enter_context(nc.sbuf_tensor("K", [64, 2, S], BF16))
        KT = es.enter_context(nc.sbuf_tensor("KT", [128, 2, CH, DH], BF16))
        VT = es.enter_context(nc.sbuf_tensor("VT", [128, 2, CH, DH + 1], BF16))
        AT = es.enter_context(nc.sbuf_tensor("AT", [128, 2, CH, DH], BF16))
        TRI = es.enter_context(nc.sbuf_tensor("TRI", [128, 128], BF16))
        M = es.enter_context(nc.sbuf_tensor("M", [128, 128], BF16))
        KV = es.enter_context(nc.sbuf_tensor("KV", [64, 2, DH + 1], BF16))
        KVA = es.enter_context(nc.sbuf_tensor("KVA", [64, DH + 1], F32))
        dn = es.enter_context(nc.sbuf_tensor("dn", [128, 1], F32))
        psA = es.enter_context(nc.psum_tensor("psA", [128, 128], F32))
        psN = es.enter_context(nc.psum_tensor("psN", [128, DH + 1], F32))
        psK = es.enter_context(nc.psum_tensor("psK", [64, DH + 1], F32))

        def prog(C):
            if C.on("sync"):
                i = C.eng.dma_start(out=TRI[:, :], in_=tri[:, :])
                C.tag(i, "lt", 16)
            else:
                C.inc("lt", 16)

            def load_head(h, buf):
                sem = f"ld{buf}"
                for (dst, src) in ((Q[:, buf, :], qf[h, :, :]),
                                   (K[:, buf, :], kf[h, :, :]),
                                   (KT[:, buf, :, :],
                                    kt[h, :, :].rearrange("(c p) d -> p c d", p=128)),
                                   (VT[:, buf, :, :],
                                    vt[h, :, :].rearrange("(c p) d -> p c d", p=128))):
                    if C.on("sync"):
                        i = C.eng.dma_start(out=dst, in_=src)
                        C.tag(i, sem, 16)
                    else:
                        C.inc(sem, 16)

            load_head(0, 0)
            C.wait("tensor", "lt")
            for h in range(NH):
                buf = h % 2
                if h + 1 < NH:
                    load_head(h + 1, 1 - buf)
                C.wait("tensor", f"ld{buf}", 64 * (h // 2 + 1))
                if h >= 2:
                    C.wait("tensor", f"st{buf}")
                for ci in range(CH):
                    csl = slice(ci * 128, (ci + 1) * 128)
                    if C.on("tensor"):
                        C.eng.matmul(psA[:, :], K[:, buf, csl], Q[:, buf, csl],
                                     start=True, stop=True)
                        i = C.eng.matmul(psK[:, :], KT[:, buf, ci, :],
                                         VT[:, buf, ci, :], start=True, stop=True)
                        C.tag(i, "m")
                    else:
                        C.inc("m")
                    C.wait("vector", "m")
                    if C.on("vector"):
                        C.eng.tensor_tensor(M[:, :], psA[:, :], TRI[:, :], OP.mult)
                        if ci == 0:
                            i = C.eng.tensor_copy(KVA[:, :], psK[:, :])
                        else:
                            i = C.eng.tensor_tensor(KVA[:, :], KVA[:, :], psK[:, :],
                                                    OP.add)
                    else:
                        i = None
                    C.fence("vector", i)
                    if C.on("vector"):
                        i = C.eng.tensor_copy(KV[:, ci % 2, :], KVA[:, :])
                        C.tag(i, "v")
                    else:
                        C.inc("v")
                    C.wait("tensor", "v")
                    if C.on("tensor"):
                        i = C.eng.matmul(psN[:, :], M[:, :], VT[:, buf, ci, :],
                                         start=True, stop=(ci == 0))
                        if ci > 0:
                            i = C.eng.matmul(psN[:, :], Q[:, buf, csl],
                                             KV[:, (ci - 1) % 2, :], start=False,
                                             stop=True)
                        C.tag(i, "m")
                    else:
                        C.inc("m")
                    C.wait("vector", "m")
                    if C.on("vector"):
                        i = C.eng.reciprocal(dn[:, :], psN[:, DH:DH + 1])
                    else:
                        i = None
                    C.fence("vector", i)
                    if C.on("vector"):
                        i = C.eng.tensor_scalar(AT[:, buf, ci, :], psN[:, 0:DH],
                                                dn[:, 0:1], None, OP.mult)
                        C.tag(i, "v")
                    else:
                        C.inc("v")
                    C.wait("tensor", "v")
                C.wait("sync", "v")
                if C.on("sync"):
                    i = C.eng.dma_start(
                        out=at[h, :, :].rearrange("(c p) d -> p c d", p=128),
                        in_=AT[:, buf, :, :])
                    C.tag(i, f"st{buf}", 16)
                else:
                    C.inc(f"st{buf}", 16)
            C.wait("sync", "st0")
            C.wait("sync", "st1")

        run_program(nc, ["lt", "ld0", "ld1", "v", "m", "fv", "st0", "st1"], prog)
    return nc



def build_l0a_out():
    nc = bass.Bass("TRN2", target_bir_lowering=False, debug=False,
                   detect_race_conditions=RACE_DETECT)
    TG = 1024
    af = nc.dram_tensor("af", [D, TOK], BF16, kind="ExternalInput").ap()
    wo = nc.dram_tensor("wo", [D, D], F32, kind="ExternalInput").ap()
    bo = nc.dram_tensor("bo", [128, NC_D], F32, kind="ExternalInput").ap()
    xo = nc.dram_tensor("xo", [D, TOK], F32, kind="ExternalInput").ap()
    out = nc.dram_tensor("out", [D, TOK], F32, kind="ExternalOutput").ap()
    es = ExitStack()
    with es:
        A = es.enter_context(nc.sbuf_tensor("A", [128, NC_D, TG], BF16))
        W = es.enter_context(nc.sbuf_tensor("W", [128, NC_D, D], BF16))
        BO = es.enter_context(nc.sbuf_tensor("BO", [128, NC_D], F32))
        X = es.enter_context(nc.sbuf_tensor("X", [128, NC_D, TG], F32))
        Y = es.enter_context(nc.sbuf_tensor("Y", [128, NC_D, TG], F32))
        OB = es.enter_context(nc.sbuf_tensor("OB", [128, NC_D, 512], F32))
        ps = [es.enter_context(nc.psum_tensor(f"ps{i}", [128, 512], F32))
              for i in range(2)]
        lnt = alloc_ln_scratch(es, nc, NC_D, TG)

        def prog(C):
            if C.on("gpsimd"):
                C.eng.dma_start(out=W[:, :, :],
                                in_=wo.rearrange("(c p) o -> p c o", p=128)
                                ).then_inc(C.sems["lw"], 16)
                i = C.eng.dma_start(out=BO[:, :], in_=bo)
                C.tag(i, "lw", 16)
                C.inc("lw", 16)
            else:
                C.inc("lw", 32)
            emit_ln_const_init(C, lnt)
            C.wait("tensor", "lw")
            C.wait("scalar", "lw")

            evac_s = []
            for g in range(TOK // TG):
                gsl = slice(g * TG, (g + 1) * TG)
                if C.on("sync"):
                    C.eng.dma_start(
                        out=A[:, :, :],
                        in_=af.rearrange("(c p) t -> p c t", p=128)[:, :, gsl]
                        ).then_inc(C.sems["ld"], 16)
                    i = C.eng.dma_start(
                        out=X[:, :, :],
                        in_=xo.rearrange("(c p) t -> p c t", p=128)[:, :, gsl])
                    C.tag(i, "ld", 16)
                    C.inc("ld", 16)
                else:
                    C.inc("ld", 32)
                C.wait("tensor", "ld")
                it = 0
                for tc in range(TG // 512):
                    tsl = slice(tc * 512, (tc + 1) * 512)
                    for oc in range(NC_D):
                        bank = it % 2
                        if len(evac_s) >= 2:
                            C.wait("tensor", "s", evac_s[-2])
                        if C.on("tensor"):
                            for c in range(NC_D):
                                i = C.eng.matmul(ps[bank][:, :],
                                                 W[:, c, oc * 128:(oc + 1) * 128],
                                                 A[:, c, tsl], start=(c == 0),
                                                 stop=(c == NC_D - 1))
                            C.tag(i, "m")
                        else:
                            C.inc("m")
                        C.wait("scalar", "m")
                        if C.on("scalar"):
                            i = C.eng.activation(out=Y[:, oc, tsl],
                                                 in_=ps[bank][:, :],
                                                 func=AF.Identity,
                                                 bias=BO[:, oc:oc + 1])
                            C.tag(i, "s")
                        else:
                            C.inc("s")
                        evac_s.append(C.cnt["s"])
                        it += 1
                C.wait("vector", "s")
                C.wait("scalar", "s")
                C.wait("tensor", "s")

                def apply(C, ti):
                    sl = slice(ti * 512, (ti + 1) * 512)
                    if C.on("vector"):
                        Rb = bcN(lnt["ps_R"], NC_D)
                        Mb = bcN(lnt["ps_M"], NC_D)
                        C.eng.tensor_tensor(OB[:, :, :], Y[:, :, sl], Rb, OP.mult)
                        C.eng.tensor_tensor(OB[:, :, :], OB[:, :, :], Mb,
                                            OP.subtract)
                        i = C.eng.tensor_tensor(OB[:, :, :], OB[:, :, :],
                                                X[:, :, sl], OP.add)
                        C.tag(i, "v")
                    else:
                        C.inc("v")
                    C.wait("sync", "v")
                    if C.on("sync"):
                        i = C.eng.dma_start(
                            out=out.rearrange("(c p) t -> p c t",
                                              p=128)[:, :, gsl][:, :, sl],
                            in_=OB[:, :, :])
                        C.tag(i, "st", 16)
                    else:
                        C.inc("st", 16)
                    C.wait("vector", "st")

                emit_ln_fm(C, lnt, Y, TG, apply)
                C.wait("sync", "v")
                C.wait("sync", "s")
                C.wait("sync", "m")
            C.wait("sync", "st")

        run_program(nc, ["ld", "lw", "v", "s", "m", "fv", "st"], prog)
    return nc



EXT = 2432
L0F_SEGS = [(3, 9), (9, 15), (15, 19)]


def _tiles(n, w=512):
    out = []
    t = 0
    while t < n:
        out.append((t, min(w, n - t)))
        t += w
    return out


def build_l0f_attn():
    nc = bass.Bass("TRN2", target_bir_lowering=False, debug=False,
                   detect_race_conditions=RACE_DETECT)
    xin = nc.dram_tensor("xin", [D, EXT], F32, kind="ExternalInput").ap()
    wqkv = nc.dram_tensor("wqkv", [D, 3 * D], F32, kind="ExternalInput").ap()
    wo = nc.dram_tensor("wo", [D, D], F32, kind="ExternalInput").ap()
    tq1 = nc.dram_tensor("tq1", [128, EXT], F32, kind="ExternalInput").ap()
    tq2 = nc.dram_tensor("tq2", [128, EXT], F32, kind="ExternalInput").ap()
    tk1 = nc.dram_tensor("tk1", [128, EXT], F32, kind="ExternalInput").ap()
    tk2 = nc.dram_tensor("tk2", [128, EXT], F32, kind="ExternalInput").ap()
    rotm = nc.dram_tensor("rotm", [128, 128], F32, kind="ExternalInput").ap()
    blk = nc.dram_tensor("blk", [128, 128], F32, kind="ExternalInput").ap()
    idm = nc.dram_tensor("idm", [128, 128], F32, kind="ExternalInput").ap()
    mneg = nc.dram_tensor("mneg", [3, 128, 384], F32, kind="ExternalInput").ap()
    xout = nc.dram_tensor("xout", [D, TOK], F32, kind="ExternalOutput").ap()

    es = ExitStack()
    with es:
        X = es.enter_context(nc.sbuf_tensor("X", [128, NC_D, EXT], F32))
        XBS = es.enter_context(nc.sbuf_tensor("XBS", [128, NC_D, 1025], BF16))
        W = es.enter_context(nc.sbuf_tensor("W", [128, NC_D, 2 * D], BF16))
        WOC = es.enter_context(nc.sbuf_tensor("WOC", [128, 2, NC_D, 128], BF16))
        TQ1 = es.enter_context(nc.sbuf_tensor("TQ1", [128, 1024], BF16))
        TQ2 = es.enter_context(nc.sbuf_tensor("TQ2", [128, 1024], BF16))
        TK1 = es.enter_context(nc.sbuf_tensor("TK1", [128, 1024], BF16))
        TK2 = es.enter_context(nc.sbuf_tensor("TK2", [128, 1024], BF16))
        ROTM = es.enter_context(nc.sbuf_tensor("ROTM", [128, 128], BF16))
        BLKM = es.enter_context(nc.sbuf_tensor("BLKM", [128, 128], BF16))
        IDM = es.enter_context(nc.sbuf_tensor("IDM", [128, 128], BF16))
        MNEG = es.enter_context(nc.sbuf_tensor("MNEG", [128, 3, 384], BF16))
        KH = es.enter_context(nc.sbuf_tensor("KH", [128, NC_D, 1024], BF16))
        QH = es.enter_context(nc.sbuf_tensor("QH", [128, NC_D, 768], BF16))
        VT = es.enter_context(nc.sbuf_tensor("VT", [128, 8, 16, DH + 1], BF16))
        ATM = es.enter_context(nc.sbuf_tensor("ATM", [128, 16, DH], BF16))
        AFM = es.enter_context(nc.sbuf_tensor("AFM", [128, NC_D, 512], BF16))
        CAR = es.enter_context(nc.sbuf_tensor("CAR", [128, NC_D, 256], F32))
        K2T = es.enter_context(nc.sbuf_tensor("K2T", [128, 512], BF16))
        RSB = es.enter_context(nc.sbuf_tensor("RSB", [128, 512], F32))
        ET = K2T
        PB = es.enter_context(nc.sbuf_tensor("PB", [128, 2, 384], BF16))
        PJI = es.enter_context(nc.sbuf_tensor("PJI", [128, 2, 3, 128], BF16))
        NMX = es.enter_context(nc.sbuf_tensor("NMX", [128, 2], F32))
        RDN = es.enter_context(nc.sbuf_tensor("RDN", [128, 2], F32))
        EPSR = es.enter_context(nc.sbuf_tensor("EPSR", [128, 1], F32))
        psP = es.enter_context(nc.psum_tensor("psP", [128, 512], F32))
        psB = es.enter_context(nc.psum_tensor("psB", [128, 512], F32))
        psR = es.enter_context(nc.psum_tensor("psR", [128, 512], F32))
        psS = es.enter_context(nc.psum_tensor("psS", [128, 384], F32))
        psT = es.enter_context(nc.psum_tensor("psT", [128, 384], BF16))
        psT2 = es.enter_context(nc.psum_tensor("psT2", [128, 384], BF16))
        psN = es.enter_context(nc.psum_tensor("psN", [128, DH + 1], F32))
        psO = es.enter_context(nc.psum_tensor("psO", [128, 512], F32))

        def prog(C):
            if C.on("sync"):
                i = C.eng.dma_start(out=X[:, :, :],
                                    in_=xin.rearrange("(c p) t -> p c t", p=128))
                C.tag(i, "ld", 16)
            else:
                C.inc("ld", 16)
            if C.on("gpsimd"):
                for dst, srcap in ((ROTM, rotm), (BLKM, blk), (IDM, idm)):
                    i = C.eng.dma_start(out=dst[:, :], in_=srcap)
                    C.tag(i, "lw", 16)
                i = C.eng.dma_start(out=MNEG[:, :, :],
                                    in_=mneg.rearrange("m p t -> p m t"))
                C.tag(i, "lw", 16)
                i = C.eng.dma_start(
                    out=W[:, :, D:2 * D],
                    in_=wqkv[:, 2 * D:3 * D].rearrange("(c p) o -> p c o", p=128))
                C.tag(i, "lw", 16)
            else:
                C.inc("lw", 80)
            if C.on("vector"):
                C.eng.memset(EPSR[:, :], 1e-24)
                i = C.eng.memset(VT[:, :, :, DH:DH + 1], 1.0)
                C.tag(i, "v")
            else:
                C.inc("v")

            def pcn_pipeline(calls):
                pp = [psP, psO]
                sP, sE1, sB, sL2, sV1, sRT, sV2 = {}, {}, {}, {}, {}, {}, {}

                def st_P(i):
                    dst, oc, dst0, src0, tw, wofs, t1, t2 = calls[i]
                    p = i % 2
                    if i >= 2:
                        C.wait("tensor", "s", sE1[i - 2])
                    if C.on("tensor"):
                        for c in range(NC_D):
                            b = 1 if c < 4 else 0
                            j = C.eng.matmul(
                                pp[p][:, 0:tw],
                                W[:, c, wofs + oc * 128:wofs + oc * 128 + 128],
                                XBS[:, c, b + src0:b + src0 + tw],
                                start=(c == 0), stop=(c == NC_D - 1))
                        C.tag(j, "m")
                    else:
                        C.inc("m")
                    sP[i] = C.cnt["m"]

                def st_E1(i):
                    dst, oc, dst0, src0, tw, wofs, t1, t2 = calls[i]
                    p = i % 2
                    dsl = slice(dst0, dst0 + tw)
                    C.wait("scalar", "m", sP[i])
                    if i >= 1:
                        C.wait("scalar", "v", sV2[i - 1])
                    if C.on("scalar"):
                        C.eng.activation(out=dst[:, oc, dsl], in_=pp[p][:, 0:tw],
                                         func=AF.Identity)
                        j = C.eng.activation(out=K2T[:, 0:tw], in_=pp[p][:, 0:tw],
                                             func=AF.Square)
                        C.tag(j, "s")
                    else:
                        C.inc("s")
                    sE1[i] = C.cnt["s"]

                def st_B(i):
                    tw = calls[i][4]
                    C.wait("tensor", "s", sE1[i])
                    if C.on("tensor"):
                        j = C.eng.matmul(psB[:, 0:tw], BLKM[:, :], K2T[:, 0:tw],
                                         start=True, stop=True)
                        C.tag(j, "m")
                    else:
                        C.inc("m")
                    sB[i] = C.cnt["m"]

                def st_L2(i):
                    tw = calls[i][4]
                    C.wait("scalar", "m", sB[i])
                    if C.on("scalar"):
                        C.eng.activation(out=RSB[:, 0:tw], in_=psB[:, 0:tw],
                                         func=AF.Ln, bias=EPSR[:, 0:1])
                        j = C.eng.activation(out=RSB[:, 0:tw], in_=RSB[:, 0:tw],
                                             func=AF.Exp, scale=-0.5)
                        C.tag(j, "s")
                    else:
                        C.inc("s")
                    sL2[i] = C.cnt["s"]

                def st_V1(i):
                    dst, oc, dst0, src0, tw, wofs, t1, t2 = calls[i]
                    dsl = slice(dst0, dst0 + tw)
                    C.wait("vector", "s", sL2[i])
                    if C.on("vector"):
                        j = C.eng.tensor_tensor(dst[:, oc, dsl], dst[:, oc, dsl],
                                                RSB[:, 0:tw], OP.mult)
                    else:
                        j = None
                    C.fence("vector", j)
                    if C.on("vector"):
                        j = C.eng.engine_nop()
                        j.then_inc(C.sems["v"], 1)
                    C.inc("v")
                    sV1[i] = C.cnt["v"]

                def st_RT(i):
                    dst, oc, dst0, src0, tw, wofs, t1, t2 = calls[i]
                    dsl = slice(dst0, dst0 + tw)
                    C.wait("tensor", "v", sV1[i])
                    if C.on("tensor"):
                        j = C.eng.matmul(psR[:, 0:tw], ROTM[:, :],
                                         dst[:, oc, dsl], start=True, stop=True)
                        C.tag(j, "m")
                    else:
                        C.inc("m")
                    sRT[i] = C.cnt["m"]

                def st_V2(i):
                    dst, oc, dst0, src0, tw, wofs, t1, t2 = calls[i]
                    dsl = slice(dst0, dst0 + tw)
                    tsl = slice(src0, src0 + tw)
                    C.wait("vector", "m", sRT[i])
                    if C.on("vector"):
                        C.eng.tensor_tensor(dst[:, oc, dsl], dst[:, oc, dsl],
                                            t1[:, tsl], OP.mult)
                        C.eng.tensor_tensor(ET[:, 0:tw], psR[:, 0:tw],
                                            t2[:, tsl], OP.mult)
                        j = C.eng.tensor_tensor(dst[:, oc, dsl], dst[:, oc, dsl],
                                                ET[:, 0:tw], OP.add)
                    else:
                        j = None
                    C.tag_fenced("vector", j, "v")
                    sV2[i] = C.cnt["v"]

                n = len(calls)
                st_P(0)
                if n > 1:
                    st_P(1)
                for i in range(n):
                    st_E1(i)
                    if i + 2 < n:
                        st_P(i + 2)
                    st_B(i)
                    st_L2(i)
                    st_V1(i)
                    st_RT(i)
                    st_V2(i)

            carry = []
            for si, (q0, q1) in enumerate(L0F_SEGS):
                kr0, kr1 = q0 - 2, q1
                e0 = kr0 * 128
                kw = (kr1 - kr0) * 128
                C.wait("vector", "ld")
                if C.on("vector"):
                    i = C.eng.tensor_copy(XBS[:, :, 0:kw + 1],
                                          X[:, :, e0 - 1:e0 + kw])
                    C.tag(i, "v")
                else:
                    C.inc("v")
                if carry:
                    cc0 = carry.pop()
                    if C.on("vector"):
                        i = C.eng.tensor_tensor(
                            X[:, :, cc0 * 128:(cc0 + 2) * 128],
                            X[:, :, cc0 * 128:(cc0 + 2) * 128],
                            CAR[:, :, :], OP.add)
                        C.tag(i, "v")
                    else:
                        C.inc("v")
                C.wait("gpsimd", "m")
                if C.on("gpsimd"):
                    for dst, srcap in ((TQ1, tq1), (TQ2, tq2), (TK1, tk1),
                                       (TK2, tk2)):
                        i = C.eng.dma_start(out=dst[:, 0:kw],
                                            in_=srcap[:, e0:e0 + kw])
                        C.tag(i, "lw", 16)
                    i = C.eng.dma_start(
                        out=W[:, :, 0:D],
                        in_=wqkv[:, D:2 * D].rearrange("(c p) o -> p c o", p=128))
                    C.tag(i, "lw", 16)
                else:
                    C.inc("lw", 80)
                C.wait("tensor", "v")
                C.wait("tensor", "lw")
                C.wait("scalar", "lw")
                kcalls = [(KH, oc, tt0, tt0, tw, 0, TK1, TK2)
                          for oc in range(NC_D) for (tt0, tw) in _tiles(kw)]
                pcn_pipeline(kcalls)
                for kc in range(kr0, kr1):
                    src0 = (kc - kr0) * 128
                    for half in range(2):
                        if C.on("tensor"):
                            for c in range(NC_D):
                                b = 1 if c < 4 else 0
                                i = C.eng.matmul(
                                    psP[:, :],
                                    XBS[:, c, b + src0:b + src0 + 128],
                                    W[:, c, D + half * 512:D + half * 512 + 512],
                                    start=(c == 0), stop=(c == NC_D - 1))
                            C.tag(i, "m")
                        else:
                            C.inc("m")
                        C.wait("vector", "m")
                        if C.on("vector"):
                            i = C.eng.tensor_copy(
                                VT[:, kc - kr0, half * 8:half * 8 + 8, 0:DH],
                                psP[:, :].rearrange("p (h d) -> p h d", h=8))
                            C.tag(i, "v")
                        else:
                            C.inc("v")
                        C.wait("tensor", "v")
                C.wait("gpsimd", "m")
                if C.on("gpsimd"):
                    i = C.eng.dma_start(
                        out=W[:, :, 0:D],
                        in_=wqkv[:, 0:D].rearrange("(c p) o -> p c o", p=128))
                    C.tag(i, "lw", 16)
                else:
                    C.inc("lw", 16)
                C.wait("tensor", "lw")
                C.wait("scalar", "lw")
                qw = (q1 - q0) * 128
                qsrc = (q0 - kr0) * 128
                qcalls = [(QH, oc, tt0, qsrc + tt0, tw, 0, TQ1, TQ2)
                          for oc in range(NC_D) for (tt0, tw) in _tiles(qw)]
                pcn_pipeline(qcalls)
                for qi, qc in enumerate(range(q0, q1)):
                    qsl = slice(qi * 128, (qi + 1) * 128)
                    ksl = slice((qc - 2 - kr0) * 128, (qc + 1 - kr0) * 128)
                    mi = qc - 3 if qc <= 4 else 2
                    bS = [psS[:, :], psP[:, 0:384]]
                    bT = [psT[:, :], psT2[:, :]]
                    bN = [psN[:, :], psR[:, 0:DH + 1]]
                    sE = {}
                    sSm = {}
                    sTm = {}
                    sAm = {}
                    sMv = {}
                    sVv = {}
                    C.wait("tensor", "s")
                    C.wait("tensor", "v")

                    def emit_S(h):
                        p = h % 2
                        hc, hr = h // 2, (h % 2) * 64
                        rows = slice(hr, hr + 64)
                        if h >= 2:
                            C.wait("tensor", "s", sE[h - 2])
                        if C.on("tensor"):
                            C.eng.matmul(bS[p], QH[rows, hc, qsl],
                                         KH[rows, hc, ksl], start=True,
                                         stop=False)
                            i = C.eng.matmul(bS[p], IDM[:, :], MNEG[:, mi, :],
                                             start=False, stop=True)
                            C.tag(i, "m")
                        else:
                            C.inc("m")
                        sSm[h] = C.cnt["m"]

                    def emit_M(h):
                        p = h % 2
                        C.wait("vector", "m", sSm[h])
                        if C.on("vector"):
                            i = C.eng.tensor_reduce(NMX[:, p:p + 1], bS[p],
                                                    mybir.AxisListType.X,
                                                    OP.max, negate=True)
                            C.tag(i, "v")
                        else:
                            C.inc("v")
                        sMv[h] = C.cnt["v"]

                    def emit_E(h):
                        p = h % 2
                        C.wait("scalar", "v", sMv[h])
                        if h >= 2:
                            C.wait("scalar", "m", sTm[h - 2])
                        if C.on("scalar"):
                            i = C.eng.activation(out=PB[:, p, :], in_=bS[p],
                                                 func=AF.Exp,
                                                 bias=NMX[:, p:p + 1])
                            C.tag(i, "s")
                        else:
                            C.inc("s")
                        sE[h] = C.cnt["s"]

                    def emit_T(h):
                        p = h % 2
                        C.wait("tensor", "s", sE[h])
                        if C.on("tensor"):
                            for b3 in range(3):
                                i = C.eng.transpose(
                                    bT[p][:, b3 * 128:(b3 + 1) * 128],
                                    PB[:, p, b3 * 128:(b3 + 1) * 128],
                                    IDM[:, :])
                            C.tag(i, "m")
                        else:
                            C.inc("m")
                        sTm[h] = C.cnt["m"]

                    def emit_V(h):
                        p = h % 2
                        C.wait("vector", "m", sTm[h])
                        if C.on("vector"):
                            i = C.eng.tensor_copy(
                                PJI[:, p, :, :],
                                bT[p][:, :].rearrange("p (b t) -> p b t", b=3))
                            C.tag(i, "v")
                        else:
                            C.inc("v")
                        sVv[h] = C.cnt["v"]

                    def emit_A(h):
                        p = h % 2
                        C.wait("tensor", "v", sVv[h])
                        if C.on("tensor"):
                            for b3 in range(3):
                                i = C.eng.matmul(bN[p], PJI[:, p, b3, :],
                                                 VT[:, qc - 2 - kr0 + b3, h, :],
                                                 start=(b3 == 0), stop=(b3 == 2))
                            C.tag(i, "m")
                        else:
                            C.inc("m")
                        sAm[h] = C.cnt["m"]

                    def emit_R(h):
                        p = h % 2
                        C.wait("vector", "m", sAm[h])
                        if C.on("vector"):
                            i = C.eng.reciprocal(RDN[:, p:p + 1],
                                                 bN[p][:, DH:DH + 1])
                        else:
                            i = None
                        C.fence("vector", i)
                        if C.on("vector"):
                            i = C.eng.tensor_scalar(ATM[:, h, :], bN[p][:, 0:DH],
                                                    RDN[:, p:p + 1], None,
                                                    OP.mult)
                            C.tag(i, "v")
                        else:
                            C.inc("v")

                    emit_S(0)
                    emit_S(1)
                    emit_M(0)
                    emit_E(0)
                    for h in range(16):
                        emit_T(h)
                        if h + 2 < 16:
                            emit_S(h + 2)
                        if h + 1 < 16:
                            emit_M(h + 1)
                        emit_V(h)
                        emit_A(h)
                        if h + 1 < 16:
                            emit_E(h + 1)
                        emit_R(h)
                    fcol = (qi * 128) % 512
                    ATMflat = ATM[:, :, :].rearrange("p h d -> p (h d)")
                    for fb in range(NC_D):
                        if C.on("tensor"):
                            i = C.eng.transpose(psT[:, 0:128],
                                                ATMflat[:, fb * 128:(fb + 1) * 128],
                                                IDM[:, :])
                            C.tag(i, "m")
                        else:
                            C.inc("m")
                        C.wait("vector", "m")
                        if C.on("vector"):
                            i = C.eng.tensor_copy(AFM[:, fb, fcol:fcol + 128],
                                                  psT[:, 0:128])
                            C.tag(i, "v")
                        else:
                            C.inc("v")
                        C.wait("tensor", "v")
                    if fcol == 384 or qc == q1 - 1:
                        nfl = fcol + 128
                        t00 = (qc + 1) * 128 - nfl
                        last = (qc == q1 - 1)
                        for oc in range(NC_D):
                            wbuf = oc % 2
                            if oc == 0:
                                C.wait("gpsimd", "m")
                                if C.on("gpsimd"):
                                    i = C.eng.dma_start(
                                        out=WOC[:, 0, :, :],
                                        in_=wo.rearrange("(c p) o -> p c o",
                                                         p=128)[:, :, 0:128])
                                    C.tag(i, "lw", 16)
                                else:
                                    C.inc("lw", 16)
                            if oc + 1 < NC_D:
                                C.wait("gpsimd", "m")
                                if C.on("gpsimd"):
                                    i = C.eng.dma_start(
                                        out=WOC[:, 1 - wbuf, :, :],
                                        in_=wo.rearrange(
                                            "(c p) o -> p c o", p=128)[
                                            :, :, (oc + 1) * 128:(oc + 2) * 128])
                                    C.tag(i, "lw", 16)
                                else:
                                    C.inc("lw", 16)
                            C.wait("tensor", "lw")
                            if C.on("tensor"):
                                for c in range(NC_D):
                                    i = C.eng.matmul(
                                        psO[:, 0:nfl],
                                        WOC[:, wbuf, c, :],
                                        AFM[:, c, 0:nfl],
                                        start=(c == 0), stop=(c == NC_D - 1))
                                C.tag(i, "m")
                            else:
                                C.inc("m")
                            C.wait("vector", "m")
                            ncar = 256 if last else 0
                            ndir = nfl - ncar
                            if C.on("vector"):
                                if ndir > 0:
                                    i = C.eng.tensor_tensor(
                                        X[:, oc, t00:t00 + ndir],
                                        X[:, oc, t00:t00 + ndir],
                                        psO[:, 0:ndir], OP.add)
                                if ncar:
                                    i = C.eng.tensor_copy(CAR[:, oc, :],
                                                          psO[:, ndir:nfl])
                                C.tag(i, "v")
                            else:
                                C.inc("v")
                            C.wait("tensor", "v")
                        if last:
                            carry.append(q1 - 2)
            if carry:
                cc0 = carry.pop()
                if C.on("vector"):
                    i = C.eng.tensor_tensor(
                        X[:, :, cc0 * 128:(cc0 + 2) * 128],
                        X[:, :, cc0 * 128:(cc0 + 2) * 128],
                        CAR[:, :, :], OP.add)
                    C.tag(i, "v")
                else:
                    C.inc("v")
            C.wait("sync", "v")
            if C.on("sync"):
                i = C.eng.dma_start(
                    out=xout.rearrange("(c p) t -> p c t", p=128),
                    in_=X[:, :, 384:EXT])
                C.tag(i, "st", 16)
            else:
                C.inc("st", 16)
            C.wait("sync", "st")

        run_program(nc, ["ld", "lw", "v", "s", "m", "fv", "st"], prog)
    return nc



NHC = 22


def build_l0f_ffn():
    nc = bass.Bass("TRN2", target_bir_lowering=False, debug=False,
                   detect_race_conditions=RACE_DETECT)
    xin = nc.dram_tensor("xin", [D, TOK + 1], F32, kind="ExternalInput").ap()
    w1i = nc.dram_tensor("w1i", [D, NHC * 256], F32, kind="ExternalInput").ap()
    b1c = nc.dram_tensor("b1c", [128, NHC * 2], F32, kind="ExternalInput").ap()
    w2 = nc.dram_tensor("w2", [NHC * 128, D], F32, kind="ExternalInput").ap()
    b2c = nc.dram_tensor("b2c", [128, NC_D], F32, kind="ExternalInput").ap()
    gln = nc.dram_tensor("gln", [128, NC_D], F32, kind="ExternalInput").ap()
    bln = nc.dram_tensor("bln", [128, NC_D], F32, kind="ExternalInput").ap()
    xout = nc.dram_tensor("xout", [D, TOK], F32, kind="ExternalOutput").ap()

    es = ExitStack()
    with es:
        XT = es.enter_context(nc.sbuf_tensor("XT", [128, NC_D, 513], F32))
        ZNB = es.enter_context(nc.sbuf_tensor("ZNB", [128, NC_D, 512], BF16))
        W1 = es.enter_context(nc.sbuf_tensor("W1", [128, NC_D, NHC * 256], BF16))
        B1C = es.enter_context(nc.sbuf_tensor("B1C", [128, NHC, 2], F32))
        W2C = es.enter_context(nc.sbuf_tensor("W2C", [128, 2, NHC, 128], BF16))
        B2C = es.enter_context(nc.sbuf_tensor("B2C", [128, NC_D], F32))
        GLN = es.enter_context(nc.sbuf_tensor("GLN", [128, NC_D], F32))
        BLN = es.enter_context(nc.sbuf_tensor("BLN", [128, NC_D], F32))
        HB = es.enter_context(nc.sbuf_tensor("HB", [128, NHC, 512], BF16))
        GEL = es.enter_context(nc.sbuf_tensor("GEL", [128, 512], BF16))
        OB = es.enter_context(nc.sbuf_tensor("OB", [128, NC_D, 512], F32))
        lnt = alloc_ln_scratch(es, nc, NC_D, 512, tag="f")
        psA = es.enter_context(nc.psum_tensor("psA", [128, 512], F32))
        psG = es.enter_context(nc.psum_tensor("psG", [128, 512], F32))
        psY = es.enter_context(nc.psum_tensor("psY", [128, 512], F32))

        def prog(C):
            if C.on("gpsimd"):
                i = C.eng.dma_start(out=W1[:, :, :],
                                    in_=w1i.rearrange("(c p) o -> p c o", p=128))
                C.tag(i, "lw", 16)
                i = C.eng.dma_start(out=B1C[:, :, :],
                                    in_=b1c.rearrange("p (h two) -> p h two", two=2))
                C.tag(i, "lw", 16)
                i = C.eng.dma_start(out=B2C[:, :], in_=b2c)
                C.tag(i, "lw", 16)
                i = C.eng.dma_start(out=GLN[:, :], in_=gln)
                C.tag(i, "lw", 16)
                i = C.eng.dma_start(out=BLN[:, :], in_=bln)
                C.tag(i, "lw", 16)
            else:
                C.inc("lw", 80)
            emit_ln_const_init(C, lnt)
            C.wait("tensor", "lw")
            C.wait("scalar", "lw")
            C.wait("vector", "lw")

            for ti in range(TOK // 512):
                if C.on("sync"):
                    i = C.eng.dma_start(
                        out=XT[:, :, :],
                        in_=xin.rearrange("(c p) t -> p c t", p=128)[
                            :, :, ti * 512:ti * 512 + 513])
                    C.tag(i, "ld", 16)
                else:
                    C.inc("ld", 16)
                C.wait("tensor", "ld")
                C.wait("scalar", "ld")
                C.wait("vector", "ld")

                def sx(c):
                    b = 1 if c < 4 else 0
                    return XT[:, c, b:b + 512]

                if C.on("scalar"):
                    for c in range(NC_D):
                        i = C.eng.activation(out=lnt["x2t"][:, c, :], in_=sx(c),
                                             func=AF.Square)
                    C.tag(i, "s")
                else:
                    C.inc("s")
                C.wait("tensor", "s")
                if C.on("tensor"):
                    for c in range(NC_D):
                        C.eng.matmul(lnt["ps_s"][:, :], lnt["onesf"][:, 0:1],
                                     sx(c), start=(c == 0), stop=(c == NC_D - 1))
                    for c in range(NC_D):
                        i = C.eng.matmul(lnt["ps_q"][:, :], lnt["onesf"][:, 0:1],
                                         lnt["x2t"][:, c, :], start=(c == 0),
                                         stop=(c == NC_D - 1))
                    C.tag(i, "m")
                else:
                    C.inc("m")
                C.wait("vector", "m")
                if C.on("vector"):
                    C.eng.tensor_copy(lnt["srow"][:, :], lnt["ps_s"][:, :])
                    i = C.eng.tensor_copy(lnt["qrow"][:, :], lnt["ps_q"][:, :])
                else:
                    i = None
                C.tag_fenced("vector", i, "v")
                Dn = 1024.0
                if C.on("vector"):
                    C.eng.tensor_scalar(lnt["srow"][:, :], lnt["srow"][:, :],
                                        1.0 / Dn, None, OP.mult)
                    C.eng.tensor_tensor(lnt["mrow"][:, :], lnt["srow"][:, :],
                                        lnt["srow"][:, :], OP.mult)
                    i = C.eng.scalar_tensor_tensor(lnt["qrow"][:, :],
                                                   lnt["qrow"][:, :], 1.0 / Dn,
                                                   lnt["mrow"][:, :], OP.mult,
                                                   OP.subtract)
                else:
                    i = None
                C.tag_fenced("vector", i, "v")
                C.wait("scalar", "v")
                if C.on("scalar"):
                    C.eng.activation(out=lnt["rrow"][:, :], in_=lnt["qrow"][:, :],
                                     func=AF.Ln, bias=lnt["epsr"][0:1, 0:1])
                    i = C.eng.activation(out=lnt["rrow"][:, :],
                                         in_=lnt["rrow"][:, :], func=AF.Exp,
                                         scale=-0.5)
                    C.tag(i, "s")
                else:
                    C.inc("s")
                C.wait("vector", "s")
                if C.on("vector"):
                    i = C.eng.tensor_tensor(lnt["mrow"][:, :], lnt["srow"][:, :],
                                            lnt["rrow"][:, :], OP.mult)
                else:
                    i = None
                C.tag_fenced("vector", i, "v")
                C.wait("tensor", "v")
                if C.on("tensor"):
                    C.eng.matmul(lnt["ps_R"][:, :], lnt["onesf"][0:1, 0:128],
                                 lnt["rrow"][:, :], start=True, stop=True)
                    i = C.eng.matmul(lnt["ps_M"][:, :], lnt["onesf"][0:1, 0:128],
                                     lnt["mrow"][:, :], start=True, stop=True)
                    C.tag(i, "m")
                else:
                    C.inc("m")
                C.wait("vector", "m")
                if C.on("vector"):
                    for c in range(NC_D):
                        C.eng.tensor_tensor(OB[:, c, :], sx(c), lnt["ps_R"][:, :],
                                            OP.mult)
                    C.eng.tensor_tensor(
                        OB[:, :, :], OB[:, :, :],
                        lnt["ps_M"][:, :].rearrange("p (o t) -> p o t",
                                                    o=1).broadcast_to(
                            (128, NC_D, 512)), OP.subtract)
                    for c in range(NC_D):
                        i = C.eng.tensor_scalar(ZNB[:, c, :], OB[:, c, :],
                                                GLN[:, c:c + 1], BLN[:, c:c + 1],
                                                OP.mult, OP.add)
                    C.tag(i, "v")
                else:
                    C.inc("v")
                C.wait("tensor", "v")

                for hc in range(NHC):
                    if C.on("tensor"):
                        for c in range(NC_D):
                            C.eng.matmul(psA[:, :], W1[:, c, hc * 256:hc * 256 + 128],
                                         ZNB[:, c, :], start=(c == 0),
                                         stop=(c == NC_D - 1))
                        for c in range(NC_D):
                            i = C.eng.matmul(psG[:, :],
                                             W1[:, c, hc * 256 + 128:hc * 256 + 256],
                                             ZNB[:, c, :], start=(c == 0),
                                             stop=(c == NC_D - 1))
                        C.tag(i, "m")
                    else:
                        C.inc("m")
                    C.wait("scalar", "m")
                    if C.on("scalar"):
                        i = C.eng.activation(out=GEL[:, :], in_=psG[:, :],
                                             func=AF.Gelu, bias=B1C[:, hc, 1:2])
                        C.tag(i, "s")
                    else:
                        C.inc("s")
                    C.wait("vector", "s")
                    if C.on("vector"):
                        i = C.eng.scalar_tensor_tensor(HB[:, hc, :], psA[:, :],
                                                       B1C[:, hc, 0:1], GEL[:, :],
                                                       OP.add, OP.mult)
                        C.tag(i, "v")
                    else:
                        C.inc("v")
                    C.wait("tensor", "v")
                    C.wait("scalar", "v")
                for oc in range(NC_D):
                    wbuf = oc % 2
                    if ti == 0 and oc == 0:
                        if C.on("gpsimd"):
                            i = C.eng.dma_start(
                                out=W2C[:, 0, :, :],
                                in_=w2.rearrange("(c p) o -> p c o", p=128)[
                                    :, :, 0:128])
                            C.tag(i, "lw", 16)
                        else:
                            C.inc("lw", 16)
                    noc = oc + 1
                    if noc < NC_D or ti + 1 < TOK // 512:
                        noc2 = noc % NC_D
                        C.wait("gpsimd", "m")
                        if C.on("gpsimd"):
                            i = C.eng.dma_start(
                                out=W2C[:, 1 - wbuf, :, :],
                                in_=w2.rearrange("(c p) o -> p c o", p=128)[
                                    :, :, noc2 * 128:(noc2 + 1) * 128])
                            C.tag(i, "lw", 16)
                        else:
                            C.inc("lw", 16)
                    C.wait("tensor", "lw")
                    if C.on("tensor"):
                        for hc in range(NHC):
                            i = C.eng.matmul(psY[:, :],
                                             W2C[:, wbuf, hc, :],
                                             HB[:, hc, :], start=(hc == 0),
                                             stop=(hc == NHC - 1))
                        C.tag(i, "m")
                    else:
                        C.inc("m")
                    C.wait("vector", "m")
                    if C.on("vector"):
                        i = C.eng.scalar_tensor_tensor(OB[:, oc, :], psY[:, :],
                                                       B2C[:, oc:oc + 1],
                                                       XT[:, oc, 1:513],
                                                       OP.add, OP.add)
                        C.tag(i, "v")
                    else:
                        C.inc("v")
                    C.wait("tensor", "v")
                C.wait("sync", "v")
                if C.on("sync"):
                    i = C.eng.dma_start(
                        out=xout.rearrange("(c p) t -> p c t", p=128)[
                            :, :, ti * 512:(ti + 1) * 512],
                        in_=OB[:, :, :])
                    C.tag(i, "st", 16)
                else:
                    C.inc("st", 16)
                C.wait("sync", "st")

        run_program(nc, ["ld", "lw", "v", "s", "m", "fv", "st"], prog)
    return nc



def build_pre_ln():
    nc = bass.Bass("TRN2", target_bir_lowering=False, debug=False,
                   detect_race_conditions=RACE_DETECT)
    xo = nc.dram_tensor("xo", [D, TOK], F32, kind="ExternalInput").ap()
    out = nc.dram_tensor("out", [D, TOK], F32, kind="ExternalOutput").ap()
    TG = 1024
    es = ExitStack()
    with es:
        X = es.enter_context(nc.sbuf_tensor("X", [128, NC_D, TG], F32))
        OB = es.enter_context(nc.sbuf_tensor("OB", [128, NC_D, 512], F32))
        lnt = alloc_ln_scratch(es, nc, NC_D, TG)

        def prog(C):
            emit_ln_const_init(C, lnt)
            xor = xo.rearrange("(c p) t -> p c t", p=128)
            outr = out.rearrange("(c p) t -> p c t", p=128)
            for g in range(TOK // TG):
                gsl = slice(g * TG, (g + 1) * TG)
                C.dma("sync", "ld", lambda e: e.dma_start(
                    out=X[:, :, :], in_=xor[:, :, gsl]))
                C.wait("tensor", "ld")
                C.wait("scalar", "ld")
                C.wait("vector", "ld")

                def apply(C, ti):
                    sl = slice(ti * 512, (ti + 1) * 512)
                    if C.on("vector"):
                        C.eng.tensor_tensor(OB[:, :, :], X[:, :, sl],
                                            bcN(lnt["ps_R"], NC_D), OP.mult)
                        i = C.eng.tensor_tensor(OB[:, :, :], OB[:, :, :],
                                                bcN(lnt["ps_M"], NC_D),
                                                OP.subtract)
                        C.tag(i, "v")
                    else:
                        C.inc("v")
                    C.wait("sync", "v")
                    C.dma("sync", "st", lambda e: e.dma_start(
                        out=outr[:, :, gsl][:, :, sl], in_=OB[:, :, :]))
                    C.wait("vector", "st")

                emit_ln_fm(C, lnt, X, TG, apply)
                C.wait("sync", "v")
                C.wait("sync", "s")
                C.wait("sync", "m")
            C.wait("sync", "st")

        run_program(nc, ["ld", "lw", "v", "s", "m", "fv", "st"], prog)
    return nc


def build_post_ln():
    nc = bass.Bass("TRN2", target_bir_lowering=False, debug=False,
                   detect_race_conditions=RACE_DETECT)
    yo = nc.dram_tensor("yo", [D, TOK], F32, kind="ExternalInput").ap()
    xo = nc.dram_tensor("xo", [D, TOK], F32, kind="ExternalInput").ap()
    out = nc.dram_tensor("out", [D, TOK], F32, kind="ExternalOutput").ap()
    TG = 1024
    es = ExitStack()
    with es:
        Y = es.enter_context(nc.sbuf_tensor("Y", [128, NC_D, TG], F32))
        X = es.enter_context(nc.sbuf_tensor("X", [128, NC_D, TG], F32))
        OB = es.enter_context(nc.sbuf_tensor("OB", [128, NC_D, 512], F32))
        lnt = alloc_ln_scratch(es, nc, NC_D, TG)

        def prog(C):
            emit_ln_const_init(C, lnt)
            yor = yo.rearrange("(c p) t -> p c t", p=128)
            xor = xo.rearrange("(c p) t -> p c t", p=128)
            outr = out.rearrange("(c p) t -> p c t", p=128)
            for g in range(TOK // TG):
                gsl = slice(g * TG, (g + 1) * TG)
                C.dma("sync", "ld", lambda e: e.dma_start(
                    out=Y[:, :, :], in_=yor[:, :, gsl]))
                C.dma("sync", "ld", lambda e: e.dma_start(
                    out=X[:, :, :], in_=xor[:, :, gsl]))
                C.wait("tensor", "ld")
                C.wait("scalar", "ld")
                C.wait("vector", "ld")

                def apply(C, ti):
                    sl = slice(ti * 512, (ti + 1) * 512)
                    if C.on("vector"):
                        C.eng.tensor_tensor(OB[:, :, :], Y[:, :, sl],
                                            bcN(lnt["ps_R"], NC_D), OP.mult)
                        C.eng.tensor_tensor(OB[:, :, :], OB[:, :, :],
                                            bcN(lnt["ps_M"], NC_D), OP.subtract)
                        i = C.eng.tensor_tensor(OB[:, :, :], OB[:, :, :],
                                                X[:, :, sl], OP.add)
                        C.tag(i, "v")
                    else:
                        C.inc("v")
                    C.wait("sync", "v")
                    C.dma("sync", "st", lambda e: e.dma_start(
                        out=outr[:, :, gsl][:, :, sl], in_=OB[:, :, :]))
                    C.wait("vector", "st")

                emit_ln_fm(C, lnt, Y, TG, apply)
                C.wait("sync", "v")
                C.wait("sync", "s")
                C.wait("sync", "m")
            C.wait("sync", "st")

        run_program(nc, ["ld", "lw", "v", "s", "m", "fv", "st"], prog)
    return nc



def build_l1a_A():
    nc = bass.Bass("TRN2", target_bir_lowering=False, debug=False,
                   detect_race_conditions=RACE_DETECT)
    TE = TOK + 2
    xo = nc.dram_tensor("xo", [D, TE], F32, kind="ExternalInput").ap()
    nsc = nc.dram_tensor("nsc", [128, NC_D], F32, kind="ExternalInput").ap()
    inw = nc.dram_tensor("inw", [D, 2 * D], F32, kind="ExternalInput").ap()
    inb = nc.dram_tensor("inb", [128, 16], F32, kind="ExternalInput").ap()
    dww = nc.dram_tensor("dww", [128, NC_D * 3], F32, kind="ExternalInput").ap()
    dwb = nc.dram_tensor("dwb", [128, NC_D], F32, kind="ExternalInput").ap()
    pww = nc.dram_tensor("pww", [D, D], F32, kind="ExternalInput").ap()
    pwb = nc.dram_tensor("pwb", [128, NC_D], F32, kind="ExternalInput").ap()
    winf = nc.dram_tensor("winf", [D, 64], F32, kind="ExternalInput").ap()
    winb = nc.dram_tensor("winb", [D, 64], F32, kind="ExternalInput").ap()
    sspar = nc.dram_tensor("sspar", [64, 6], F32, kind="ExternalInput").ap()
    edge = nc.dram_tensor("edge", [128, 2], F32, kind="ExternalInput").ap()
    t_out = nc.dram_tensor("t_out", [D, TOK], F32, kind="ExternalOutput").ap()
    g_out = nc.dram_tensor("g_out", [D, TOK], BF16, kind="ExternalOutput").ap()
    hf_out = nc.dram_tensor("hf_out", [64, 2 * TOK], F32,
                            kind="ExternalOutput").ap()

    es = ExitStack()
    with es:
        XT = es.enter_context(nc.sbuf_tensor("XT", [128, NC_D, 514], F32))
        TT = es.enter_context(nc.sbuf_tensor("TT", [128, NC_D, 514], F32))
        TNB = es.enter_context(nc.sbuf_tensor("TNB", [128, NC_D, 514], BF16))
        XC = es.enter_context(nc.sbuf_tensor("XC", [128, NC_D, 514], F32))
        CVB = es.enter_context(nc.sbuf_tensor("CVB", [128, NC_D, 512], BF16))
        GAT = es.enter_context(nc.sbuf_tensor("GAT", [128, NC_D, 512], BF16))
        CVT = es.enter_context(nc.sbuf_tensor("CVT", [128, 512], F32))
        W = es.enter_context(nc.sbuf_tensor("W", [128, NC_D, 2 * D], BF16))
        WIN = es.enter_context(nc.sbuf_tensor("WIN", [128, NC_D, 128], BF16))
        NSC = es.enter_context(nc.sbuf_tensor("NSC", [128, NC_D], F32))
        INB = es.enter_context(nc.sbuf_tensor("INB", [128, 16], F32))
        DWW = es.enter_context(nc.sbuf_tensor("DWW", [128, NC_D, 3], F32))
        DWB = es.enter_context(nc.sbuf_tensor("DWB", [128, NC_D], F32))
        PWB = es.enter_context(nc.sbuf_tensor("PWB", [128, NC_D], F32))
        SSP = es.enter_context(nc.sbuf_tensor("SSP", [64, 6], F32))
        EDG = es.enter_context(nc.sbuf_tensor("EDG", [128, 2], F32))
        PRM = es.enter_context(nc.sbuf_tensor("PRM", [64, 8], F32))
        ABC = es.enter_context(nc.sbuf_tensor("ABC", [64, 4], F32))
        ABT = es.enter_context(nc.sbuf_tensor("ABT", [64, 2, TOK], F32))
        UF = es.enter_context(nc.sbuf_tensor("UF", [64, 2, TOK], F32))
        HF = es.enter_context(nc.sbuf_tensor("HF", [64, 2, TOK], F32))
        Z0 = es.enter_context(nc.sbuf_tensor("Z0", [64, 1], F32))
        MSK = es.enter_context(nc.sbuf_tensor("MSK", [64, 1], mybir.dt.int32))
        lnt = alloc_ln_scratch(es, nc, NC_D, 514)
        ps = [es.enter_context(nc.psum_tensor(f"ps{i}", [128, 512], F32))
              for i in range(2)]
        psU = es.enter_context(nc.psum_tensor("psU", [64, 512], F32))

        def prog(C):
            xor = xo.rearrange("(c p) t -> p c t", p=128)
            tor = t_out.rearrange("(c p) t -> p c t", p=128)
            gor = g_out.rearrange("(c p) t -> p c t", p=128)
            C.dma("gpsimd", "lw", lambda e: e.dma_start(out=NSC[:, :], in_=nsc))
            C.dma("gpsimd", "lw", lambda e: e.dma_start(out=INB[:, :], in_=inb))
            C.dma("gpsimd", "lw", lambda e: e.dma_start(
                out=DWW[:, :, :], in_=dww.rearrange("p (c k) -> p c k", k=3)))
            C.dma("gpsimd", "lw", lambda e: e.dma_start(out=DWB[:, :], in_=dwb))
            C.dma("gpsimd", "lw", lambda e: e.dma_start(out=PWB[:, :], in_=pwb))
            C.dma("gpsimd", "lw", lambda e: e.dma_start(out=SSP[:, :], in_=sspar))
            C.dma("gpsimd", "lw", lambda e: e.dma_start(out=EDG[:, :], in_=edge))
            C.dma("gpsimd", "lw", lambda e: e.dma_start(
                out=WIN[:, :, 0:64],
                in_=winf.rearrange("(c p) n -> p c n", p=128)))
            C.dma("gpsimd", "lw", lambda e: e.dma_start(
                out=WIN[:, :, 64:128],
                in_=winb.rearrange("(c p) n -> p c n", p=128)))
            emit_ln_const_init(C, lnt)

            C.wait("scalar", "lw")
            for di in range(2):
                la, lb, ldt = 3 * di, 3 * di + 1, 3 * di + 2
                C.op("scalar", "s", lambda e: e.activation(
                    out=PRM[:, 0:1], in_=SSP[:, la:la + 1], func=AF.Exp))
                C.op("scalar", "s", lambda e: e.activation(
                    out=PRM[:, 1:2], in_=SSP[:, ldt:ldt + 1], func=AF.Exp))
                C.wait("vector", "s")
                if C.on("vector"):
                    i = C.eng.tensor_tensor(PRM[:, 2:3], PRM[:, 0:1],
                                            PRM[:, 1:2], OP.mult)
                else:
                    i = None
                C.tag_fenced("vector", i, "v")
                C.wait("scalar", "v")
                C.op("scalar", "s", lambda e: e.activation(
                    out=PRM[:, 3:4], in_=PRM[:, 2:3], func=AF.Exp, scale=-1.0))
                C.wait("vector", "s")
                if C.on("vector"):
                    i = C.eng.reciprocal(PRM[:, 4:5], PRM[:, 0:1])
                else:
                    i = None
                C.tag_fenced("vector", i, "v")
                if C.on("vector"):
                    i = C.eng.tensor_scalar(PRM[:, 5:6], PRM[:, 3:4], -1.0, 1.0,
                                            OP.mult, OP.add)
                else:
                    i = None
                C.tag_fenced("vector", i, "v")
                if C.on("vector"):
                    i = C.eng.tensor_tensor(PRM[:, 7:8], PRM[:, 5:6],
                                            PRM[:, 4:5], OP.mult)
                else:
                    i = None
                C.tag_fenced("vector", i, "v")
                if C.on("vector"):
                    i = C.eng.tensor_scalar(MSK[:, 0:1], SSP[:, la:la + 1],
                                            float(np.log(1e-6)), None, OP.is_lt)
                else:
                    i = None
                C.tag_fenced("vector", i, "v")
                if C.on("vector"):
                    i = C.eng.select(Z0[:, 0:1], MSK[:, 0:1], PRM[:, 1:2],
                                     PRM[:, 7:8])
                else:
                    i = None
                C.tag_fenced("vector", i, "v")
                if C.on("vector"):
                    C.eng.tensor_tensor(ABC[:, 2 * di + 1:2 * di + 2],
                                        Z0[:, 0:1], SSP[:, lb:lb + 1], OP.mult)
                    i = C.eng.tensor_copy(ABC[:, 2 * di:2 * di + 1], PRM[:, 3:4])
                else:
                    i = None
                C.tag_fenced("vector", i, "v")
            if C.on("vector"):
                C.eng.tensor_copy(ABT[:, 0, :],
                                  ABC[:, 0:1].broadcast_to((64, TOK)))
                i = C.eng.tensor_copy(ABT[:, 1, :],
                                      ABC[:, 2:3].broadcast_to((64, TOK)))
                C.tag(i, "v")
            else:
                C.inc("v")

            for g in range(TOK // 512):
                t0 = g * 512
                C.wait("sync", "s")
                C.wait("sync", "m")
                C.wait("sync", "v")
                C.dma("sync", "ld", lambda e: e.dma_start(
                    out=XT[:, :, :], in_=xor[:, :, t0:t0 + 514]))
                if g > 0:
                    C.wait("gpsimd", "m")
                    C.dma("gpsimd", "lw", lambda e: e.dma_start(
                        out=W[:, :, 0:D],
                        in_=inw[:, 0:D].rearrange("(c p) o -> p c o", p=128)))
                else:
                    C.dma("gpsimd", "lw", lambda e: e.dma_start(
                        out=W[:, :, :],
                        in_=inw.rearrange("(c p) o -> p c o", p=128)))
                C.wait("tensor", "ld")
                C.wait("scalar", "ld")
                C.wait("vector", "ld")
                C.wait("tensor", "lw")
                C.wait("scalar", "lw")

                def applyT(C, ti, base=0):
                    sl = slice(base + ti * 512, base + (ti + 1) * 512)
                    if C.on("vector"):
                        C.eng.tensor_tensor(TT[:, :, sl], XT[:, :, sl],
                                            bcN(lnt["ps_R"], NC_D), OP.mult)
                        i = C.eng.tensor_tensor(TT[:, :, sl], TT[:, :, sl],
                                                bcN(lnt["ps_M"], NC_D),
                                                OP.subtract)
                        C.tag(i, "v")
                    else:
                        C.inc("v")
                    C.wait("tensor", "v")

                emit_ln_fm(C, lnt, XT[:, :, 0:512], 512, applyT)
                emit_ln_fm(C, lnt, XT[:, :, 2:514], 512,
                           lambda C, ti: applyT(C, ti, base=2))
                C.wait("sync", "v")
                C.dma("sync", "st", lambda e: e.dma_start(
                    out=tor[:, :, t0:t0 + 512], in_=TT[:, :, 1:513]))

                def applyR(C, ti, base=0):
                    sl = slice(base + ti * 512, base + (ti + 1) * 512)
                    if C.on("vector"):
                        C.eng.tensor_tensor(lnt["x2t"][:, :, :], TT[:, :, sl],
                                            bcN(lnt["ps_R"], NC_D), OP.mult)
                        for c in range(NC_D):
                            i = C.eng.tensor_scalar(TNB[:, c, sl],
                                                    lnt["x2t"][:, c, :],
                                                    NSC[:, c:c + 1], None,
                                                    OP.mult)
                        C.tag(i, "v")
                    else:
                        C.inc("v")
                    C.wait("tensor", "v")
                    C.wait("scalar", "v")

                emit_ln_fm(C, lnt, TT[:, :, 0:512], 512, applyR, rms=True,
                           eps=1e-6)
                emit_ln_fm(C, lnt, TT[:, :, 2:514], 512,
                           lambda C, ti: applyR(C, ti, base=2), rms=True,
                           eps=1e-6)

                sev = []
                for oc in range(NC_D):
                    for (a0, aw) in ((0, 512), (512, 2)):
                        bank = len(sev) % 2
                        if len(sev) >= 2:
                            C.wait("tensor", "s", sev[-2])
                        if C.on("tensor"):
                            for c in range(NC_D):
                                i = C.eng.matmul(ps[bank][:, 0:aw],
                                                 W[:, c, oc * 128:(oc + 1) * 128],
                                                 TNB[:, c, a0:a0 + aw],
                                                 start=(c == 0),
                                                 stop=(c == NC_D - 1))
                            C.tag(i, "m")
                        else:
                            C.inc("m")
                        C.wait("scalar", "m")
                        C.op("scalar", "s", lambda e, oc=oc, a0=a0, aw=aw,
                             bank=bank: e.activation(
                                 out=XC[:, oc, a0:a0 + aw], in_=ps[bank][:, 0:aw],
                                 func=AF.Identity, bias=INB[:, oc:oc + 1]))
                        sev.append(C.cnt["s"])
                for oc in range(NC_D):
                    bank = len(sev) % 2
                    if len(sev) >= 2:
                        C.wait("tensor", "s", sev[-2])
                    if C.on("tensor"):
                        for c in range(NC_D):
                            i = C.eng.matmul(ps[bank][:, :],
                                             W[:, c, D + oc * 128:D + (oc + 1) * 128],
                                             TNB[:, c, 1:513], start=(c == 0),
                                             stop=(c == NC_D - 1))
                        C.tag(i, "m")
                    else:
                        C.inc("m")
                    C.wait("scalar", "m")
                    C.op("scalar", "s", lambda e, oc=oc, bank=bank: e.activation(
                        out=GAT[:, oc, :], in_=ps[bank][:, :], func=AF.Sigmoid,
                        bias=INB[:, 8 + oc:9 + oc]))
                    sev.append(C.cnt["s"])
                C.wait("sync", "s")
                C.dma("sync", "st", lambda e: e.dma_start(
                    out=gor[:, :, t0:t0 + 512], in_=GAT[:, :, :]))

                C.wait("vector", "s")
                C.wait("vector", "lw")
                if g == 0:
                    if C.on("vector"):
                        i = C.eng.tensor_scalar(XC[:, :, 0:1], XC[:, :, 0:1],
                                                EDG[:, 0:1], None, OP.mult)
                        C.tag(i, "v")
                    else:
                        C.inc("v")
                if g == TOK // 512 - 1:
                    if C.on("vector"):
                        i = C.eng.tensor_scalar(XC[:, :, 513:514],
                                                XC[:, :, 513:514],
                                                EDG[:, 1:2], None, OP.mult)
                        C.tag(i, "v")
                    else:
                        C.inc("v")
                for c in range(NC_D):
                    if C.on("vector"):
                        C.eng.tensor_scalar(CVT[:, :], XC[:, c, 0:512],
                                            DWW[:, c, 0:1], None, OP.mult)
                        C.eng.scalar_tensor_tensor(CVT[:, :], XC[:, c, 1:513],
                                                   DWW[:, c, 1:2], CVT[:, :],
                                                   OP.mult, OP.add)
                        C.eng.scalar_tensor_tensor(CVT[:, :], XC[:, c, 2:514],
                                                   DWW[:, c, 2:3], CVT[:, :],
                                                   OP.mult, OP.add)
                        i = C.eng.tensor_scalar(CVB[:, c, :], CVT[:, :],
                                                DWB[:, c:c + 1], None, OP.add)
                    else:
                        i = None
                    C.tag_fenced("vector", i, "v")
                C.wait("gpsimd", "m")
                C.dma("gpsimd", "lw", lambda e: e.dma_start(
                    out=W[:, :, 0:D],
                    in_=pww.rearrange("(c p) o -> p c o", p=128)))
                C.wait("tensor", "lw")
                C.wait("tensor", "v")
                sev2 = []
                for oc in range(NC_D):
                    bank = len(sev2) % 2
                    if len(sev2) >= 2:
                        C.wait("tensor", "s", sev2[-2])
                    if C.on("tensor"):
                        for c in range(NC_D):
                            i = C.eng.matmul(ps[bank][:, :],
                                             W[:, c, oc * 128:(oc + 1) * 128],
                                             CVB[:, c, :], start=(c == 0),
                                             stop=(c == NC_D - 1))
                        C.tag(i, "m")
                    else:
                        C.inc("m")
                    C.wait("scalar", "m")
                    C.op("scalar", "s", lambda e, oc=oc, bank=bank: e.activation(
                        out=XC[:, oc, 0:512], in_=ps[bank][:, :],
                        func=AF.Identity, bias=PWB[:, oc:oc + 1]))
                    sev2.append(C.cnt["s"])
                C.wait("vector", "s")
                if C.on("vector"):
                    i = C.eng.tensor_copy(CVB[:, :, :], XC[:, :, 0:512])
                    C.tag(i, "v")
                else:
                    C.inc("v")
                C.wait("tensor", "v")
                for di in range(2):
                    if C.on("tensor"):
                        for c in range(NC_D):
                            i = C.eng.matmul(psU[:, :],
                                             WIN[:, c, di * 64:(di + 1) * 64],
                                             CVB[:, :, :][:, c, :],
                                             start=(c == 0),
                                             stop=(c == NC_D - 1))
                        C.tag(i, "m")
                    else:
                        C.inc("m")
                    C.wait("vector", "m")
                    if C.on("vector"):
                        i = C.eng.tensor_scalar(UF[:, di, t0:t0 + 512],
                                                psU[:, :],
                                                ABC[:, 2 * di + 1:2 * di + 2],
                                                None, OP.mult)
                    else:
                        i = None
                    C.tag_fenced("vector", i, "v")
                    C.wait("tensor", "v")

            if C.on("vector"):
                i = C.eng.engine_nop()
            else:
                i = None
            C.fence("vector", i)
            if C.on("vector"):
                C.eng.tensor_tensor_scan(HF[:, 0, :], ABT[:, 0, :], UF[:, 0, :],
                                         0.0, OP.mult, OP.add)
                i = C.eng.tensor_tensor_scan(HF[:, 1, ::-1], ABT[:, 1, ::-1],
                                             UF[:, 1, ::-1], 0.0,
                                             OP.mult, OP.add)
                C.tag(i, "v")
            else:
                C.inc("v")
            C.wait("sync", "v")
            C.dma("sync", "st", lambda e: e.dma_start(
                out=hf_out, in_=HF[:, :, :].rearrange("p a t -> p (a t)")))
            C.wait("sync", "st")

        run_program(nc, ["ld", "lw", "v", "s", "m", "fv", "st"], prog)
    return nc



def build_l1a_B(dbg=False):
    nc = bass.Bass("TRN2", target_bir_lowering=False, debug=False,
                   detect_race_conditions=RACE_DETECT)
    hfin = nc.dram_tensor("hfin", [64, 2 * TOK], F32, kind="ExternalInput").ap()
    carry = nc.dram_tensor("carry", [64, 2], F32, kind="ExternalInput").ap()
    sspar = nc.dram_tensor("sspar", [64, 6], F32, kind="ExternalInput").ap()
    cpar = nc.dram_tensor("cpar", [64, 2], F32, kind="ExternalInput").ap()
    woutT = nc.dram_tensor("woutT", [64, 2 * D], F32, kind="ExternalInput").ap()
    outwT = nc.dram_tensor("outwT", [D, D], F32, kind="ExternalInput").ap()
    outb = nc.dram_tensor("outb", [128, NC_D], F32, kind="ExternalInput").ap()
    gin = nc.dram_tensor("gin", [D, TOK], BF16, kind="ExternalInput").ap()
    tin = nc.dram_tensor("tin", [D, TOK], F32, kind="ExternalInput").ap()
    xin = nc.dram_tensor("xin", [D, TOK], F32, kind="ExternalInput").ap()
    out = nc.dram_tensor("out", [D, TOK], F32, kind="ExternalOutput").ap()
    if dbg:
        d_pow = nc.dram_tensor("d_pow", [64, 2 * TOK], F32,
                               kind="ExternalOutput").ap()
        d_hf = nc.dram_tensor("d_hf", [64, 2 * TOK], F32,
                              kind="ExternalOutput").ap()
        d_y = nc.dram_tensor("d_y", [D, TOK], F32, kind="ExternalOutput").ap()
    TG = 512

    es = ExitStack()
    with es:
        HF = es.enter_context(nc.sbuf_tensor("HF", [64, 2, TOK], F32))
        HTB = es.enter_context(nc.sbuf_tensor("HTB", [64, 2, TOK], BF16))
        IOT = es.enter_context(nc.sbuf_tensor("IOT", [64, 2, TOK], F32))
        POW = es.enter_context(nc.sbuf_tensor("POW", [64, TOK], F32))
        CAR = es.enter_context(nc.sbuf_tensor("CAR", [64, 2], F32))
        SSP = es.enter_context(nc.sbuf_tensor("SSP", [64, 6], F32))
        CPA = es.enter_context(nc.sbuf_tensor("CPA", [64, 2], F32))
        PRM = es.enter_context(nc.sbuf_tensor("PRM", [64, 4], F32))
        WOT = es.enter_context(nc.sbuf_tensor("WOT", [64, 2, D], BF16))
        OWT = es.enter_context(nc.sbuf_tensor("OWT", [128, NC_D, D], BF16))
        OBC = es.enter_context(nc.sbuf_tensor("OBC", [128, NC_D], F32))
        GATE = es.enter_context(nc.sbuf_tensor("GATE", [128, NC_D, TG], BF16))
        TIN = es.enter_context(nc.sbuf_tensor("TIN", [128, NC_D, TG], F32))
        XIN = es.enter_context(nc.sbuf_tensor("XIN", [128, NC_D, TG], F32))
        Y = es.enter_context(nc.sbuf_tensor("Y", [128, NC_D, TG], F32))
        YGB = es.enter_context(nc.sbuf_tensor("YGB", [128, NC_D, TG], BF16))
        OB = es.enter_context(nc.sbuf_tensor("OB", [128, NC_D, 512], F32))
        lnt = alloc_ln_scratch(es, nc, NC_D, TG)
        psY = es.enter_context(nc.psum_tensor("psY", [128, 512], F32))
        psO = es.enter_context(nc.psum_tensor("psO", [128, 512], F32))

        def prog(C):
            gir = gin.rearrange("(c p) t -> p c t", p=128)
            tir = tin.rearrange("(c p) t -> p c t", p=128)
            xir = xin.rearrange("(c p) t -> p c t", p=128)
            outr = out.rearrange("(c p) t -> p c t", p=128)
            C.dma("sync", "ld", lambda e: e.dma_start(
                out=HF[:, :, :], in_=hfin.rearrange("p (a t) -> p a t", a=2)))
            C.dma("gpsimd", "lw", lambda e: e.dma_start(out=CAR[:, :], in_=carry))
            C.dma("gpsimd", "lw", lambda e: e.dma_start(out=SSP[:, :], in_=sspar))
            C.dma("gpsimd", "lw", lambda e: e.dma_start(out=CPA[:, :], in_=cpar))
            C.dma("gpsimd", "lw", lambda e: e.dma_start(
                out=WOT[:, :, :], in_=woutT.rearrange("p (a d) -> p a d", a=2)))
            C.dma("gpsimd", "lw", lambda e: e.dma_start(
                out=OWT[:, :, :], in_=outwT.rearrange("(c p) o -> p c o", p=128)))
            C.dma("gpsimd", "lw", lambda e: e.dma_start(out=OBC[:, :], in_=outb))
            emit_ln_const_init(C, lnt)
            C.op("gpsimd", "v", lambda e: e.iota(
                IOT[:, 0, :], pattern=[[-1, TOK]], base=-1, channel_multiplier=0,
                allow_small_or_imprecise_dtypes=True))
            C.op("gpsimd", "v", lambda e: e.iota(
                IOT[:, 1, :], pattern=[[1, TOK]], base=-TOK,
                channel_multiplier=0, allow_small_or_imprecise_dtypes=True))
            C.wait("scalar", "lw")
            C.wait("vector", "lw")
            C.wait("vector", "ld")
            for di in range(2):
                la, ldt = 3 * di, 3 * di + 2
                C.op("scalar", "s", lambda e, la=la: e.activation(
                    out=PRM[:, 0:1], in_=SSP[:, la:la + 1], func=AF.Exp))
                C.op("scalar", "s", lambda e, ldt=ldt: e.activation(
                    out=PRM[:, 1:2], in_=SSP[:, ldt:ldt + 1], func=AF.Exp))
                C.wait("vector", "s")
                if C.on("vector"):
                    i = C.eng.tensor_tensor(PRM[:, 2:3], PRM[:, 0:1],
                                            PRM[:, 1:2], OP.mult)
                else:
                    i = None
                C.tag_fenced("vector", i, "v")
                C.wait("scalar", "v")
                ram = IOT[:, di, :]
                C.op("scalar", "s", lambda e, ram=ram: e.activation(
                    out=POW[:, :], in_=ram, func=AF.Exp, scale=PRM[:, 2:3]))
                C.wait("vector", "s")
                if C.on("vector"):
                    i = C.eng.scalar_tensor_tensor(HF[:, di, :], POW[:, :],
                                                   CAR[:, di:di + 1],
                                                   HF[:, di, :], OP.mult, OP.add)
                else:
                    i = None
                C.tag_fenced("vector", i, "v")
                if C.on("vector"):
                    i = C.eng.tensor_scalar(HTB[:, di, :], HF[:, di, :],
                                            CPA[:, di:di + 1], None, OP.mult)
                    C.tag(i, "v")
                else:
                    C.inc("v")
                if dbg:
                    C.wait("sync", "v")
                    C.dma("sync", "st", lambda e, di=di: e.dma_start(
                        out=d_pow.rearrange("p (a t) -> p a t", a=2)[:, di, :],
                        in_=POW[:, :]))
                    C.dma("sync", "st", lambda e, di=di: e.dma_start(
                        out=d_hf.rearrange("p (a t) -> p a t", a=2)[:, di, :],
                        in_=HF[:, di, :]))
            C.wait("tensor", "lw")
            C.wait("tensor", "v")
            for g in range(TOK // TG):
                gsl = slice(g * TG, (g + 1) * TG)
                C.wait("sync", "s")
                C.wait("sync", "m")
                C.wait("sync", "v")
                C.dma("sync", "ld", lambda e: e.dma_start(
                    out=GATE[:, :, :], in_=gir[:, :, gsl]))
                C.dma("sync", "ld", lambda e: e.dma_start(
                    out=TIN[:, :, :], in_=tir[:, :, gsl]))
                C.dma("sync", "ld", lambda e: e.dma_start(
                    out=XIN[:, :, :], in_=xir[:, :, gsl]))
                C.wait("tensor", "ld")
                C.wait("vector", "ld")
                for oc in range(NC_D):
                    for (tt0, tw) in _tiles(TG):
                        if C.on("tensor"):
                            C.eng.matmul(psY[:, 0:tw],
                                         WOT[:, 0, oc * 128:(oc + 1) * 128],
                                         HTB[:, 0, g * TG + tt0:g * TG + tt0 + tw],
                                         start=True, stop=False)
                            i = C.eng.matmul(psY[:, 0:tw],
                                             WOT[:, 1, oc * 128:(oc + 1) * 128],
                                             HTB[:, 1, g * TG + tt0:g * TG + tt0 + tw],
                                             start=False, stop=True)
                            C.tag(i, "m")
                        else:
                            C.inc("m")
                        C.wait("vector", "m")
                        if C.on("vector"):
                            i = C.eng.tensor_tensor(
                                YGB[:, oc, tt0:tt0 + tw], psY[:, 0:tw],
                                GATE[:, oc, tt0:tt0 + tw], OP.mult)
                            C.tag(i, "v")
                        else:
                            C.inc("v")
                        C.wait("tensor", "v")
                for oc in range(NC_D):
                    for (tt0, tw) in _tiles(TG):
                        if C.on("tensor"):
                            for c in range(NC_D):
                                i = C.eng.matmul(psO[:, 0:tw],
                                                 OWT[:, c, oc * 128:(oc + 1) * 128],
                                                 YGB[:, c, tt0:tt0 + tw],
                                                 start=(c == 0),
                                                 stop=(c == NC_D - 1))
                            C.tag(i, "m")
                        else:
                            C.inc("m")
                        C.wait("vector", "m")
                        if C.on("vector"):
                            i = C.eng.scalar_tensor_tensor(
                                Y[:, oc, tt0:tt0 + tw], psO[:, 0:tw],
                                OBC[:, oc:oc + 1], TIN[:, oc, tt0:tt0 + tw],
                                OP.add, OP.add)
                            C.tag(i, "v")
                        else:
                            C.inc("v")
                        C.wait("tensor", "v")
                        if dbg:
                            C.wait("sync", "v")
                            C.dma("sync", "st", lambda e, oc=oc, tt0=tt0, tw=tw:
                                  e.dma_start(
                                      out=out.rearrange(
                                          "(c p) t -> p c t", p=128)[
                                          :, oc, gsl][:, tt0:tt0 + tw]
                                      if False else
                                      d_y.rearrange("(c p) t -> p c t", p=128)[
                                          :, oc, gsl][:, tt0:tt0 + tw],
                                      in_=Y[:, oc, tt0:tt0 + tw]))
                C.wait("scalar", "v")

                def apply(C, ti):
                    sl = slice(ti * 512, (ti + 1) * 512)
                    if C.on("vector"):
                        C.eng.tensor_tensor(OB[:, :, :], Y[:, :, sl],
                                            bcN(lnt["ps_R"], NC_D), OP.mult)
                        C.eng.tensor_tensor(OB[:, :, :], OB[:, :, :],
                                            bcN(lnt["ps_M"], NC_D), OP.subtract)
                        i = C.eng.tensor_tensor(OB[:, :, :], OB[:, :, :],
                                                XIN[:, :, sl], OP.add)
                        C.tag(i, "v")
                    else:
                        C.inc("v")
                    C.wait("sync", "v")
                    C.dma("sync", "st", lambda e: e.dma_start(
                        out=outr[:, :, gsl][:, :, sl], in_=OB[:, :, :]))
                    C.wait("vector", "st")

                emit_ln_fm(C, lnt, Y, TG, apply)
            C.wait("sync", "st")

        run_program(nc, ["ld", "lw", "v", "s", "m", "fv", "st"], prog)
    return nc



DH1 = 4096
NCH = DH1 // 128


def build_l1f_a():
    nc = bass.Bass("TRN2", target_bir_lowering=False, debug=False,
                   detect_race_conditions=RACE_DETECT)
    xin = nc.dram_tensor("xin", [D, TOK], F32, kind="ExternalInput").ap()
    w1T = nc.dram_tensor("w1T", [D, DH1], F32, kind="ExternalInput").ap()
    w2T = nc.dram_tensor("w2T", [D, DH1], F32, kind="ExternalInput").ap()
    s1c = nc.dram_tensor("s1c", [128, NCH], F32, kind="ExternalInput").ap()
    s2c = nc.dram_tensor("s2c", [128, NCH], F32, kind="ExternalInput").ap()
    y1o = nc.dram_tensor("y1o", [DH1, TOK], BF16, kind="ExternalOutput").ap()
    TG = 512
    es = ExitStack()
    with es:
        XIN = es.enter_context(nc.sbuf_tensor("XIN", [128, NC_D, TG], F32))
        ZB = es.enter_context(nc.sbuf_tensor("ZB", [128, NC_D, TOK], BF16))
        WA = es.enter_context(nc.sbuf_tensor("WA", [128, 2, NCH, 128], BF16))
        SIL = es.enter_context(nc.sbuf_tensor("SIL", [128, 512], BF16))
        OUTB = es.enter_context(nc.sbuf_tensor("OUTB", [128, 2, 512], BF16))
        S1 = es.enter_context(nc.sbuf_tensor("S1", [128, NCH], F32))
        S2 = es.enter_context(nc.sbuf_tensor("S2", [128, NCH], F32))
        lnt = alloc_ln_scratch(es, nc, NC_D, TG)
        psA = es.enter_context(nc.psum_tensor("psA", [128, 512], F32))
        psG = es.enter_context(nc.psum_tensor("psG", [128, 512], F32))

        def prog(C):
            xir = xin.rearrange("(c p) t -> p c t", p=128)
            y1r = y1o.rearrange("(c p) t -> p c t", p=128)
            for dst, srcap in ((S1, s1c), (S2, s2c)):
                C.dma("gpsimd", "lw",
                      lambda e, dst=dst, srcap=srcap: e.dma_start(
                          out=dst[:, :], in_=srcap))
            emit_ln_const_init(C, lnt)
            C.wait("scalar", "lw")
            C.wait("vector", "lw")
            for g in range(TOK // TG):
                gsl = slice(g * TG, (g + 1) * TG)
                C.wait("sync", "v")
                C.wait("sync", "m")
                C.wait("sync", "s")
                C.dma("sync", "ld", lambda e: e.dma_start(
                    out=XIN[:, :, :], in_=xir[:, :, gsl]))
                C.wait("tensor", "ld")
                C.wait("scalar", "ld")
                C.wait("vector", "ld")

                def applyZ(C, ti):
                    if C.on("vector"):
                        C.eng.tensor_tensor(lnt["x2t"][:, :, :], XIN[:, :, :],
                                            bcN(lnt["ps_R"], NC_D), OP.mult)
                        i = C.eng.tensor_tensor(ZB[:, :, gsl],
                                                lnt["x2t"][:, :, :],
                                                bcN(lnt["ps_M"], NC_D),
                                                OP.subtract)
                        C.tag(i, "v")
                    else:
                        C.inc("v")
                    C.wait("tensor", "v")

                emit_ln_fm(C, lnt, XIN, TG, applyZ)

            def stream_w(wap, nk, oc, buf):
                C.wait("gpsimd", "m")
                C.dma("gpsimd", "lw", lambda e: e.dma_start(
                    out=WA[:, buf, 0:nk, :],
                    in_=wap.rearrange("(c p) o -> p c o", p=128)[
                        :, :, oc * 128:(oc + 1) * 128]))

            nst = []
            for oc in range(NCH):
                stream_w(w1T, NC_D, oc, 0)
                stream_w(w2T, NC_D, oc, 1)
                C.wait("tensor", "lw")
                for tj, (tt0, tw) in enumerate(_tiles(TOK)):
                    ob = (oc * len(_tiles(TOK)) + tj) % 2
                    if len(nst) >= 2:
                        C.wait("vector", f"st{ob}", nst[-2])
                    if C.on("tensor"):
                        for c in range(NC_D):
                            C.eng.matmul(psA[:, 0:tw], WA[:, 0, c, :],
                                         ZB[:, c, tt0:tt0 + tw], start=(c == 0),
                                         stop=(c == NC_D - 1))
                        for c in range(NC_D):
                            i = C.eng.matmul(psG[:, 0:tw], WA[:, 1, c, :],
                                             ZB[:, c, tt0:tt0 + tw],
                                             start=(c == 0),
                                             stop=(c == NC_D - 1))
                        C.tag(i, "m")
                    else:
                        C.inc("m")
                    C.wait("scalar", "m")
                    C.op("scalar", "s", lambda e, oc=oc, tw=tw: e.activation(
                        out=SIL[:, 0:tw], in_=psA[:, 0:tw], func=AF.Silu,
                        scale=S1[:, oc:oc + 1]))
                    C.wait("vector", "s")
                    if C.on("vector"):
                        C.eng.tensor_scalar(psG[:, 0:tw], psG[:, 0:tw],
                                            S2[:, oc:oc + 1], None, OP.mult)
                        i = C.eng.tensor_tensor(OUTB[:, ob, 0:tw], psG[:, 0:tw],
                                                SIL[:, 0:tw], OP.mult)
                        C.tag(i, "v")
                    else:
                        C.inc("v")
                    C.wait("tensor", "v")
                    C.wait("scalar", "v")
                    C.wait("sync", "v")
                    C.dma("sync", f"st{ob}", lambda e, oc=oc, tt0=tt0, tw=tw,
                          ob=ob: e.dma_start(out=y1r[:, oc, tt0:tt0 + tw],
                                             in_=OUTB[:, ob, 0:tw]))
                    nst.append(C.cnt[f"st{ob}"])
            C.wait("sync", "st0")
            C.wait("sync", "st1")

        run_program(nc, ["ld", "lw", "v", "s", "m", "fv", "st0", "st1"], prog)
    return nc


def build_l1f_b():
    nc = bass.Bass("TRN2", target_bir_lowering=False, debug=False,
                   detect_race_conditions=RACE_DETECT)
    y1i = nc.dram_tensor("y1i", [DH1, TOK], BF16, kind="ExternalInput").ap()
    h2T = nc.dram_tensor("h2T", [DH1, DH1], F32, kind="ExternalInput").ap()
    s4c = nc.dram_tensor("s4c", [128, NCH], F32, kind="ExternalInput").ap()
    y2o = nc.dram_tensor("y2o", [DH1, TOK], BF16, kind="ExternalOutput").ap()
    es = ExitStack()
    with es:
        HB = es.enter_context(nc.sbuf_tensor("HB", [128, NCH, TOK], BF16))
        WA = es.enter_context(nc.sbuf_tensor("WA", [128, 2, NCH, 128], BF16))
        OUTB = es.enter_context(nc.sbuf_tensor("OUTB", [128, 2, 512], BF16))
        S4 = es.enter_context(nc.sbuf_tensor("S4", [128, NCH], F32))
        psA = es.enter_context(nc.psum_tensor("psA", [128, 512], F32))

        def prog(C):
            y1r = y1i.rearrange("(c p) t -> p c t", p=128)
            y2r = y2o.rearrange("(c p) t -> p c t", p=128)
            for q4 in range(4):
                C.dma("sync", "ld", lambda e, q4=q4: e.dma_start(
                    out=HB[:, q4 * 8:(q4 + 1) * 8, :],
                    in_=y1r[:, q4 * 8:(q4 + 1) * 8, :]))
            C.dma("gpsimd", "lw", lambda e: e.dma_start(out=S4[:, :], in_=s4c))
            C.wait("scalar", "lw")
            C.wait("tensor", "ld")
            nst = []
            for oc in range(NCH):
                buf = oc % 2
                C.wait("gpsimd", "m")
                C.dma("gpsimd", "lw", lambda e, oc=oc, buf=buf: e.dma_start(
                    out=WA[:, buf, :, :],
                    in_=h2T.rearrange("(c p) o -> p c o", p=128)[
                        :, :, oc * 128:(oc + 1) * 128]))
                C.wait("tensor", "lw")
                for tj, (tt0, tw) in enumerate(_tiles(TOK)):
                    ob = (oc * len(_tiles(TOK)) + tj) % 2
                    if len(nst) >= 2:
                        C.wait("scalar", f"st{ob}", nst[-2])
                    C.wait("tensor", "s")
                    if C.on("tensor"):
                        for c in range(NCH):
                            i = C.eng.matmul(psA[:, 0:tw], WA[:, buf, c, :],
                                             HB[:, c, tt0:tt0 + tw],
                                             start=(c == 0), stop=(c == NCH - 1))
                        C.tag(i, "m")
                    else:
                        C.inc("m")
                    C.wait("scalar", "m")
                    C.op("scalar", "s", lambda e, tw=tw, oc=oc, ob=ob:
                         e.activation(out=OUTB[:, ob, 0:tw], in_=psA[:, 0:tw],
                                      func=AF.Gelu, scale=S4[:, oc:oc + 1]))
                    C.wait("sync", "s")
                    C.dma("sync", f"st{ob}", lambda e, oc=oc, tt0=tt0, tw=tw,
                          ob=ob: e.dma_start(
                              out=y2r[:, oc, tt0:tt0 + tw],
                              in_=OUTB[:, ob, 0:tw]))
                    nst.append(C.cnt[f"st{ob}"])
            C.wait("sync", "st0")
            C.wait("sync", "st1")

        run_program(nc, ["ld", "lw", "v", "s", "m", "fv", "st0", "st1"], prog)
    return nc


def build_l1f_c():
    nc = bass.Bass("TRN2", target_bir_lowering=False, debug=False,
                   detect_race_conditions=RACE_DETECT)
    y2i = nc.dram_tensor("y2i", [DH1, TOK], BF16, kind="ExternalInput").ap()
    w3T = nc.dram_tensor("w3T", [DH1, D], F32, kind="ExternalInput").ap()
    s5c = nc.dram_tensor("s5c", [128, NC_D], F32, kind="ExternalInput").ap()
    xin = nc.dram_tensor("xin", [D, TOK], F32, kind="ExternalInput").ap()
    out = nc.dram_tensor("out", [D, TOK], F32, kind="ExternalOutput").ap()
    TG = 512
    es = ExitStack()
    with es:
        HB = es.enter_context(nc.sbuf_tensor("HB", [128, NCH, TOK], BF16))
        WA = es.enter_context(nc.sbuf_tensor("WA", [128, 2, NCH, 128], BF16))
        S5 = es.enter_context(nc.sbuf_tensor("S5", [128, NC_D], F32))
        Y = es.enter_context(nc.sbuf_tensor("Y", [128, NC_D, TG], F32))
        XIN = es.enter_context(nc.sbuf_tensor("XIN", [128, NC_D, TG], F32))
        lnt = alloc_ln_scratch(es, nc, NC_D, TG)
        psA = es.enter_context(nc.psum_tensor("psA", [128, 512], F32))

        def prog(C):
            y2r = y2i.rearrange("(c p) t -> p c t", p=128)
            xir = xin.rearrange("(c p) t -> p c t", p=128)
            outr = out.rearrange("(c p) t -> p c t", p=128)
            for q4 in range(4):
                C.dma("sync", "ld", lambda e, q4=q4: e.dma_start(
                    out=HB[:, q4 * 8:(q4 + 1) * 8, :],
                    in_=y2r[:, q4 * 8:(q4 + 1) * 8, :]))
            C.dma("gpsimd", "lw", lambda e: e.dma_start(out=S5[:, :], in_=s5c))
            emit_ln_const_init(C, lnt)
            C.wait("vector", "lw")
            C.wait("tensor", "ld")
            for g in range(TOK // TG):
                gsl = slice(g * TG, (g + 1) * TG)
                C.wait("sync", "v")
                C.wait("sync", "m")
                C.wait("sync", "s")
                C.dma("sync", "ld", lambda e: e.dma_start(
                    out=XIN[:, :, :], in_=xir[:, :, gsl]))
                C.wait("vector", "ld")
                for oc in range(NC_D):
                    buf = oc % 2
                    C.wait("gpsimd", "m")
                    C.dma("gpsimd", "lw", lambda e, oc=oc, buf=buf: e.dma_start(
                        out=WA[:, buf, :, :],
                        in_=w3T.rearrange("(c p) o -> p c o", p=128)[
                            :, :, oc * 128:(oc + 1) * 128]))
                    C.wait("tensor", "lw")
                    for (tt0, tw) in _tiles(TG):
                        if C.on("tensor"):
                            for c in range(NCH):
                                i = C.eng.matmul(
                                    psA[:, 0:tw], WA[:, buf, c, :],
                                    HB[:, c, g * TG + tt0:g * TG + tt0 + tw],
                                    start=(c == 0), stop=(c == NCH - 1))
                            C.tag(i, "m")
                        else:
                            C.inc("m")
                        C.wait("vector", "m")
                        if C.on("vector"):
                            i = C.eng.tensor_scalar(Y[:, oc, tt0:tt0 + tw],
                                                    psA[:, 0:tw],
                                                    S5[:, oc:oc + 1], None,
                                                    OP.mult)
                            C.tag(i, "v")
                        else:
                            C.inc("v")
                        C.wait("tensor", "v")

                def apply(C, ti):
                    sl = slice(ti * 512, (ti + 1) * 512)
                    if C.on("vector"):
                        C.eng.tensor_tensor(Y[:, :, sl], Y[:, :, sl],
                                            bcN(lnt["ps_R"], NC_D), OP.mult)
                        C.eng.tensor_tensor(Y[:, :, sl], Y[:, :, sl],
                                            bcN(lnt["ps_M"], NC_D), OP.subtract)
                        i = C.eng.tensor_tensor(Y[:, :, sl], Y[:, :, sl],
                                                XIN[:, :, sl], OP.add)
                        C.tag(i, "v")
                    else:
                        C.inc("v")
                    C.wait("sync", "v")
                    C.dma("sync", "st", lambda e: e.dma_start(
                        out=outr[:, :, gsl][:, :, sl], in_=Y[:, :, sl]))
                    C.wait("vector", "st")

                emit_ln_fm(C, lnt, Y, TG, apply)
            C.wait("sync", "st")

        run_program(nc, ["ld", "lw", "v", "s", "m", "fv", "st"], prog)
    return nc



_PROG_CACHE = {}


def _get(name, builder):
    if name not in _PROG_CACHE:
        _PROG_CACHE[name] = builder()
    return _PROG_CACHE[name]


def _bf16(x):
    import ml_dtypes
    return np.asarray(x).astype(ml_dtypes.bfloat16)


def _tri_mask():
    return _bf16(np.tril(np.ones((128, 128), np.float32)).T)


def l0_attn_forward(x, Wq, bq, Wk, bk, Wv, bv, Wo, bo):
    import ml_dtypes
    xT = [np.ascontiguousarray(x[b].T) for b in range(B)]
    wcat = np.concatenate([
        np.concatenate([np.asarray(Wq).T, np.asarray(Wk).T, np.asarray(Wv).T],
                       axis=1),
        np.concatenate([bq, bk, bv])[None, :]], axis=0)
    wcat = np.ascontiguousarray(wcat, np.float32)

    nc1 = _get("l0a_proj", build_l0a_proj)
    in_maps = []
    for c in range(NCORES):
        b, h = divmod(c, 2)
        in_maps.append({
            "xo": np.ascontiguousarray(xT[b][:, h * TOK:(h + 1) * TOK]),
            "wc": wcat,
        })
    res = run_bass_kernel_spmd(nc1, in_maps, core_ids=list(range(NCORES)))
    qkv = [np.concatenate([res.results[2 * b]["qkv"], res.results[2 * b + 1]["qkv"]],
                          axis=0) for b in range(B)]

    nc2 = _get("l0a_attn", build_l0a_attn)
    tri = _tri_mask()
    in_maps = []
    for c in range(NCORES):
        b, hh = divmod(c, 2)
        hs = slice(hh * 8 * DH, (hh + 1) * 8 * DH)
        q = qkv[b][:, 0 * D:1 * D][:, hs].reshape(S, 8, DH)
        k = qkv[b][:, 1 * D:2 * D][:, hs].reshape(S, 8, DH)
        v = qkv[b][:, 2 * D:3 * D][:, hs].reshape(S, 8, DH)
        ones = np.ones((8, S, 1), ml_dtypes.bfloat16)
        in_maps.append({
            "qf": np.ascontiguousarray(q.transpose(1, 2, 0)),
            "kf": np.ascontiguousarray(k.transpose(1, 2, 0)),
            "kt": np.ascontiguousarray(k.transpose(1, 0, 2)),
            "vt": np.ascontiguousarray(
                np.concatenate([v.transpose(1, 0, 2), ones], axis=2)),
            "tri": tri})
    res = run_bass_kernel_spmd(nc2, in_maps, core_ids=list(range(NCORES)))
    at = []
    for b in range(B):
        ab = np.concatenate([res.results[2 * b]["at"], res.results[2 * b + 1]["at"]],
                            axis=0)
        at.append(np.ascontiguousarray(ab.transpose(1, 0, 2).reshape(S, D)))

    nc3 = _get("l0a_out", build_l0a_out)
    woT = np.ascontiguousarray(np.asarray(Wo).T, np.float32)
    bo_r = np.ascontiguousarray(np.asarray(bo).reshape(NC_D, 128).T, np.float32)
    in_maps = []
    for c in range(NCORES):
        b, h = divmod(c, 2)
        in_maps.append({
            "af": np.ascontiguousarray(at[b][h * TOK:(h + 1) * TOK, :].T),
            "wo": woT,
            "bo": bo_r,
            "xo": np.ascontiguousarray(xT[b][:, h * TOK:(h + 1) * TOK]),
        })
    res = run_bass_kernel_spmd(nc3, in_maps, core_ids=list(range(NCORES)))
    out = np.empty((B, S, D), np.float32)
    for c in range(NCORES):
        b, h = divmod(c, 2)
        out[b, h * TOK:(h + 1) * TOK, :] = res.results[c]["out"].T
    return out


INNER_W = 256


def _l0f_tables(o0):
    d = np.arange(128) % 64
    half = (d % 32).astype(np.float64)
    inv_freq = 1.0 / (10000.0 ** (2 * half / 64.0))
    base_scale = (2 * half + 0.4 * 64) / (1.4 * 64)
    g = np.arange(o0 - 384, o0 + TOK, dtype=np.float64)
    freqs = g[None, :] * inv_freq[:, None]
    power = (g - S // 2) / (INNER_W // 2)
    scale = base_scale[:, None] ** power[None, :]
    cos, sin = np.cos(freqs), np.sin(freqs)
    return (cos * scale).astype(np.float32), (sin * scale).astype(np.float32), \
           (cos / scale).astype(np.float32), (sin / scale).astype(np.float32)


def _rho_sigma():
    d = np.arange(128) % 64
    rho = np.where(d < 32, np.arange(128) + 32, np.arange(128) - 32)
    sigma = np.where(d < 32, -1.0, 1.0)
    return rho, sigma


def _l0f_attn_consts():
    rho, sigma = _rho_sigma()
    R = np.zeros((128, 128), np.float32)
    for op in range(128):
        R[rho[op], op] = sigma[op]
    blk = np.zeros((128, 128), np.float32)
    blk[:64, :64] = 1.0
    blk[64:, 64:] = 1.0
    idm = np.eye(128, dtype=np.float32)
    return R, blk, idm


def _l0f_masks(h):
    NEG = -1e38
    il = np.arange(128)[:, None]
    jl = np.arange(384)[None, :] % 128
    blkid = np.arange(384)[None, :] // 128
    interior = np.zeros((128, 384), np.float32)
    interior[(blkid == 0) & (jl < il)] = NEG
    interior[(blkid == 2) & (jl > il)] = NEG
    m3 = interior.copy()
    m4 = interior.copy()
    if h == 0:
        m3[blkid.repeat(128, 0) <= 1] = NEG
        m4[blkid.repeat(128, 0) == 0] = NEG
    return np.stack([m3, m4, interior])


def l0_ffn_forward(x, Wqkv, q_scale, k_scale, Wo, ln_g, ln_b, W1, b1, W2, b2, nlayers=6):
    nc_a = _get("l0f_attn", build_l0f_attn)
    nc_f = _get("l0f_ffn", build_l0f_ffn)
    rho, _ = _rho_sigma()
    R, blkm, idm = _l0f_attn_consts()
    xT = [np.ascontiguousarray(x[b].T, np.float32) for b in range(B)]
    DIM_INNER = 2730
    for li in range(nlayers):
        qs = np.asarray(q_scale[li], np.float32)
        ks = np.asarray(k_scale[li], np.float32)
        d64 = np.arange(128) % 64
        qs_c = qs[d64]
        qs_r = qs[d64[rho]]
        ks_c = ks[d64]
        ks_r = ks[d64[rho]]
        wq = np.ascontiguousarray(Wqkv[li], np.float32)
        woL = np.ascontiguousarray(Wo[li], np.float32)
        in_maps = []
        for c in range(NCORES):
            b, h = divmod(c, 2)
            o0 = h * TOK
            ext = np.zeros((D, EXT), np.float32)
            lo = max(0, o0 - 384)
            ext[:, 384 - (o0 - lo):384] = xT[b][:, lo:o0]
            ext[:, 384:] = xT[b][:, o0:o0 + TOK]
            c1, s1, c2, s2 = _l0f_tables(o0)
            in_maps.append({
                "xin": ext, "wqkv": wq, "wo": woL,
                "tq1": np.ascontiguousarray(8.0 * c1 * qs_c[:, None]),
                "tq2": np.ascontiguousarray(8.0 * s1 * qs_r[:, None]),
                "tk1": np.ascontiguousarray(c2 * ks_c[:, None]),
                "tk2": np.ascontiguousarray(s2 * ks_r[:, None]),
                "rotm": R, "blk": blkm, "idm": idm,
                "mneg": _l0f_masks(h)})
        res = run_bass_kernel_spmd(nc_a, in_maps, core_ids=list(range(NCORES)))
        for c in range(NCORES):
            b, h = divmod(c, 2)
            xT[b][:, h * TOK:(h + 1) * TOK] = res.results[c]["xout"]
        W1a = np.zeros((D, NHC * 128), np.float32)
        W1g = np.zeros((D, NHC * 128), np.float32)
        W1a[:, :DIM_INNER] = W1[li][:, :DIM_INNER]
        W1g[:, :DIM_INNER] = W1[li][:, DIM_INNER:]
        w1iL = np.zeros((D, NHC * 256), np.float32)
        for hc in range(NHC):
            w1iL[:, hc * 256:hc * 256 + 128] = W1a[:, hc * 128:(hc + 1) * 128]
            w1iL[:, hc * 256 + 128:hc * 256 + 256] = W1g[:, hc * 128:(hc + 1) * 128]
        b1aP = np.zeros(NHC * 128, np.float32)
        b1gP = np.zeros(NHC * 128, np.float32)
        b1aP[:DIM_INNER] = b1[li][:DIM_INNER]
        b1gP[:DIM_INNER] = b1[li][DIM_INNER:]
        b1cL = np.zeros((128, NHC, 2), np.float32)
        for hc in range(NHC):
            b1cL[:, hc, 0] = b1aP[hc * 128:(hc + 1) * 128]
            b1cL[:, hc, 1] = b1gP[hc * 128:(hc + 1) * 128]
        b1cL = np.ascontiguousarray(b1cL.reshape(128, NHC * 2))
        w2P = np.zeros((NHC * 128, D), np.float32)
        w2P[:DIM_INNER] = W2[li]
        b2cL = np.ascontiguousarray(np.asarray(b2[li], np.float32).reshape(NC_D, 128).T)
        glnL = np.ascontiguousarray(np.asarray(ln_g[li], np.float32).reshape(NC_D, 128).T)
        blnL = np.ascontiguousarray(np.asarray(ln_b[li], np.float32).reshape(NC_D, 128).T)
        in_maps = []
        for c in range(NCORES):
            b, h = divmod(c, 2)
            o0 = h * TOK
            ext = np.zeros((D, TOK + 1), np.float32)
            if o0 > 0:
                ext[:, 0] = xT[b][:, o0 - 1]
            ext[:, 1:] = xT[b][:, o0:o0 + TOK]
            in_maps.append({
                "xin": ext, "w1i": w1iL, "b1c": b1cL, "w2": w2P, "b2c": b2cL,
                "gln": glnL, "bln": blnL})
        res = run_bass_kernel_spmd(nc_f, in_maps, core_ids=list(range(NCORES)))
        for c in range(NCORES):
            b, h = divmod(c, 2)
            xT[b][:, h * TOK:(h + 1) * TOK] = res.results[c]["xout"]
    out = np.stack([xT[b].T for b in range(B)])
    return np.ascontiguousarray(out)


def sandwich_pre(xT):
    nc = _get("pre_ln", build_pre_ln)
    in_maps = []
    for c in range(NCORES):
        b, h = divmod(c, 2)
        in_maps.append({"xo": np.ascontiguousarray(
            xT[b][:, h * TOK:(h + 1) * TOK])})
    res = run_bass_kernel_spmd(nc, in_maps, core_ids=list(range(NCORES)))
    return [np.concatenate([res.results[2 * b]["out"],
                            res.results[2 * b + 1]["out"]], axis=1)
            for b in range(B)]


def sandwich_post(yT, xT):
    nc = _get("post_ln", build_post_ln)
    in_maps = []
    for c in range(NCORES):
        b, h = divmod(c, 2)
        sl = slice(h * TOK, (h + 1) * TOK)
        in_maps.append({"yo": np.ascontiguousarray(yT[b][:, sl]),
                        "xo": np.ascontiguousarray(xT[b][:, sl])})
    res = run_bass_kernel_spmd(nc, in_maps, core_ids=list(range(NCORES)))
    return [np.concatenate([res.results[2 * b]["out"],
                            res.results[2 * b + 1]["out"]], axis=1)
            for b in range(B)]


def _cols(v, n):
    return np.ascontiguousarray(np.asarray(v, np.float32).reshape(n, 128).T)


def l1_attn_forward(xT, inp):
    nc_a = _get("l1a_A", build_l1a_A)
    nc_b = _get("l1a_B", build_l1a_B)
    nsc = _cols(inp["L1_attn_norm_scale"], NC_D)
    inwT = np.ascontiguousarray(np.asarray(inp["L1_attn_in_w"], np.float32).T)
    inb = _cols(inp["L1_attn_in_b"], 16)
    dww = np.ascontiguousarray(np.asarray(inp["L1_attn_dw_w"], np.float32)
                               .reshape(NC_D, 128, 3).transpose(1, 0, 2)
                               .reshape(128, NC_D * 3))
    dwb = _cols(inp["L1_attn_dw_b"], NC_D)
    pwwT = np.ascontiguousarray(np.asarray(inp["L1_attn_pw_w"], np.float32).T)
    pwb = _cols(inp["L1_attn_pw_b"], NC_D)
    winf = np.ascontiguousarray(np.asarray(inp["L1_attn_f_win"], np.float32).T)
    winb = np.ascontiguousarray(np.asarray(inp["L1_attn_b_win"], np.float32).T)
    sspar = np.stack([
        np.asarray(inp["L1_attn_f_logA"], np.float32),
        np.asarray(inp["L1_attn_f_B"], np.float32),
        np.full(64, float(inp["L1_attn_f_logdt"]), np.float32),
        np.asarray(inp["L1_attn_b_logA"], np.float32),
        np.asarray(inp["L1_attn_b_B"], np.float32),
        np.full(64, float(inp["L1_attn_b_logdt"]), np.float32)], axis=1)
    sspar = np.ascontiguousarray(sspar)
    in_maps = []
    for c in range(NCORES):
        b, h = divmod(c, 2)
        o0 = h * TOK
        ext = np.zeros((D, TOK + 2), np.float32)
        if o0 > 0:
            ext[:, 0] = xT[b][:, o0 - 1]
        ext[:, 1:TOK + 1] = xT[b][:, o0:o0 + TOK]
        if o0 + TOK < S:
            ext[:, TOK + 1] = xT[b][:, o0 + TOK]
        edge = np.ones((128, 2), np.float32)
        if h == 0:
            edge[:, 0] = 0.0
        if h == 1:
            edge[:, 1] = 0.0
        in_maps.append({"xo": ext, "nsc": nsc, "inw": inwT, "inb": inb,
                        "dww": dww, "dwb": dwb, "pww": pwwT, "pwb": pwb,
                        "winf": winf, "winb": winb, "sspar": sspar,
                        "edge": np.ascontiguousarray(edge)})
    res = run_bass_kernel_spmd(nc_a, in_maps, core_ids=list(range(NCORES)))
    rA = [res.results[c] for c in range(NCORES)]

    cpar = np.ascontiguousarray(np.stack(
        [np.asarray(inp["L1_attn_f_C"], np.float32),
         np.asarray(inp["L1_attn_b_C"], np.float32)], axis=1))
    woutT = np.ascontiguousarray(np.concatenate(
        [np.asarray(inp["L1_attn_f_wout"], np.float32).T,
         np.asarray(inp["L1_attn_b_wout"], np.float32).T], axis=1))
    outwT = np.ascontiguousarray(np.asarray(inp["L1_attn_out_w"], np.float32).T)
    outb = _cols(inp["L1_attn_out_b"], NC_D)
    in_maps = []
    for c in range(NCORES):
        b, h = divmod(c, 2)
        carry = np.zeros((64, 2), np.float32)
        if h == 1:
            carry[:, 0] = rA[2 * b]["hf_out"][:, TOK - 1]
        if h == 0:
            carry[:, 1] = rA[2 * b + 1]["hf_out"][:, TOK]
        in_maps.append({
            "hfin": rA[c]["hf_out"], "carry": np.ascontiguousarray(carry),
            "sspar": sspar, "cpar": cpar, "woutT": woutT, "outwT": outwT,
            "outb": outb, "gin": rA[c]["g_out"], "tin": rA[c]["t_out"],
            "xin": np.ascontiguousarray(xT[b][:, h * TOK:(h + 1) * TOK])})
    res = run_bass_kernel_spmd(nc_b, in_maps, core_ids=list(range(NCORES)))
    return [np.concatenate([res.results[2 * b]["out"],
                            res.results[2 * b + 1]["out"]], axis=1)
            for b in range(B)]


def l1_ffn_forward(xT, inp):
    nc_a = _get("l1f_a", build_l1f_a)
    nc_b = _get("l1f_b", build_l1f_b)
    nc_c = _get("l1f_c", build_l1f_c)
    NCH_ = NCH
    w1T = np.ascontiguousarray(np.asarray(inp["L1_ffn_w1_w"], np.float32).T)
    w2T = np.ascontiguousarray(np.asarray(inp["L1_ffn_w2_w"], np.float32).T)
    h1T = np.ascontiguousarray(np.asarray(inp["L1_ffn_h1_w"], np.float32).T)
    h2T = np.ascontiguousarray(np.asarray(inp["L1_ffn_h2_w"], np.float32).T)
    w3T = np.ascontiguousarray(np.asarray(inp["L1_ffn_w3_w"], np.float32).T)
    s1 = _cols(inp["L1_ffn_w1_s"], NCH_)
    s2 = _cols(inp["L1_ffn_w2_s"], NCH_)
    s3 = _cols(inp["L1_ffn_h1_s"], NCH_)
    s4 = _cols(inp["L1_ffn_h2_s"], NCH_)
    s5 = _cols(inp["L1_ffn_w3_s"], NC_D)
    xs = [np.ascontiguousarray(xT[c // 2][:, (c % 2) * TOK:(c % 2 + 1) * TOK])
          for c in range(NCORES)]
    in_maps = [{"xin": xs[c], "w1T": w1T, "w2T": w2T, "s1c": s1, "s2c": s2}
               for c in range(NCORES)]
    res = run_bass_kernel_spmd(nc_a, in_maps, core_ids=list(range(NCORES)))
    y1 = [res.results[c]["y1o"] for c in range(NCORES)]
    in_maps = [{"y1i": y1[c], "h2T": h1T if False else h1T, "s4c": s3}
               for c in range(NCORES)]
    res = run_bass_kernel_spmd(nc_b, in_maps, core_ids=list(range(NCORES)))
    y2 = [res.results[c]["y2o"] for c in range(NCORES)]
    in_maps = [{"y1i": y2[c], "h2T": h2T, "s4c": s4} for c in range(NCORES)]
    res = run_bass_kernel_spmd(nc_b, in_maps, core_ids=list(range(NCORES)))
    y3 = [res.results[c]["y2o"] for c in range(NCORES)]
    in_maps = [{"y2i": y3[c], "w3T": w3T, "s5c": s5, "xin": xs[c]}
               for c in range(NCORES)]
    res = run_bass_kernel_spmd(nc_c, in_maps, core_ids=list(range(NCORES)))
    return [np.concatenate([res.results[2 * b]["out"],
                            res.results[2 * b + 1]["out"]], axis=1)
            for b in range(B)]


def kernel(**inputs):
    x = np.asarray(inputs["L0_attn_x"], np.float32)
    x = l0_attn_forward(x, inputs["L0_attn_Wq"], inputs["L0_attn_bq"],
                        inputs["L0_attn_Wk"], inputs["L0_attn_bk"],
                        inputs["L0_attn_Wv"], inputs["L0_attn_bv"],
                        inputs["L0_attn_Wo"], inputs["L0_attn_bo"])
    xT = [np.ascontiguousarray(x[b].T) for b in range(B)]
    zT = sandwich_pre(xT)
    z = np.stack([zT[b].T for b in range(B)])
    z = l0_ffn_forward(z, inputs["L0_ffn_Wqkv"], inputs["L0_ffn_q_scale"],
                       inputs["L0_ffn_k_scale"], inputs["L0_ffn_Wo"],
                       inputs["L0_ffn_ln_g"], inputs["L0_ffn_ln_b"],
                       inputs["L0_ffn_W1"], inputs["L0_ffn_b1"],
                       inputs["L0_ffn_W2"], inputs["L0_ffn_b2"])
    zT = [np.ascontiguousarray(z[b].T) for b in range(B)]
    xT = sandwich_post(zT, xT)
    xT = l1_attn_forward(xT, inputs)
    xT = l1_ffn_forward(xT, inputs)
    out = np.stack([xT[b].T for b in range(B)])
    return np.ascontiguousarray(out)


# version 04 (final)
# speedup vs baseline: 1.6561x; 1.4248x over previous
import numpy as np
from contextlib import ExitStack

import concourse.bass as bass
import concourse.mybir as mybir
from concourse.bass_utils import run_bass_kernel_spmd
from concourse.alu_op_type import AluOpType

F32 = mybir.dt.float32
BF16 = mybir.dt.bfloat16
AF = mybir.ActivationFunctionType
OP = AluOpType

NCORES = 8
B, S, D = 4, 4096, 1024
TOK = 2048
NC_D = D // 128
DH = 64
EPS_LN = 1e-5

ENGS = ("tensor", "vector", "scalar", "gpsimd", "sync")
RACE_DETECT = False


class Ctx:

    def __init__(self, E, eng, sems):
        self.E = E
        self.eng = eng
        self.sems = sems
        self.cnt = {k: 0 for k in sems}

    def on(self, E):
        return self.E == E

    def inc(self, name, n=1):
        self.cnt[name] += n

    def tag(self, inst, name, n=1):
        if inst is not None:
            inst.then_inc(self.sems[name], n)
        self.cnt[name] += n

    def wait(self, E, name, val=None):
        if self.E == E:
            self.eng.wait_ge(self.sems[name], self.cnt[name] if val is None else val)

    def dma(self, E, sem, fn, n=16):
        if self.E == E:
            fn(self.eng).then_inc(self.sems[sem], n)
        self.cnt[sem] += n

    def op(self, E, sem, fn, n=1):
        if self.E == E:
            fn(self.eng).then_inc(self.sems[sem], n)
        self.cnt[sem] += n

    def run(self, E, fn):
        if self.E == E:
            fn(self.eng)

    def tag_fenced(self, E, inst, name, n=1):
        self.fence(E, inst)
        if self.E == E:
            i = self.eng.engine_nop()
            i.then_inc(self.sems[name], n)
        self.cnt[name] += n

    def fence(self, E, inst):
        name = "f" + E[0]
        if inst is not None:
            inst.then_inc(self.sems[name], 1)
        self.cnt[name] += 1
        if self.E == E:
            self.eng.wait_ge(self.sems[name], self.cnt[name])


def run_program(nc, names, prog):
    with ExitStack() as es:
        sems = {n: es.enter_context(nc.semaphore(n)) for n in names}
        block = es.enter_context(nc.Block())
        for E in ENGS:
            def mk(E):
                def _(eng):
                    prog(Ctx(E, eng, sems))
                return _
            getattr(block, E)(mk(E))



def alloc_ln_scratch(es, nc, NC, T, tag=""):
    t = {}
    t["x2t"] = es.enter_context(nc.sbuf_tensor(f"ln_x2t{tag}", [128, NC, 512], F32))
    t["srow"] = es.enter_context(nc.sbuf_tensor(f"ln_srow{tag}", [1, T], F32))
    t["qrow"] = es.enter_context(nc.sbuf_tensor(f"ln_qrow{tag}", [1, T], F32))
    t["rrow"] = es.enter_context(nc.sbuf_tensor(f"ln_rrow{tag}", [1, T], F32))
    t["mrow"] = es.enter_context(nc.sbuf_tensor(f"ln_mrow{tag}", [1, T], F32))
    t["ps_s"] = es.enter_context(nc.psum_tensor(f"ln_ps_s{tag}", [1, 512], F32))
    t["ps_q"] = es.enter_context(nc.psum_tensor(f"ln_ps_q{tag}", [1, 512], F32))
    t["ps_R"] = es.enter_context(nc.psum_tensor(f"ln_ps_R{tag}", [128, 512], F32))
    t["ps_M"] = es.enter_context(nc.psum_tensor(f"ln_ps_M{tag}", [128, 512], F32))
    t["onesf"] = es.enter_context(nc.sbuf_tensor(f"ln_onesf{tag}", [128, 128], F32))
    t["onesb"] = es.enter_context(nc.sbuf_tensor(f"ln_onesb{tag}", [1, 128], BF16))
    t["epsr"] = es.enter_context(nc.sbuf_tensor(f"ln_epsr{tag}", [1, 1], F32))
    return t


def emit_ln_const_init(C, t, eps=EPS_LN):
    if C.on("vector"):
        C.eng.memset(t["onesf"][:, :], 1.0)
        C.eng.memset(t["onesb"][:, :], 1.0)
        i = C.eng.memset(t["epsr"][:, :], eps)
        C.tag(i, "v")
    else:
        C.inc("v")


def emit_ln_fm(C, t, X, T, apply, rms=False, eps=EPS_LN, shift=None):
    NC = X.shape[1]
    NT = T // 512
    C.wait("tensor", "v")
    C.wait("scalar", "v")
    for ti in range(NT):
        sl = slice(ti * 512, (ti + 1) * 512)
        C.wait("scalar", "ld")
        if C.on("scalar"):
            i = C.eng.activation(out=t["x2t"][:, :, :], in_=X[:, :, sl],
                                 func=AF.Square)
            C.tag(i, "s")
        else:
            C.inc("s")
        C.wait("tensor", "s")
        C.wait("tensor", "ld")
        if C.on("tensor"):
            if not rms:
                for c in range(NC):
                    C.eng.matmul(t["ps_s"][:, :], t["onesf"][:, 0:1], X[:, c, sl],
                                 start=(c == 0), stop=(c == NC - 1))
            for c in range(NC):
                i = C.eng.matmul(t["ps_q"][:, :], t["onesf"][:, 0:1],
                                 t["x2t"][:, c, :], start=(c == 0),
                                 stop=(c == NC - 1))
            C.tag(i, "m")
        else:
            C.inc("m")
        C.wait("vector", "m")
        if C.on("vector"):
            if not rms:
                C.eng.tensor_copy(t["srow"][:, sl], t["ps_s"][:, :])
            i = C.eng.tensor_copy(t["qrow"][:, sl], t["ps_q"][:, :])
        else:
            i = None
        C.tag_fenced("vector", i, "v")
        C.wait("tensor", "v")
        C.wait("scalar", "v")
    Dn = float(X.shape[0] * NC)
    if not rms:
        if C.on("vector"):
            C.eng.tensor_scalar(t["srow"][:, 0:T], t["srow"][:, 0:T], 1.0 / Dn,
                                None, OP.mult)
            C.eng.tensor_tensor(t["mrow"][:, 0:T], t["srow"][:, 0:T],
                                t["srow"][:, 0:T], OP.mult)
            i = C.eng.scalar_tensor_tensor(t["qrow"][:, 0:T], t["qrow"][:, 0:T],
                                           1.0 / Dn, t["mrow"][:, 0:T],
                                           OP.mult, OP.subtract)
            C.tag(i, "v")
        else:
            C.inc("v")
        C.wait("scalar", "v")
        if C.on("scalar"):
            C.eng.activation(out=t["rrow"][:, 0:T], in_=t["qrow"][:, 0:T],
                             func=AF.Ln, bias=t["epsr"][0:1, 0:1])
            i = C.eng.activation(out=t["rrow"][:, 0:T], in_=t["rrow"][:, 0:T],
                                 func=AF.Exp, scale=-0.5)
            C.tag(i, "s")
        else:
            C.inc("s")
        C.wait("vector", "s")
        if C.on("vector"):
            i = C.eng.tensor_tensor(t["mrow"][:, 0:T], t["srow"][:, 0:T],
                                    t["rrow"][:, 0:T], OP.mult)
            C.tag(i, "v")
        else:
            C.inc("v")
    else:
        if C.on("scalar"):
            C.eng.activation(out=t["rrow"][:, 0:T], in_=t["qrow"][:, 0:T],
                             func=AF.Ln, scale=1.0 / Dn, bias=t["epsr"][0:1, 0:1])
            i = C.eng.activation(out=t["rrow"][:, 0:T], in_=t["rrow"][:, 0:T],
                                 func=AF.Exp, scale=-0.5)
            C.tag(i, "s")
        else:
            C.inc("s")
    C.wait("tensor", "v")
    C.wait("tensor", "s")
    for ti in range(NT):
        sl = slice(ti * 512, (ti + 1) * 512)
        if C.on("tensor"):
            i = C.eng.matmul(t["ps_R"][:, :], t["onesf"][0:1, 0:128],
                             t["rrow"][:, sl], start=True, stop=True)
            if not rms:
                i = C.eng.matmul(t["ps_M"][:, :], t["onesf"][0:1, 0:128],
                                 t["mrow"][:, sl], start=True, stop=True)
            C.tag(i, "m")
        else:
            C.inc("m")
        C.wait("vector", "m")
        apply(C, ti)
        C.wait("tensor", "v")


def bcN(ps, NC):
    return ps[:, :].rearrange("p (o t) -> p o t", o=1).broadcast_to((128, NC, 512))



def build_l0a_proj():
    nc = bass.Bass("TRN2", target_bir_lowering=False, debug=False,
                   detect_race_conditions=RACE_DETECT)
    xo = nc.dram_tensor("xo", [D, TOK], F32, kind="ExternalInput").ap()
    wc = nc.dram_tensor("wc", [D + 1, 3 * D], F32, kind="ExternalInput").ap()
    qkv = nc.dram_tensor("qkv", [TOK, 3 * D], BF16, kind="ExternalOutput").ap()
    es = ExitStack()
    with es:
        X = es.enter_context(nc.sbuf_tensor("X", [128, NC_D, TOK], F32))
        zb = es.enter_context(nc.sbuf_tensor("zb", [128, NC_D, TOK], BF16))
        W = es.enter_context(nc.sbuf_tensor("W", [128, NC_D, 3 * D // 2], BF16))
        wb = es.enter_context(nc.sbuf_tensor("wb", [1, 3 * D], BF16))
        ob = es.enter_context(nc.sbuf_tensor("ob", [128, 2, 512], BF16))
        ps = [es.enter_context(nc.psum_tensor(f"ps{i}", [128, 512], F32))
              for i in range(2)]
        lnt = alloc_ln_scratch(es, nc, NC_D, TOK)

        def prog(C):
            if C.on("sync"):
                i = C.eng.dma_start(out=X[:, :, :],
                                    in_=xo.rearrange("(c p) t -> p c t", p=128))
                C.tag(i, "ld", 16)
            else:
                C.inc("ld", 16)
            if C.on("gpsimd"):
                i = C.eng.dma_start(
                    out=W[:, :, :],
                    in_=wc[0:D, 0:3 * D // 2].rearrange("(c p) o -> p c o", p=128))
                C.tag(i, "lw", 16)
                i = C.eng.dma_start(out=wb[:, :], in_=wc[D:D + 1, :])
                C.tag(i, "lw", 16)
            else:
                C.inc("lw", 32)
            emit_ln_const_init(C, lnt)

            def apply(C, ti):
                sl = slice(ti * 512, (ti + 1) * 512)
                if C.on("vector"):
                    Rb = bcN(lnt["ps_R"], NC_D)
                    Mb = bcN(lnt["ps_M"], NC_D)
                    C.eng.tensor_tensor(lnt["x2t"][:, :, :], X[:, :, sl], Rb, OP.mult)
                    i = C.eng.tensor_tensor(zb[:, :, sl], lnt["x2t"][:, :, :], Mb,
                                            OP.subtract)
                    C.tag(i, "v")
                else:
                    C.inc("v")

            emit_ln_fm(C, lnt, X, TOK, apply)

            it = 0
            for og in range(2):
                if og == 1:
                    C.wait("gpsimd", "m")
                    if C.on("gpsimd"):
                        i = C.eng.dma_start(
                            out=W[:, :, :],
                            in_=wc[0:D, 3 * D // 2:3 * D].rearrange(
                                "(c p) o -> p c o", p=128))
                        C.tag(i, "lw", 16)
                    else:
                        C.inc("lw", 16)
                C.wait("tensor", "lw")
                C.wait("tensor", "v")
                for tc in range(TOK // 128):
                    for oh in range(3):
                        o = og * 3 + oh
                        bank = it % 2
                        osl = slice(o * 512, (o + 1) * 512)
                        wsl = slice(oh * 512, (oh + 1) * 512)
                        if it >= 2:
                            C.wait("tensor", f"st{bank}")
                        if C.on("tensor"):
                            for c in range(NC_D):
                                C.eng.matmul(ps[bank][:, :],
                                             zb[:, c, tc * 128:(tc + 1) * 128],
                                             W[:, c, wsl], start=(c == 0),
                                             stop=False)
                            i = C.eng.matmul(ps[bank][:, :], lnt["onesb"][:, 0:128],
                                             wb[:, osl], start=False, stop=True)
                            C.tag(i, "m")
                        else:
                            C.inc("m")
                        C.wait("scalar", "m")
                        if C.on("scalar"):
                            fn = AF.Exp if o < 4 else AF.Identity
                            i = C.eng.activation(out=ob[:, bank, :],
                                                 in_=ps[bank][:, :], func=fn)
                            C.tag(i, "s")
                        else:
                            C.inc("s")
                        C.wait("sync", "s")
                        if C.on("sync"):
                            i = C.eng.dma_start(out=qkv[tc * 128:(tc + 1) * 128, osl],
                                                in_=ob[:, bank, :])
                            C.tag(i, f"st{bank}", 16)
                        else:
                            C.inc(f"st{bank}", 16)
                        it += 1
            C.wait("sync", "st0")
            C.wait("sync", "st1")

        run_program(nc, ["ld", "lw", "v", "s", "m", "fv", "st0", "st1"], prog)
    return nc



def build_l0a_attn():
    nc = bass.Bass("TRN2", target_bir_lowering=False, debug=False,
                   detect_race_conditions=RACE_DETECT)
    NH = 8
    CH = S // 128
    qf = nc.dram_tensor("qf", [NH, DH, S], BF16, kind="ExternalInput").ap()
    kf = nc.dram_tensor("kf", [NH, DH, S], BF16, kind="ExternalInput").ap()
    kt = nc.dram_tensor("kt", [NH, S, DH], BF16, kind="ExternalInput").ap()
    vt = nc.dram_tensor("vt", [NH, S, DH + 1], BF16, kind="ExternalInput").ap()
    tri = nc.dram_tensor("tri", [128, 128], BF16, kind="ExternalInput").ap()
    at = nc.dram_tensor("at", [NH, S, DH], BF16, kind="ExternalOutput").ap()
    es = ExitStack()
    with es:
        Q = es.enter_context(nc.sbuf_tensor("Q", [64, 2, S], BF16))
        K = es.enter_context(nc.sbuf_tensor("K", [64, 2, S], BF16))
        KT = es.enter_context(nc.sbuf_tensor("KT", [128, 2, CH, DH], BF16))
        VT = es.enter_context(nc.sbuf_tensor("VT", [128, 2, CH, DH + 1], BF16))
        AT = es.enter_context(nc.sbuf_tensor("AT", [128, 2, CH, DH], BF16))
        TRI = es.enter_context(nc.sbuf_tensor("TRI", [128, 128], BF16))
        M = es.enter_context(nc.sbuf_tensor("M", [128, 128], BF16))
        KV = es.enter_context(nc.sbuf_tensor("KV", [64, 2, DH + 1], BF16))
        KVA = es.enter_context(nc.sbuf_tensor("KVA", [64, DH + 1], F32))
        dn = es.enter_context(nc.sbuf_tensor("dn", [128, 1], F32))
        psA = es.enter_context(nc.psum_tensor("psA", [128, 128], F32))
        psN = es.enter_context(nc.psum_tensor("psN", [128, DH + 1], F32))
        psK = es.enter_context(nc.psum_tensor("psK", [64, DH + 1], F32))

        def prog(C):
            if C.on("sync"):
                i = C.eng.dma_start(out=TRI[:, :], in_=tri[:, :])
                C.tag(i, "lt", 16)
            else:
                C.inc("lt", 16)

            def load_head(h, buf):
                sem = f"ld{buf}"
                for (dst, src) in ((Q[:, buf, :], qf[h, :, :]),
                                   (K[:, buf, :], kf[h, :, :]),
                                   (KT[:, buf, :, :],
                                    kt[h, :, :].rearrange("(c p) d -> p c d", p=128)),
                                   (VT[:, buf, :, :],
                                    vt[h, :, :].rearrange("(c p) d -> p c d", p=128))):
                    if C.on("sync"):
                        i = C.eng.dma_start(out=dst, in_=src)
                        C.tag(i, sem, 16)
                    else:
                        C.inc(sem, 16)

            load_head(0, 0)
            C.wait("tensor", "lt")
            for h in range(NH):
                buf = h % 2
                if h + 1 < NH:
                    load_head(h + 1, 1 - buf)
                C.wait("tensor", f"ld{buf}", 64 * (h // 2 + 1))
                if h >= 2:
                    C.wait("tensor", f"st{buf}")
                for ci in range(CH):
                    csl = slice(ci * 128, (ci + 1) * 128)
                    if C.on("tensor"):
                        C.eng.matmul(psA[:, :], K[:, buf, csl], Q[:, buf, csl],
                                     start=True, stop=True)
                        i = C.eng.matmul(psK[:, :], KT[:, buf, ci, :],
                                         VT[:, buf, ci, :], start=True, stop=True)
                        C.tag(i, "m")
                    else:
                        C.inc("m")
                    C.wait("vector", "m")
                    if C.on("vector"):
                        C.eng.tensor_tensor(M[:, :], psA[:, :], TRI[:, :], OP.mult)
                        if ci == 0:
                            i = C.eng.tensor_copy(KVA[:, :], psK[:, :])
                        else:
                            i = C.eng.tensor_tensor(KVA[:, :], KVA[:, :], psK[:, :],
                                                    OP.add)
                    else:
                        i = None
                    C.fence("vector", i)
                    if C.on("vector"):
                        i = C.eng.tensor_copy(KV[:, ci % 2, :], KVA[:, :])
                        C.tag(i, "v")
                    else:
                        C.inc("v")
                    C.wait("tensor", "v")
                    if C.on("tensor"):
                        i = C.eng.matmul(psN[:, :], M[:, :], VT[:, buf, ci, :],
                                         start=True, stop=(ci == 0))
                        if ci > 0:
                            i = C.eng.matmul(psN[:, :], Q[:, buf, csl],
                                             KV[:, (ci - 1) % 2, :], start=False,
                                             stop=True)
                        C.tag(i, "m")
                    else:
                        C.inc("m")
                    C.wait("vector", "m")
                    if C.on("vector"):
                        i = C.eng.reciprocal(dn[:, :], psN[:, DH:DH + 1])
                    else:
                        i = None
                    C.fence("vector", i)
                    if C.on("vector"):
                        i = C.eng.tensor_scalar(AT[:, buf, ci, :], psN[:, 0:DH],
                                                dn[:, 0:1], None, OP.mult)
                        C.tag(i, "v")
                    else:
                        C.inc("v")
                    C.wait("tensor", "v")
                C.wait("sync", "v")
                if C.on("sync"):
                    i = C.eng.dma_start(
                        out=at[h, :, :].rearrange("(c p) d -> p c d", p=128),
                        in_=AT[:, buf, :, :])
                    C.tag(i, f"st{buf}", 16)
                else:
                    C.inc(f"st{buf}", 16)
            C.wait("sync", "st0")
            C.wait("sync", "st1")

        run_program(nc, ["lt", "ld0", "ld1", "v", "m", "fv", "st0", "st1"], prog)
    return nc



def build_l0a_out():
    nc = bass.Bass("TRN2", target_bir_lowering=False, debug=False,
                   detect_race_conditions=RACE_DETECT)
    TG = 1024
    af = nc.dram_tensor("af", [D, TOK], BF16, kind="ExternalInput").ap()
    wo = nc.dram_tensor("wo", [D, D], F32, kind="ExternalInput").ap()
    bo = nc.dram_tensor("bo", [128, NC_D], F32, kind="ExternalInput").ap()
    xo = nc.dram_tensor("xo", [D, TOK], F32, kind="ExternalInput").ap()
    out = nc.dram_tensor("out", [D, TOK], F32, kind="ExternalOutput").ap()
    es = ExitStack()
    with es:
        A = es.enter_context(nc.sbuf_tensor("A", [128, NC_D, TG], BF16))
        W = es.enter_context(nc.sbuf_tensor("W", [128, NC_D, D], BF16))
        BO = es.enter_context(nc.sbuf_tensor("BO", [128, NC_D], F32))
        X = es.enter_context(nc.sbuf_tensor("X", [128, NC_D, TG], F32))
        Y = es.enter_context(nc.sbuf_tensor("Y", [128, NC_D, TG], F32))
        OB = es.enter_context(nc.sbuf_tensor("OB", [128, NC_D, 512], F32))
        ps = [es.enter_context(nc.psum_tensor(f"ps{i}", [128, 512], F32))
              for i in range(2)]
        lnt = alloc_ln_scratch(es, nc, NC_D, TG)

        def prog(C):
            if C.on("gpsimd"):
                C.eng.dma_start(out=W[:, :, :],
                                in_=wo.rearrange("(c p) o -> p c o", p=128)
                                ).then_inc(C.sems["lw"], 16)
                i = C.eng.dma_start(out=BO[:, :], in_=bo)
                C.tag(i, "lw", 16)
                C.inc("lw", 16)
            else:
                C.inc("lw", 32)
            emit_ln_const_init(C, lnt)
            C.wait("tensor", "lw")
            C.wait("scalar", "lw")

            evac_s = []
            for g in range(TOK // TG):
                gsl = slice(g * TG, (g + 1) * TG)
                if C.on("sync"):
                    C.eng.dma_start(
                        out=A[:, :, :],
                        in_=af.rearrange("(c p) t -> p c t", p=128)[:, :, gsl]
                        ).then_inc(C.sems["ld"], 16)
                    i = C.eng.dma_start(
                        out=X[:, :, :],
                        in_=xo.rearrange("(c p) t -> p c t", p=128)[:, :, gsl])
                    C.tag(i, "ld", 16)
                    C.inc("ld", 16)
                else:
                    C.inc("ld", 32)
                C.wait("tensor", "ld")
                it = 0
                for tc in range(TG // 512):
                    tsl = slice(tc * 512, (tc + 1) * 512)
                    for oc in range(NC_D):
                        bank = it % 2
                        if len(evac_s) >= 2:
                            C.wait("tensor", "s", evac_s[-2])
                        if C.on("tensor"):
                            for c in range(NC_D):
                                i = C.eng.matmul(ps[bank][:, :],
                                                 W[:, c, oc * 128:(oc + 1) * 128],
                                                 A[:, c, tsl], start=(c == 0),
                                                 stop=(c == NC_D - 1))
                            C.tag(i, "m")
                        else:
                            C.inc("m")
                        C.wait("scalar", "m")
                        if C.on("scalar"):
                            i = C.eng.activation(out=Y[:, oc, tsl],
                                                 in_=ps[bank][:, :],
                                                 func=AF.Identity,
                                                 bias=BO[:, oc:oc + 1])
                            C.tag(i, "s")
                        else:
                            C.inc("s")
                        evac_s.append(C.cnt["s"])
                        it += 1
                C.wait("vector", "s")
                C.wait("scalar", "s")
                C.wait("tensor", "s")

                def apply(C, ti):
                    sl = slice(ti * 512, (ti + 1) * 512)
                    if C.on("vector"):
                        Rb = bcN(lnt["ps_R"], NC_D)
                        Mb = bcN(lnt["ps_M"], NC_D)
                        C.eng.tensor_tensor(OB[:, :, :], Y[:, :, sl], Rb, OP.mult)
                        C.eng.tensor_tensor(OB[:, :, :], OB[:, :, :], Mb,
                                            OP.subtract)
                        i = C.eng.tensor_tensor(OB[:, :, :], OB[:, :, :],
                                                X[:, :, sl], OP.add)
                        C.tag(i, "v")
                    else:
                        C.inc("v")
                    C.wait("sync", "v")
                    if C.on("sync"):
                        i = C.eng.dma_start(
                            out=out.rearrange("(c p) t -> p c t",
                                              p=128)[:, :, gsl][:, :, sl],
                            in_=OB[:, :, :])
                        C.tag(i, "st", 16)
                    else:
                        C.inc("st", 16)
                    C.wait("vector", "st")

                emit_ln_fm(C, lnt, Y, TG, apply)
                C.wait("sync", "v")
                C.wait("sync", "s")
                C.wait("sync", "m")
            C.wait("sync", "st")

        run_program(nc, ["ld", "lw", "v", "s", "m", "fv", "st"], prog)
    return nc



EXT = 2432
L0F_SEGS = [(3, 9), (9, 15), (15, 19)]


def _tiles(n, w=512):
    out = []
    t = 0
    while t < n:
        out.append((t, min(w, n - t)))
        t += w
    return out


def build_l0f_attn():
    nc = bass.Bass("TRN2", target_bir_lowering=False, debug=False,
                   detect_race_conditions=RACE_DETECT)
    xin = nc.dram_tensor("xin", [D, EXT], F32, kind="ExternalInput").ap()
    wqkv = nc.dram_tensor("wqkv", [D, 3 * D], F32, kind="ExternalInput").ap()
    wo = nc.dram_tensor("wo", [NC_D, 128, NC_D * 128], F32,
                        kind="ExternalInput").ap()
    tq1 = nc.dram_tensor("tq1", [128, EXT], F32, kind="ExternalInput").ap()
    tq2 = nc.dram_tensor("tq2", [128, EXT], F32, kind="ExternalInput").ap()
    tk1 = nc.dram_tensor("tk1", [128, EXT], F32, kind="ExternalInput").ap()
    tk2 = nc.dram_tensor("tk2", [128, EXT], F32, kind="ExternalInput").ap()
    rotm = nc.dram_tensor("rotm", [128, 128], F32, kind="ExternalInput").ap()
    blk = nc.dram_tensor("blk", [128, 128], F32, kind="ExternalInput").ap()
    idm = nc.dram_tensor("idm", [128, 128], F32, kind="ExternalInput").ap()
    mneg = nc.dram_tensor("mneg", [3, 128, 384], F32, kind="ExternalInput").ap()
    xout = nc.dram_tensor("xout", [D, TOK], F32, kind="ExternalOutput").ap()

    es = ExitStack()
    with es:
        X = es.enter_context(nc.sbuf_tensor("X", [128, NC_D, EXT], F32))
        XBS = es.enter_context(nc.sbuf_tensor("XBS", [128, NC_D, 1025], BF16))
        W = es.enter_context(nc.sbuf_tensor("W", [128, NC_D, 2 * D], BF16))
        WOC = es.enter_context(nc.sbuf_tensor("WOC", [128, 2, NC_D, 128], BF16))
        TQ1 = es.enter_context(nc.sbuf_tensor("TQ1", [128, 1024], BF16))
        TQ2 = es.enter_context(nc.sbuf_tensor("TQ2", [128, 1024], BF16))
        TK1 = es.enter_context(nc.sbuf_tensor("TK1", [128, 1024], BF16))
        TK2 = es.enter_context(nc.sbuf_tensor("TK2", [128, 1024], BF16))
        ROTM = es.enter_context(nc.sbuf_tensor("ROTM", [128, 128], BF16))
        BLKM = es.enter_context(nc.sbuf_tensor("BLKM", [128, 128], BF16))
        IDM = es.enter_context(nc.sbuf_tensor("IDM", [128, 128], BF16))
        MNEG = es.enter_context(nc.sbuf_tensor("MNEG", [128, 3, 384], BF16))
        KH = es.enter_context(nc.sbuf_tensor("KH", [128, NC_D, 1024], BF16))
        QH = es.enter_context(nc.sbuf_tensor("QH", [128, NC_D, 768], BF16))
        VT = es.enter_context(nc.sbuf_tensor("VT", [128, 8, 16, DH + 1], BF16))
        ATM = es.enter_context(nc.sbuf_tensor("ATM", [128, 16, DH], BF16))
        AFM = es.enter_context(nc.sbuf_tensor("AFM", [128, NC_D, 512], BF16))
        CAR = es.enter_context(nc.sbuf_tensor("CAR", [128, NC_D, 256], F32))
        K2T = es.enter_context(nc.sbuf_tensor("K2T", [128, 512], BF16))
        RSB = es.enter_context(nc.sbuf_tensor("RSB", [128, 512], F32))
        ATMv = ATM[:, :, :].rearrange("p h d -> p (h d)")
        ET = ATMv[:, 512:1024]
        PB = es.enter_context(nc.sbuf_tensor("PB", [128, 2, 384], BF16))
        PJI = es.enter_context(nc.sbuf_tensor("PJI", [128, 2, 3, 128], BF16))
        NMX = es.enter_context(nc.sbuf_tensor("NMX", [128, 2], F32))
        RDN = es.enter_context(nc.sbuf_tensor("RDN", [128, 2], F32))
        EPSR = es.enter_context(nc.sbuf_tensor("EPSR", [128, 1], F32))
        psP = es.enter_context(nc.psum_tensor("psP", [128, 512], F32))
        psB = es.enter_context(nc.psum_tensor("psB", [128, 512], F32))
        psR = es.enter_context(nc.psum_tensor("psR", [128, 512], F32))
        psS = es.enter_context(nc.psum_tensor("psS", [128, 384], F32))
        psT = es.enter_context(nc.psum_tensor("psT", [128, 384], BF16))
        psT2 = es.enter_context(nc.psum_tensor("psT2", [128, 384], BF16))
        psN = es.enter_context(nc.psum_tensor("psN", [128, DH + 1], F32))
        psO = es.enter_context(nc.psum_tensor("psO", [128, 512], F32))

        def prog(C):
            if C.on("sync"):
                i = C.eng.dma_start(out=X[:, :, :],
                                    in_=xin.rearrange("(c p) t -> p c t", p=128))
                C.tag(i, "ld", 16)
            else:
                C.inc("ld", 16)
            if C.on("gpsimd"):
                for dst, srcap in ((ROTM, rotm), (BLKM, blk), (IDM, idm)):
                    i = C.eng.dma_start(out=dst[:, :], in_=srcap)
                    C.tag(i, "lw", 16)
                i = C.eng.dma_start(out=MNEG[:, :, :],
                                    in_=mneg.rearrange("m p t -> p m t"))
                C.tag(i, "lw", 16)
                i = C.eng.dma_start(
                    out=W[:, :, D:2 * D],
                    in_=wqkv[:, 2 * D:3 * D].rearrange("(c p) o -> p c o", p=128))
                C.tag(i, "lw", 16)
            else:
                C.inc("lw", 80)
            if C.on("vector"):
                C.eng.memset(EPSR[:, :], 1e-24)
                i = C.eng.memset(VT[:, :, :, DH:DH + 1], 1.0)
                C.tag(i, "v")
            else:
                C.inc("v")

            def pcn_pipeline(calls):
                pp = [psP, psO]
                sP, sE1, sB, sL2, sV1, sRT, sV2 = {}, {}, {}, {}, {}, {}, {}

                def st_P(i):
                    dst, oc, dst0, src0, tw, wofs, t1, t2 = calls[i]
                    p = i % 2
                    if i >= 2:
                        C.wait("tensor", "s", sE1[i - 2])
                    if C.on("tensor"):
                        for c in range(NC_D):
                            b = 1 if c < 4 else 0
                            j = C.eng.matmul(
                                pp[p][:, 0:tw],
                                W[:, c, wofs + oc * 128:wofs + oc * 128 + 128],
                                XBS[:, c, b + src0:b + src0 + tw],
                                start=(c == 0), stop=(c == NC_D - 1))
                        C.tag(j, "m")
                    else:
                        C.inc("m")
                    sP[i] = C.cnt["m"]

                def st_E1(i):
                    dst, oc, dst0, src0, tw, wofs, t1, t2 = calls[i]
                    p = i % 2
                    dsl = slice(dst0, dst0 + tw)
                    C.wait("scalar", "m", sB[i - 2] if i >= 2 else sP[i])
                    if C.on("scalar"):
                        C.eng.activation(out=dst[:, oc, dsl], in_=pp[p][:, 0:tw],
                                         func=AF.Identity)
                        k2o = K2T[:, 0:tw] if p == 0 else ATMv[:, 0:tw]
                        j = C.eng.activation(out=k2o,
                                             in_=pp[p][:, 0:tw],
                                             func=AF.Square)
                        C.tag(j, "s")
                    else:
                        C.inc("s")
                    sE1[i] = C.cnt["s"]

                def st_B(i):
                    tw = calls[i][4]
                    C.wait("tensor", "s", sE1[i])
                    if C.on("tensor"):
                        k2i = K2T[:, 0:tw] if i % 2 == 0 else ATMv[:, 0:tw]
                        j = C.eng.matmul(psB[:, 0:tw], BLKM[:, :], k2i,
                                         start=True, stop=True)
                        C.tag(j, "m")
                    else:
                        C.inc("m")
                    sB[i] = C.cnt["m"]

                def st_L2(i):
                    tw = calls[i][4]
                    C.wait("scalar", "m", sB[i])
                    if C.on("scalar"):
                        C.eng.activation(out=RSB[:, 0:tw], in_=psB[:, 0:tw],
                                         func=AF.Ln, bias=EPSR[:, 0:1])
                        j = C.eng.activation(out=RSB[:, 0:tw], in_=RSB[:, 0:tw],
                                             func=AF.Exp, scale=-0.5)
                        C.tag(j, "s")
                    else:
                        C.inc("s")
                    sL2[i] = C.cnt["s"]

                def st_V1(i):
                    dst, oc, dst0, src0, tw, wofs, t1, t2 = calls[i]
                    dsl = slice(dst0, dst0 + tw)
                    C.wait("vector", "s", sL2[i])
                    if C.on("vector"):
                        j = C.eng.tensor_tensor(dst[:, oc, dsl], dst[:, oc, dsl],
                                                RSB[:, 0:tw], OP.mult)
                    else:
                        j = None
                    C.fence("vector", j)
                    if C.on("vector"):
                        j = C.eng.engine_nop()
                        j.then_inc(C.sems["v"], 1)
                    C.inc("v")
                    sV1[i] = C.cnt["v"]

                def st_RT(i):
                    dst, oc, dst0, src0, tw, wofs, t1, t2 = calls[i]
                    dsl = slice(dst0, dst0 + tw)
                    C.wait("tensor", "v", max(sV1[i], sV2.get(i - 1, 0)))
                    if C.on("tensor"):
                        j = C.eng.matmul(psR[:, 0:tw], ROTM[:, :],
                                         dst[:, oc, dsl], start=True, stop=True)
                        C.tag(j, "m")
                    else:
                        C.inc("m")
                    sRT[i] = C.cnt["m"]

                def st_V2(i):
                    dst, oc, dst0, src0, tw, wofs, t1, t2 = calls[i]
                    dsl = slice(dst0, dst0 + tw)
                    tsl = slice(src0, src0 + tw)
                    C.wait("vector", "m", sRT[i])
                    if C.on("vector"):
                        C.eng.tensor_tensor(dst[:, oc, dsl], dst[:, oc, dsl],
                                            t1[:, tsl], OP.mult)
                        C.eng.tensor_tensor(ET[:, 0:tw], psR[:, 0:tw],
                                            t2[:, tsl], OP.mult)
                        j = C.eng.tensor_tensor(dst[:, oc, dsl], dst[:, oc, dsl],
                                                ET[:, 0:tw], OP.add)
                    else:
                        j = None
                    C.tag_fenced("vector", j, "v")
                    sV2[i] = C.cnt["v"]

                n = len(calls)
                st_P(0)
                if n > 1:
                    st_P(1)
                for i in range(n):
                    st_E1(i)
                    if i + 2 < n:
                        st_P(i + 2)
                    st_B(i)
                    if i >= 2:
                        st_RT(i - 2)
                    st_L2(i)
                    st_V1(i)
                    if i >= 2:
                        st_V2(i - 2)
                for i in (n - 2, n - 1):
                    st_RT(i)
                    st_V2(i)

            carry = []
            for si, (q0, q1) in enumerate(L0F_SEGS):
                kr0, kr1 = q0 - 2, q1
                e0 = kr0 * 128
                kw = (kr1 - kr0) * 128
                C.wait("vector", "ld")
                if C.on("vector"):
                    i = C.eng.tensor_copy(XBS[:, :, 0:kw + 1],
                                          X[:, :, e0 - 1:e0 + kw])
                    C.tag(i, "v")
                else:
                    C.inc("v")
                if carry:
                    cc0 = carry.pop()
                    if C.on("vector"):
                        i = C.eng.tensor_tensor(
                            X[:, :, cc0 * 128:(cc0 + 2) * 128],
                            X[:, :, cc0 * 128:(cc0 + 2) * 128],
                            CAR[:, :, :], OP.add)
                        C.tag(i, "v")
                    else:
                        C.inc("v")
                def seg_loads(si2):
                    q0_, q1_ = L0F_SEGS[si2]
                    e0_ = (q0_ - 2) * 128
                    kw_ = (q1_ - q0_ + 2) * 128
                    C.wait("gpsimd", "m")
                    C.wait("gpsimd", "v")
                    if C.on("gpsimd"):
                        for dst, srcap in ((TQ1, tq1), (TQ2, tq2), (TK1, tk1),
                                           (TK2, tk2)):
                            i = C.eng.dma_start(out=dst[:, 0:kw_],
                                                in_=srcap[:, e0_:e0_ + kw_])
                            C.tag(i, "lw", 16)
                        i = C.eng.dma_start(
                            out=W[:, :, 0:D],
                            in_=wqkv[:, D:2 * D].rearrange("(c p) o -> p c o",
                                                           p=128))
                        C.tag(i, "lw", 16)
                    else:
                        C.inc("lw", 80)
                if si == 0:
                    seg_loads(0)
                C.wait("tensor", "v")
                C.wait("tensor", "lw")
                C.wait("scalar", "lw")
                kcalls = [(KH, oc, tt0, tt0, tw, 0, TK1, TK2)
                          for oc in range(NC_D) for (tt0, tw) in _tiles(kw)]
                pcn_pipeline(kcalls)
                C.wait("gpsimd", "m")
                if C.on("gpsimd"):
                    i = C.eng.dma_start(
                        out=W[:, :, 0:D],
                        in_=wqkv[:, 0:D].rearrange("(c p) o -> p c o", p=128))
                    C.tag(i, "lw", 16)
                else:
                    C.inc("lw", 16)
                vpp = [psP, psO]
                vmm, vvv = {}, {}
                vit = 0
                for kc in range(kr0, kr1):
                    src0 = (kc - kr0) * 128
                    for half in range(2):
                        p2 = vit % 2
                        if vit >= 2:
                            C.wait("tensor", "v", vvv[vit - 2])
                        if C.on("tensor"):
                            for c in range(NC_D):
                                b = 1 if c < 4 else 0
                                i = C.eng.matmul(
                                    vpp[p2][:, :],
                                    XBS[:, c, b + src0:b + src0 + 128],
                                    W[:, c, D + half * 512:D + half * 512 + 512],
                                    start=(c == 0), stop=(c == NC_D - 1))
                            C.tag(i, "m")
                        else:
                            C.inc("m")
                        vmm[vit] = C.cnt["m"]
                        C.wait("vector", "m", vmm[vit])
                        if C.on("vector"):
                            i = C.eng.tensor_copy(
                                VT[:, kc - kr0, half * 8:half * 8 + 8, 0:DH],
                                vpp[p2][:, :].rearrange("p (h d) -> p h d", h=8))
                            C.tag(i, "v")
                        else:
                            C.inc("v")
                        vvv[vit] = C.cnt["v"]
                        vit += 1
                C.wait("tensor", "v")
                C.wait("tensor", "lw")
                C.wait("scalar", "lw")
                qw = (q1 - q0) * 128
                qsrc = (q0 - kr0) * 128
                qcalls = [(QH, oc, tt0, qsrc + tt0, tw, 0, TQ1, TQ2)
                          for oc in range(NC_D) for (tt0, tw) in _tiles(qw)]
                pcn_pipeline(qcalls)
                if si + 1 < len(L0F_SEGS):
                    seg_loads(si + 1)
                for qi, qc in enumerate(range(q0, q1)):
                    qsl = slice(qi * 128, (qi + 1) * 128)
                    ksl = slice((qc - 2 - kr0) * 128, (qc + 1 - kr0) * 128)
                    mi = qc - 3 if qc <= 4 else 2
                    bS = [psS[:, :], psP[:, 0:384]]
                    bT = [psT[:, :], psT2[:, :]]
                    bN = [psN[:, :], psR[:, 0:DH + 1]]
                    sE = {}
                    sSm = {}
                    sTm = {}
                    sAm = {}
                    sMv = {}
                    sVv = {}
                    C.wait("tensor", "s")
                    C.wait("tensor", "v")

                    def emit_S(h):
                        p = h % 2
                        hc, hr = h // 2, (h % 2) * 64
                        rows = slice(hr, hr + 64)
                        if h >= 2:
                            C.wait("tensor", "s", sE[h - 2])
                        if C.on("tensor"):
                            C.eng.matmul(bS[p], QH[rows, hc, qsl],
                                         KH[rows, hc, ksl], start=True,
                                         stop=False)
                            i = C.eng.matmul(bS[p], IDM[:, :], MNEG[:, mi, :],
                                             start=False, stop=True)
                            C.tag(i, "m")
                        else:
                            C.inc("m")
                        sSm[h] = C.cnt["m"]

                    def emit_M(h):
                        p = h % 2
                        C.wait("vector", "m", sSm[h])
                        if C.on("vector"):
                            i = C.eng.tensor_reduce(NMX[:, p:p + 1], bS[p],
                                                    mybir.AxisListType.X,
                                                    OP.max, negate=True)
                            C.tag(i, "v")
                        else:
                            C.inc("v")
                        sMv[h] = C.cnt["v"]

                    def emit_E(h):
                        p = h % 2
                        C.wait("scalar", "v", sMv[h])
                        if h >= 2:
                            C.wait("scalar", "m", sTm[h - 2])
                        if C.on("scalar"):
                            i = C.eng.activation(out=PB[:, p, :], in_=bS[p],
                                                 func=AF.Exp,
                                                 bias=NMX[:, p:p + 1])
                            C.tag(i, "s")
                        else:
                            C.inc("s")
                        sE[h] = C.cnt["s"]

                    def emit_T(h):
                        p = h % 2
                        C.wait("tensor", "s", sE[h])
                        if C.on("tensor"):
                            for b3 in range(3):
                                i = C.eng.transpose(
                                    bT[p][:, b3 * 128:(b3 + 1) * 128],
                                    PB[:, p, b3 * 128:(b3 + 1) * 128],
                                    IDM[:, :])
                            C.tag(i, "m")
                        else:
                            C.inc("m")
                        sTm[h] = C.cnt["m"]

                    def emit_V(h):
                        p = h % 2
                        C.wait("vector", "m", sTm[h])
                        if C.on("vector"):
                            i = C.eng.tensor_copy(
                                PJI[:, p, :, :],
                                bT[p][:, :].rearrange("p (b t) -> p b t", b=3))
                            C.tag(i, "v")
                        else:
                            C.inc("v")
                        sVv[h] = C.cnt["v"]

                    def emit_A(h):
                        p = h % 2
                        C.wait("tensor", "v", sVv[h])
                        if C.on("tensor"):
                            for b3 in range(3):
                                i = C.eng.matmul(bN[p], PJI[:, p, b3, :],
                                                 VT[:, qc - 2 - kr0 + b3, h, :],
                                                 start=(b3 == 0), stop=(b3 == 2))
                            C.tag(i, "m")
                        else:
                            C.inc("m")
                        sAm[h] = C.cnt["m"]

                    def emit_R(h):
                        p = h % 2
                        C.wait("vector", "m", sAm[h])
                        if C.on("vector"):
                            i = C.eng.reciprocal(RDN[:, p:p + 1],
                                                 bN[p][:, DH:DH + 1])
                        else:
                            i = None
                        C.fence("vector", i)
                        if C.on("vector"):
                            i = C.eng.tensor_scalar(ATM[:, h, :], bN[p][:, 0:DH],
                                                    RDN[:, p:p + 1], None,
                                                    OP.mult)
                            C.tag(i, "v")
                        else:
                            C.inc("v")

                    emit_S(0)
                    emit_S(1)
                    emit_M(0)
                    emit_E(0)
                    for h in range(16):
                        emit_T(h)
                        if h + 2 < 16:
                            emit_S(h + 2)
                        if h + 1 < 16:
                            emit_M(h + 1)
                        emit_V(h)
                        emit_A(h)
                        if h + 1 < 16:
                            emit_E(h + 1)
                        emit_R(h)
                    fcol = (qi * 128) % 512
                    ATMflat = ATM[:, :, :].rearrange("p h d -> p (h d)")
                    tpp = [psT, psT2]
                    tmm, tvv = {}, {}
                    for fb in range(NC_D):
                        p2 = fb % 2
                        if fb >= 2:
                            C.wait("tensor", "v", tvv[fb - 2])
                        if C.on("tensor"):
                            i = C.eng.transpose(tpp[p2][:, 0:128],
                                                ATMflat[:, fb * 128:(fb + 1) * 128],
                                                IDM[:, :])
                            C.tag(i, "m")
                        else:
                            C.inc("m")
                        tmm[fb] = C.cnt["m"]
                        C.wait("vector", "m", tmm[fb])
                        if C.on("vector"):
                            i = C.eng.tensor_copy(AFM[:, fb, fcol:fcol + 128],
                                                  tpp[p2][:, 0:128])
                            C.tag(i, "v")
                        else:
                            C.inc("v")
                        tvv[fb] = C.cnt["v"]
                    C.wait("tensor", "v")
                    if fcol == 384 or qc == q1 - 1:
                        nfl = fcol + 128
                        t00 = (qc + 1) * 128 - nfl
                        last = (qc == q1 - 1)
                        fpp = [psO, psP]
                        fmm, fvv = {}, {}
                        for oc in range(NC_D):
                            wbuf = oc % 2
                            if oc == 0:
                                C.wait("gpsimd", "m")
                                if C.on("gpsimd"):
                                    i = C.eng.dma_start(
                                        out=WOC[:, 0, :, :],
                                        in_=wo[0, :, :].rearrange(
                                            "p (c o) -> p c o", c=NC_D))
                                    C.tag(i, "lwa", 16)
                                else:
                                    C.inc("lwa", 16)
                            if oc + 1 < NC_D:
                                C.wait("gpsimd", "m", fmm.get(oc - 1, C.cnt["m"])
                                       if oc else C.cnt["m"])
                                if C.on("gpsimd"):
                                    i = C.eng.dma_start(
                                        out=WOC[:, 1 - wbuf, :, :],
                                        in_=wo[oc + 1, :, :].rearrange(
                                            "p (c o) -> p c o", c=NC_D))
                                    C.tag(i, "lwa" if (oc + 1) % 2 == 0
                                          else "lwb", 16)
                                else:
                                    C.inc("lwa" if (oc + 1) % 2 == 0
                                          else "lwb", 16)
                            C.wait("tensor", "lwa" if wbuf == 0 else "lwb")
                            if oc >= 2:
                                C.wait("tensor", "v", fvv[oc - 2])
                            if C.on("tensor"):
                                for c in range(NC_D):
                                    i = C.eng.matmul(
                                        fpp[wbuf][:, 0:nfl],
                                        WOC[:, wbuf, c, :],
                                        AFM[:, c, 0:nfl],
                                        start=(c == 0), stop=(c == NC_D - 1))
                                C.tag(i, "m")
                            else:
                                C.inc("m")
                            fmm[oc] = C.cnt["m"]
                            C.wait("vector", "m", fmm[oc])
                            ncar = 256 if last else 0
                            ndir = nfl - ncar
                            if C.on("vector"):
                                if ndir > 0:
                                    i = C.eng.tensor_tensor(
                                        X[:, oc, t00:t00 + ndir],
                                        X[:, oc, t00:t00 + ndir],
                                        fpp[wbuf][:, 0:ndir], OP.add)
                                if ncar:
                                    i = C.eng.tensor_copy(CAR[:, oc, :],
                                                          fpp[wbuf][:, ndir:nfl])
                                C.tag(i, "v")
                            else:
                                C.inc("v")
                            fvv[oc] = C.cnt["v"]
                        C.wait("tensor", "v")
                        if last:
                            carry.append(q1 - 2)
            if carry:
                cc0 = carry.pop()
                if C.on("vector"):
                    i = C.eng.tensor_tensor(
                        X[:, :, cc0 * 128:(cc0 + 2) * 128],
                        X[:, :, cc0 * 128:(cc0 + 2) * 128],
                        CAR[:, :, :], OP.add)
                    C.tag(i, "v")
                else:
                    C.inc("v")
            C.wait("sync", "v")
            if C.on("sync"):
                i = C.eng.dma_start(
                    out=xout.rearrange("(c p) t -> p c t", p=128),
                    in_=X[:, :, 384:EXT])
                C.tag(i, "st", 16)
            else:
                C.inc("st", 16)
            C.wait("sync", "st")

        run_program(nc, ["ld", "lw", "lwa", "lwb", "v", "s", "m", "fv",
                         "st"], prog)
    return nc



NHC = 22


def build_l0f_ffn():
    nc = bass.Bass("TRN2", target_bir_lowering=False, debug=False,
                   detect_race_conditions=RACE_DETECT)
    xin = nc.dram_tensor("xin", [D, TOK + 1], F32, kind="ExternalInput").ap()
    w1i = nc.dram_tensor("w1i", [D, NHC * 256], F32, kind="ExternalInput").ap()
    b1c = nc.dram_tensor("b1c", [128, NHC * 2], F32, kind="ExternalInput").ap()
    w2 = nc.dram_tensor("w2", [NC_D, 128, NHC * 128], F32,
                        kind="ExternalInput").ap()
    b2c = nc.dram_tensor("b2c", [128, NC_D], F32, kind="ExternalInput").ap()
    gln = nc.dram_tensor("gln", [128, NC_D], F32, kind="ExternalInput").ap()
    bln = nc.dram_tensor("bln", [128, NC_D], F32, kind="ExternalInput").ap()
    xout = nc.dram_tensor("xout", [D, TOK], F32, kind="ExternalOutput").ap()

    es = ExitStack()
    with es:
        XT = es.enter_context(nc.sbuf_tensor("XT", [128, NC_D, 513], F32))
        ZNB = es.enter_context(nc.sbuf_tensor("ZNB", [128, NC_D, 512], BF16))
        W1 = es.enter_context(nc.sbuf_tensor("W1", [128, NC_D, NHC * 256], BF16))
        B1C = es.enter_context(nc.sbuf_tensor("B1C", [128, NHC, 2], F32))
        W2C = es.enter_context(nc.sbuf_tensor("W2C", [128, 2, NHC, 128], BF16))
        B2C = es.enter_context(nc.sbuf_tensor("B2C", [128, NC_D], F32))
        GLN = es.enter_context(nc.sbuf_tensor("GLN", [128, NC_D], F32))
        BLN = es.enter_context(nc.sbuf_tensor("BLN", [128, NC_D], F32))
        HB = es.enter_context(nc.sbuf_tensor("HB", [128, NHC, 512], BF16))
        GEL = es.enter_context(nc.sbuf_tensor("GEL", [128, 2, 512], BF16))
        OB = es.enter_context(nc.sbuf_tensor("OB", [128, NC_D, 512], F32))
        lnt = alloc_ln_scratch(es, nc, NC_D, 512, tag="f")
        psA = es.enter_context(nc.psum_tensor("psA", [128, 512], F32))
        psG = es.enter_context(nc.psum_tensor("psG", [128, 512], F32))
        psY = es.enter_context(nc.psum_tensor("psY", [128, 512], F32))

        def prog(C):
            if C.on("gpsimd"):
                i = C.eng.dma_start(out=W1[:, :, :],
                                    in_=w1i.rearrange("(c p) o -> p c o", p=128))
                C.tag(i, "lw", 16)
                i = C.eng.dma_start(out=B1C[:, :, :],
                                    in_=b1c.rearrange("p (h two) -> p h two", two=2))
                C.tag(i, "lw", 16)
                i = C.eng.dma_start(out=B2C[:, :], in_=b2c)
                C.tag(i, "lw", 16)
                i = C.eng.dma_start(out=GLN[:, :], in_=gln)
                C.tag(i, "lw", 16)
                i = C.eng.dma_start(out=BLN[:, :], in_=bln)
                C.tag(i, "lw", 16)
            else:
                C.inc("lw", 80)
            emit_ln_const_init(C, lnt)
            C.wait("tensor", "lw")
            C.wait("scalar", "lw")
            C.wait("vector", "lw")

            for ti in range(TOK // 512):
                if C.on("sync"):
                    i = C.eng.dma_start(
                        out=XT[:, :, :],
                        in_=xin.rearrange("(c p) t -> p c t", p=128)[
                            :, :, ti * 512:ti * 512 + 513])
                    C.tag(i, "ld", 16)
                else:
                    C.inc("ld", 16)
                C.wait("tensor", "ld")
                C.wait("scalar", "ld")
                C.wait("vector", "ld")

                def sx(c):
                    b = 1 if c < 4 else 0
                    return XT[:, c, b:b + 512]

                if C.on("scalar"):
                    for c in range(NC_D):
                        i = C.eng.activation(out=lnt["x2t"][:, c, :], in_=sx(c),
                                             func=AF.Square)
                    C.tag(i, "s")
                else:
                    C.inc("s")
                C.wait("tensor", "s")
                if C.on("tensor"):
                    for c in range(NC_D):
                        C.eng.matmul(lnt["ps_s"][:, :], lnt["onesf"][:, 0:1],
                                     sx(c), start=(c == 0), stop=(c == NC_D - 1))
                    for c in range(NC_D):
                        i = C.eng.matmul(lnt["ps_q"][:, :], lnt["onesf"][:, 0:1],
                                         lnt["x2t"][:, c, :], start=(c == 0),
                                         stop=(c == NC_D - 1))
                    C.tag(i, "m")
                else:
                    C.inc("m")
                C.wait("vector", "m")
                if C.on("vector"):
                    C.eng.tensor_copy(lnt["srow"][:, :], lnt["ps_s"][:, :])
                    i = C.eng.tensor_copy(lnt["qrow"][:, :], lnt["ps_q"][:, :])
                else:
                    i = None
                C.tag_fenced("vector", i, "v")
                Dn = 1024.0
                if C.on("vector"):
                    C.eng.tensor_scalar(lnt["srow"][:, :], lnt["srow"][:, :],
                                        1.0 / Dn, None, OP.mult)
                    C.eng.tensor_tensor(lnt["mrow"][:, :], lnt["srow"][:, :],
                                        lnt["srow"][:, :], OP.mult)
                    i = C.eng.scalar_tensor_tensor(lnt["qrow"][:, :],
                                                   lnt["qrow"][:, :], 1.0 / Dn,
                                                   lnt["mrow"][:, :], OP.mult,
                                                   OP.subtract)
                else:
                    i = None
                C.tag_fenced("vector", i, "v")
                C.wait("scalar", "v")
                if C.on("scalar"):
                    C.eng.activation(out=lnt["rrow"][:, :], in_=lnt["qrow"][:, :],
                                     func=AF.Ln, bias=lnt["epsr"][0:1, 0:1])
                    i = C.eng.activation(out=lnt["rrow"][:, :],
                                         in_=lnt["rrow"][:, :], func=AF.Exp,
                                         scale=-0.5)
                    C.tag(i, "s")
                else:
                    C.inc("s")
                C.wait("vector", "s")
                if C.on("vector"):
                    i = C.eng.tensor_tensor(lnt["mrow"][:, :], lnt["srow"][:, :],
                                            lnt["rrow"][:, :], OP.mult)
                else:
                    i = None
                C.tag_fenced("vector", i, "v")
                C.wait("tensor", "v")
                if C.on("tensor"):
                    C.eng.matmul(lnt["ps_R"][:, :], lnt["onesf"][0:1, 0:128],
                                 lnt["rrow"][:, :], start=True, stop=True)
                    i = C.eng.matmul(lnt["ps_M"][:, :], lnt["onesf"][0:1, 0:128],
                                     lnt["mrow"][:, :], start=True, stop=True)
                    C.tag(i, "m")
                else:
                    C.inc("m")
                C.wait("vector", "m")
                if C.on("vector"):
                    for c in range(NC_D):
                        C.eng.tensor_tensor(OB[:, c, :], sx(c), lnt["ps_R"][:, :],
                                            OP.mult)
                    C.eng.tensor_tensor(
                        OB[:, :, :], OB[:, :, :],
                        lnt["ps_M"][:, :].rearrange("p (o t) -> p o t",
                                                    o=1).broadcast_to(
                            (128, NC_D, 512)), OP.subtract)
                    for c in range(NC_D):
                        i = C.eng.tensor_scalar(ZNB[:, c, :], OB[:, c, :],
                                                GLN[:, c:c + 1], BLN[:, c:c + 1],
                                                OP.mult, OP.add)
                    C.tag(i, "v")
                else:
                    C.inc("v")
                C.wait("tensor", "v")

                bA = [psA[:, :], lnt["ps_R"][:, :]]
                bG = [psG[:, :], lnt["ps_M"][:, :]]
                sMA, sG, sH = {}, {}, {}

                def st_MA(i):
                    p = i % 2
                    if i >= 2:
                        C.wait("tensor", "v", sH[i - 2])
                        C.wait("tensor", "s", sG[i - 2])
                    if C.on("tensor"):
                        for c in range(NC_D):
                            C.eng.matmul(bA[p], W1[:, c, i * 256:i * 256 + 128],
                                         ZNB[:, c, :], start=(c == 0),
                                         stop=(c == NC_D - 1))
                        for c in range(NC_D):
                            j = C.eng.matmul(bG[p],
                                             W1[:, c, i * 256 + 128:i * 256 + 256],
                                             ZNB[:, c, :], start=(c == 0),
                                             stop=(c == NC_D - 1))
                        C.tag(j, "m")
                    else:
                        C.inc("m")
                    sMA[i] = C.cnt["m"]

                def st_G(i):
                    p = i % 2
                    C.wait("scalar", "m", sMA[i])
                    if i >= 2:
                        C.wait("scalar", "v", sH[i - 2])
                    if C.on("scalar"):
                        j = C.eng.activation(out=GEL[:, p, :], in_=bG[p],
                                             func=AF.Gelu, bias=B1C[:, i, 1:2])
                        C.tag(j, "s")
                    else:
                        C.inc("s")
                    sG[i] = C.cnt["s"]

                def st_H(i):
                    p = i % 2
                    C.wait("vector", "s", sG[i])
                    if C.on("vector"):
                        j = C.eng.scalar_tensor_tensor(HB[:, i, :], bA[p],
                                                       B1C[:, i, 0:1],
                                                       GEL[:, p, :],
                                                       OP.add, OP.mult)
                        C.tag(j, "v")
                    else:
                        C.inc("v")
                    sH[i] = C.cnt["v"]

                st_MA(0)
                st_MA(1)
                st_G(0)
                for i in range(NHC):
                    st_H(i)
                    if i + 2 < NHC:
                        st_MA(i + 2)
                    if i + 1 < NHC:
                        st_G(i + 1)
                ypp = [psY, psA]
                ymm, yvv = {}, {}
                for oc in range(NC_D):
                    wbuf = oc % 2
                    if ti == 0 and oc == 0:
                        if C.on("gpsimd"):
                            i = C.eng.dma_start(
                                out=W2C[:, 0, :, :],
                                in_=w2[0, :, :].rearrange(
                                    "p (c o) -> p c o", c=NHC))
                            C.tag(i, "lwa", 16)
                        else:
                            C.inc("lwa", 16)
                    noc = oc + 1
                    if noc < NC_D or ti + 1 < TOK // 512:
                        noc2 = noc % NC_D
                        C.wait("gpsimd", "m",
                               ymm[oc - 1] if oc >= 1 else C.cnt["m"])
                        if C.on("gpsimd"):
                            i = C.eng.dma_start(
                                out=W2C[:, 1 - wbuf, :, :],
                                in_=w2[noc2, :, :].rearrange(
                                    "p (c o) -> p c o", c=NHC))
                            C.tag(i, "lwa" if noc % 2 == 0 else "lwb", 16)
                        else:
                            C.inc("lwa" if noc % 2 == 0 else "lwb", 16)
                    C.wait("tensor", "lwa" if wbuf == 0 else "lwb")
                    if oc >= 2:
                        C.wait("tensor", "v", yvv[oc - 2])
                    if C.on("tensor"):
                        for hc in range(NHC):
                            i = C.eng.matmul(ypp[wbuf][:, :],
                                             W2C[:, wbuf, hc, :],
                                             HB[:, hc, :], start=(hc == 0),
                                             stop=(hc == NHC - 1))
                        C.tag(i, "m")
                    else:
                        C.inc("m")
                    ymm[oc] = C.cnt["m"]
                    C.wait("vector", "m", ymm[oc])
                    if C.on("vector"):
                        i = C.eng.scalar_tensor_tensor(OB[:, oc, :],
                                                       ypp[wbuf][:, :],
                                                       B2C[:, oc:oc + 1],
                                                       XT[:, oc, 1:513],
                                                       OP.add, OP.add)
                        C.tag(i, "v")
                    else:
                        C.inc("v")
                    yvv[oc] = C.cnt["v"]
                C.wait("tensor", "v")
                C.wait("sync", "v")
                if C.on("sync"):
                    i = C.eng.dma_start(
                        out=xout.rearrange("(c p) t -> p c t", p=128)[
                            :, :, ti * 512:(ti + 1) * 512],
                        in_=OB[:, :, :])
                    C.tag(i, "st", 16)
                else:
                    C.inc("st", 16)
                C.wait("sync", "st")

        run_program(nc, ["ld", "lw", "lwa", "lwb", "v", "s", "m",
                         "fv", "st"], prog)
    return nc



def build_pre_ln():
    nc = bass.Bass("TRN2", target_bir_lowering=False, debug=False,
                   detect_race_conditions=RACE_DETECT)
    xo = nc.dram_tensor("xo", [D, TOK], F32, kind="ExternalInput").ap()
    out = nc.dram_tensor("out", [D, TOK], F32, kind="ExternalOutput").ap()
    TG = 1024
    es = ExitStack()
    with es:
        X = es.enter_context(nc.sbuf_tensor("X", [128, NC_D, TG], F32))
        OB = es.enter_context(nc.sbuf_tensor("OB", [128, NC_D, 512], F32))
        lnt = alloc_ln_scratch(es, nc, NC_D, TG)

        def prog(C):
            emit_ln_const_init(C, lnt)
            xor = xo.rearrange("(c p) t -> p c t", p=128)
            outr = out.rearrange("(c p) t -> p c t", p=128)
            for g in range(TOK // TG):
                gsl = slice(g * TG, (g + 1) * TG)
                C.dma("sync", "ld", lambda e: e.dma_start(
                    out=X[:, :, :], in_=xor[:, :, gsl]))
                C.wait("tensor", "ld")
                C.wait("scalar", "ld")
                C.wait("vector", "ld")

                def apply(C, ti):
                    sl = slice(ti * 512, (ti + 1) * 512)
                    if C.on("vector"):
                        C.eng.tensor_tensor(OB[:, :, :], X[:, :, sl],
                                            bcN(lnt["ps_R"], NC_D), OP.mult)
                        i = C.eng.tensor_tensor(OB[:, :, :], OB[:, :, :],
                                                bcN(lnt["ps_M"], NC_D),
                                                OP.subtract)
                        C.tag(i, "v")
                    else:
                        C.inc("v")
                    C.wait("sync", "v")
                    C.dma("sync", "st", lambda e: e.dma_start(
                        out=outr[:, :, gsl][:, :, sl], in_=OB[:, :, :]))
                    C.wait("vector", "st")

                emit_ln_fm(C, lnt, X, TG, apply)
                C.wait("sync", "v")
                C.wait("sync", "s")
                C.wait("sync", "m")
            C.wait("sync", "st")

        run_program(nc, ["ld", "lw", "v", "s", "m", "fv", "st"], prog)
    return nc


def build_post_ln():
    nc = bass.Bass("TRN2", target_bir_lowering=False, debug=False,
                   detect_race_conditions=RACE_DETECT)
    yo = nc.dram_tensor("yo", [D, TOK], F32, kind="ExternalInput").ap()
    xo = nc.dram_tensor("xo", [D, TOK], F32, kind="ExternalInput").ap()
    out = nc.dram_tensor("out", [D, TOK], F32, kind="ExternalOutput").ap()
    TG = 1024
    es = ExitStack()
    with es:
        Y = es.enter_context(nc.sbuf_tensor("Y", [128, NC_D, TG], F32))
        X = es.enter_context(nc.sbuf_tensor("X", [128, NC_D, TG], F32))
        OB = es.enter_context(nc.sbuf_tensor("OB", [128, NC_D, 512], F32))
        lnt = alloc_ln_scratch(es, nc, NC_D, TG)

        def prog(C):
            emit_ln_const_init(C, lnt)
            yor = yo.rearrange("(c p) t -> p c t", p=128)
            xor = xo.rearrange("(c p) t -> p c t", p=128)
            outr = out.rearrange("(c p) t -> p c t", p=128)
            for g in range(TOK // TG):
                gsl = slice(g * TG, (g + 1) * TG)
                C.dma("sync", "ld", lambda e: e.dma_start(
                    out=Y[:, :, :], in_=yor[:, :, gsl]))
                C.dma("sync", "ld", lambda e: e.dma_start(
                    out=X[:, :, :], in_=xor[:, :, gsl]))
                C.wait("tensor", "ld")
                C.wait("scalar", "ld")
                C.wait("vector", "ld")

                def apply(C, ti):
                    sl = slice(ti * 512, (ti + 1) * 512)
                    if C.on("vector"):
                        C.eng.tensor_tensor(OB[:, :, :], Y[:, :, sl],
                                            bcN(lnt["ps_R"], NC_D), OP.mult)
                        C.eng.tensor_tensor(OB[:, :, :], OB[:, :, :],
                                            bcN(lnt["ps_M"], NC_D), OP.subtract)
                        i = C.eng.tensor_tensor(OB[:, :, :], OB[:, :, :],
                                                X[:, :, sl], OP.add)
                        C.tag(i, "v")
                    else:
                        C.inc("v")
                    C.wait("sync", "v")
                    C.dma("sync", "st", lambda e: e.dma_start(
                        out=outr[:, :, gsl][:, :, sl], in_=OB[:, :, :]))
                    C.wait("vector", "st")

                emit_ln_fm(C, lnt, Y, TG, apply)
                C.wait("sync", "v")
                C.wait("sync", "s")
                C.wait("sync", "m")
            C.wait("sync", "st")

        run_program(nc, ["ld", "lw", "v", "s", "m", "fv", "st"], prog)
    return nc



def build_l1a_A():
    nc = bass.Bass("TRN2", target_bir_lowering=False, debug=False,
                   detect_race_conditions=RACE_DETECT)
    TE = TOK + 2
    xo = nc.dram_tensor("xo", [D, TE], F32, kind="ExternalInput").ap()
    nsc = nc.dram_tensor("nsc", [128, NC_D], F32, kind="ExternalInput").ap()
    inw = nc.dram_tensor("inw", [D, 2 * D], F32, kind="ExternalInput").ap()
    inb = nc.dram_tensor("inb", [128, 16], F32, kind="ExternalInput").ap()
    dww = nc.dram_tensor("dww", [128, NC_D * 3], F32, kind="ExternalInput").ap()
    dwb = nc.dram_tensor("dwb", [128, NC_D], F32, kind="ExternalInput").ap()
    pww = nc.dram_tensor("pww", [D, D], F32, kind="ExternalInput").ap()
    pwb = nc.dram_tensor("pwb", [128, NC_D], F32, kind="ExternalInput").ap()
    winf = nc.dram_tensor("winf", [D, 64], F32, kind="ExternalInput").ap()
    winb = nc.dram_tensor("winb", [D, 64], F32, kind="ExternalInput").ap()
    sspar = nc.dram_tensor("sspar", [64, 6], F32, kind="ExternalInput").ap()
    edge = nc.dram_tensor("edge", [128, 2], F32, kind="ExternalInput").ap()
    t_out = nc.dram_tensor("t_out", [D, TOK], F32, kind="ExternalOutput").ap()
    g_out = nc.dram_tensor("g_out", [D, TOK], BF16, kind="ExternalOutput").ap()
    hf_out = nc.dram_tensor("hf_out", [64, 2 * TOK], F32,
                            kind="ExternalOutput").ap()

    es = ExitStack()
    with es:
        XT = es.enter_context(nc.sbuf_tensor("XT", [128, NC_D, 514], F32))
        TT = es.enter_context(nc.sbuf_tensor("TT", [128, NC_D, 514], F32))
        TNB = es.enter_context(nc.sbuf_tensor("TNB", [128, NC_D, 514], BF16))
        XC = es.enter_context(nc.sbuf_tensor("XC", [128, NC_D, 514], F32))
        CVB = es.enter_context(nc.sbuf_tensor("CVB", [128, NC_D, 512], BF16))
        GAT = es.enter_context(nc.sbuf_tensor("GAT", [128, NC_D, 512], BF16))
        CVT = es.enter_context(nc.sbuf_tensor("CVT", [128, 512], F32))
        W = es.enter_context(nc.sbuf_tensor("W", [128, NC_D, 2 * D], BF16))
        WIN = es.enter_context(nc.sbuf_tensor("WIN", [128, NC_D, 128], BF16))
        NSC = es.enter_context(nc.sbuf_tensor("NSC", [128, NC_D], F32))
        INB = es.enter_context(nc.sbuf_tensor("INB", [128, 16], F32))
        DWW = es.enter_context(nc.sbuf_tensor("DWW", [128, NC_D, 3], F32))
        DWB = es.enter_context(nc.sbuf_tensor("DWB", [128, NC_D], F32))
        PWB = es.enter_context(nc.sbuf_tensor("PWB", [128, NC_D], F32))
        SSP = es.enter_context(nc.sbuf_tensor("SSP", [64, 6], F32))
        EDG = es.enter_context(nc.sbuf_tensor("EDG", [128, 2], F32))
        PRM = es.enter_context(nc.sbuf_tensor("PRM", [64, 8], F32))
        ABC = es.enter_context(nc.sbuf_tensor("ABC", [64, 4], F32))
        ABT = es.enter_context(nc.sbuf_tensor("ABT", [64, 2, TOK], F32))
        UF = es.enter_context(nc.sbuf_tensor("UF", [64, 2, TOK], F32))
        HF = es.enter_context(nc.sbuf_tensor("HF", [64, 2, TOK], F32))
        Z0 = es.enter_context(nc.sbuf_tensor("Z0", [64, 1], F32))
        MSK = es.enter_context(nc.sbuf_tensor("MSK", [64, 1], mybir.dt.int32))
        lnt = alloc_ln_scratch(es, nc, NC_D, 514)
        ps = [es.enter_context(nc.psum_tensor(f"ps{i}", [128, 512], F32))
              for i in range(2)]
        psU = es.enter_context(nc.psum_tensor("psU", [64, 512], F32))

        def prog(C):
            xor = xo.rearrange("(c p) t -> p c t", p=128)
            tor = t_out.rearrange("(c p) t -> p c t", p=128)
            gor = g_out.rearrange("(c p) t -> p c t", p=128)
            C.dma("gpsimd", "lw", lambda e: e.dma_start(out=NSC[:, :], in_=nsc))
            C.dma("gpsimd", "lw", lambda e: e.dma_start(out=INB[:, :], in_=inb))
            C.dma("gpsimd", "lw", lambda e: e.dma_start(
                out=DWW[:, :, :], in_=dww.rearrange("p (c k) -> p c k", k=3)))
            C.dma("gpsimd", "lw", lambda e: e.dma_start(out=DWB[:, :], in_=dwb))
            C.dma("gpsimd", "lw", lambda e: e.dma_start(out=PWB[:, :], in_=pwb))
            C.dma("gpsimd", "lw", lambda e: e.dma_start(out=SSP[:, :], in_=sspar))
            C.dma("gpsimd", "lw", lambda e: e.dma_start(out=EDG[:, :], in_=edge))
            C.dma("gpsimd", "lw", lambda e: e.dma_start(
                out=WIN[:, :, 0:64],
                in_=winf.rearrange("(c p) n -> p c n", p=128)))
            C.dma("gpsimd", "lw", lambda e: e.dma_start(
                out=WIN[:, :, 64:128],
                in_=winb.rearrange("(c p) n -> p c n", p=128)))
            emit_ln_const_init(C, lnt)

            C.wait("scalar", "lw")
            for di in range(2):
                la, lb, ldt = 3 * di, 3 * di + 1, 3 * di + 2
                C.op("scalar", "s", lambda e: e.activation(
                    out=PRM[:, 0:1], in_=SSP[:, la:la + 1], func=AF.Exp))
                C.op("scalar", "s", lambda e: e.activation(
                    out=PRM[:, 1:2], in_=SSP[:, ldt:ldt + 1], func=AF.Exp))
                C.wait("vector", "s")
                if C.on("vector"):
                    i = C.eng.tensor_tensor(PRM[:, 2:3], PRM[:, 0:1],
                                            PRM[:, 1:2], OP.mult)
                else:
                    i = None
                C.tag_fenced("vector", i, "v")
                C.wait("scalar", "v")
                C.op("scalar", "s", lambda e: e.activation(
                    out=PRM[:, 3:4], in_=PRM[:, 2:3], func=AF.Exp, scale=-1.0))
                C.wait("vector", "s")
                if C.on("vector"):
                    i = C.eng.reciprocal(PRM[:, 4:5], PRM[:, 0:1])
                else:
                    i = None
                C.tag_fenced("vector", i, "v")
                if C.on("vector"):
                    i = C.eng.tensor_scalar(PRM[:, 5:6], PRM[:, 3:4], -1.0, 1.0,
                                            OP.mult, OP.add)
                else:
                    i = None
                C.tag_fenced("vector", i, "v")
                if C.on("vector"):
                    i = C.eng.tensor_tensor(PRM[:, 7:8], PRM[:, 5:6],
                                            PRM[:, 4:5], OP.mult)
                else:
                    i = None
                C.tag_fenced("vector", i, "v")
                if C.on("vector"):
                    i = C.eng.tensor_scalar(MSK[:, 0:1], SSP[:, la:la + 1],
                                            float(np.log(1e-6)), None, OP.is_lt)
                else:
                    i = None
                C.tag_fenced("vector", i, "v")
                if C.on("vector"):
                    i = C.eng.select(Z0[:, 0:1], MSK[:, 0:1], PRM[:, 1:2],
                                     PRM[:, 7:8])
                else:
                    i = None
                C.tag_fenced("vector", i, "v")
                if C.on("vector"):
                    C.eng.tensor_tensor(ABC[:, 2 * di + 1:2 * di + 2],
                                        Z0[:, 0:1], SSP[:, lb:lb + 1], OP.mult)
                    i = C.eng.tensor_copy(ABC[:, 2 * di:2 * di + 1], PRM[:, 3:4])
                else:
                    i = None
                C.tag_fenced("vector", i, "v")
            if C.on("vector"):
                C.eng.tensor_copy(ABT[:, 0, :],
                                  ABC[:, 0:1].broadcast_to((64, TOK)))
                i = C.eng.tensor_copy(ABT[:, 1, :],
                                      ABC[:, 2:3].broadcast_to((64, TOK)))
                C.tag(i, "v")
            else:
                C.inc("v")

            for g in range(TOK // 512):
                t0 = g * 512
                C.wait("sync", "s")
                C.wait("sync", "m")
                C.wait("sync", "v")
                C.dma("sync", "ld", lambda e: e.dma_start(
                    out=XT[:, :, :], in_=xor[:, :, t0:t0 + 514]))
                if g > 0:
                    C.wait("gpsimd", "m")
                    C.dma("gpsimd", "lw", lambda e: e.dma_start(
                        out=W[:, :, 0:D],
                        in_=inw[:, 0:D].rearrange("(c p) o -> p c o", p=128)))
                else:
                    C.dma("gpsimd", "lw", lambda e: e.dma_start(
                        out=W[:, :, :],
                        in_=inw.rearrange("(c p) o -> p c o", p=128)))
                C.wait("tensor", "ld")
                C.wait("scalar", "ld")
                C.wait("vector", "ld")
                C.wait("tensor", "lw")
                C.wait("scalar", "lw")

                def applyT(C, ti, base=0):
                    sl = slice(base + ti * 512, base + (ti + 1) * 512)
                    if C.on("vector"):
                        C.eng.tensor_tensor(TT[:, :, sl], XT[:, :, sl],
                                            bcN(lnt["ps_R"], NC_D), OP.mult)
                        i = C.eng.tensor_tensor(TT[:, :, sl], TT[:, :, sl],
                                                bcN(lnt["ps_M"], NC_D),
                                                OP.subtract)
                        C.tag(i, "v")
                    else:
                        C.inc("v")
                    C.wait("tensor", "v")

                emit_ln_fm(C, lnt, XT[:, :, 0:512], 512, applyT)
                emit_ln_fm(C, lnt, XT[:, :, 2:514], 512,
                           lambda C, ti: applyT(C, ti, base=2))
                C.wait("sync", "v")
                C.dma("sync", "st", lambda e: e.dma_start(
                    out=tor[:, :, t0:t0 + 512], in_=TT[:, :, 1:513]))

                def applyR(C, ti, base=0):
                    sl = slice(base + ti * 512, base + (ti + 1) * 512)
                    if C.on("vector"):
                        C.eng.tensor_tensor(lnt["x2t"][:, :, :], TT[:, :, sl],
                                            bcN(lnt["ps_R"], NC_D), OP.mult)
                        for c in range(NC_D):
                            i = C.eng.tensor_scalar(TNB[:, c, sl],
                                                    lnt["x2t"][:, c, :],
                                                    NSC[:, c:c + 1], None,
                                                    OP.mult)
                        C.tag(i, "v")
                    else:
                        C.inc("v")
                    C.wait("tensor", "v")
                    C.wait("scalar", "v")

                emit_ln_fm(C, lnt, TT[:, :, 0:512], 512, applyR, rms=True,
                           eps=1e-6)
                emit_ln_fm(C, lnt, TT[:, :, 2:514], 512,
                           lambda C, ti: applyR(C, ti, base=2), rms=True,
                           eps=1e-6)

                sev = []
                for oc in range(NC_D):
                    for (a0, aw) in ((0, 512), (512, 2)):
                        bank = len(sev) % 2
                        if len(sev) >= 2:
                            C.wait("tensor", "s", sev[-2])
                        if C.on("tensor"):
                            for c in range(NC_D):
                                i = C.eng.matmul(ps[bank][:, 0:aw],
                                                 W[:, c, oc * 128:(oc + 1) * 128],
                                                 TNB[:, c, a0:a0 + aw],
                                                 start=(c == 0),
                                                 stop=(c == NC_D - 1))
                            C.tag(i, "m")
                        else:
                            C.inc("m")
                        C.wait("scalar", "m")
                        C.op("scalar", "s", lambda e, oc=oc, a0=a0, aw=aw,
                             bank=bank: e.activation(
                                 out=XC[:, oc, a0:a0 + aw], in_=ps[bank][:, 0:aw],
                                 func=AF.Identity, bias=INB[:, oc:oc + 1]))
                        sev.append(C.cnt["s"])
                for oc in range(NC_D):
                    bank = len(sev) % 2
                    if len(sev) >= 2:
                        C.wait("tensor", "s", sev[-2])
                    if C.on("tensor"):
                        for c in range(NC_D):
                            i = C.eng.matmul(ps[bank][:, :],
                                             W[:, c, D + oc * 128:D + (oc + 1) * 128],
                                             TNB[:, c, 1:513], start=(c == 0),
                                             stop=(c == NC_D - 1))
                        C.tag(i, "m")
                    else:
                        C.inc("m")
                    C.wait("scalar", "m")
                    C.op("scalar", "s", lambda e, oc=oc, bank=bank: e.activation(
                        out=GAT[:, oc, :], in_=ps[bank][:, :], func=AF.Sigmoid,
                        bias=INB[:, 8 + oc:9 + oc]))
                    sev.append(C.cnt["s"])
                C.wait("sync", "s")
                C.dma("sync", "st", lambda e: e.dma_start(
                    out=gor[:, :, t0:t0 + 512], in_=GAT[:, :, :]))

                C.wait("vector", "s")
                C.wait("vector", "lw")
                if g == 0:
                    if C.on("vector"):
                        i = C.eng.tensor_scalar(XC[:, :, 0:1], XC[:, :, 0:1],
                                                EDG[:, 0:1], None, OP.mult)
                        C.tag(i, "v")
                    else:
                        C.inc("v")
                if g == TOK // 512 - 1:
                    if C.on("vector"):
                        i = C.eng.tensor_scalar(XC[:, :, 513:514],
                                                XC[:, :, 513:514],
                                                EDG[:, 1:2], None, OP.mult)
                        C.tag(i, "v")
                    else:
                        C.inc("v")
                for c in range(NC_D):
                    if C.on("vector"):
                        C.eng.tensor_scalar(CVT[:, :], XC[:, c, 0:512],
                                            DWW[:, c, 0:1], None, OP.mult)
                        C.eng.scalar_tensor_tensor(CVT[:, :], XC[:, c, 1:513],
                                                   DWW[:, c, 1:2], CVT[:, :],
                                                   OP.mult, OP.add)
                        C.eng.scalar_tensor_tensor(CVT[:, :], XC[:, c, 2:514],
                                                   DWW[:, c, 2:3], CVT[:, :],
                                                   OP.mult, OP.add)
                        i = C.eng.tensor_scalar(CVB[:, c, :], CVT[:, :],
                                                DWB[:, c:c + 1], None, OP.add)
                    else:
                        i = None
                    C.tag_fenced("vector", i, "v")
                C.wait("gpsimd", "m")
                C.dma("gpsimd", "lw", lambda e: e.dma_start(
                    out=W[:, :, 0:D],
                    in_=pww.rearrange("(c p) o -> p c o", p=128)))
                C.wait("tensor", "lw")
                C.wait("tensor", "v")
                sev2 = []
                for oc in range(NC_D):
                    bank = len(sev2) % 2
                    if len(sev2) >= 2:
                        C.wait("tensor", "s", sev2[-2])
                    if C.on("tensor"):
                        for c in range(NC_D):
                            i = C.eng.matmul(ps[bank][:, :],
                                             W[:, c, oc * 128:(oc + 1) * 128],
                                             CVB[:, c, :], start=(c == 0),
                                             stop=(c == NC_D - 1))
                        C.tag(i, "m")
                    else:
                        C.inc("m")
                    C.wait("scalar", "m")
                    C.op("scalar", "s", lambda e, oc=oc, bank=bank: e.activation(
                        out=XC[:, oc, 0:512], in_=ps[bank][:, :],
                        func=AF.Identity, bias=PWB[:, oc:oc + 1]))
                    sev2.append(C.cnt["s"])
                C.wait("vector", "s")
                if C.on("vector"):
                    i = C.eng.tensor_copy(CVB[:, :, :], XC[:, :, 0:512])
                    C.tag(i, "v")
                else:
                    C.inc("v")
                C.wait("tensor", "v")
                for di in range(2):
                    if C.on("tensor"):
                        for c in range(NC_D):
                            i = C.eng.matmul(psU[:, :],
                                             WIN[:, c, di * 64:(di + 1) * 64],
                                             CVB[:, :, :][:, c, :],
                                             start=(c == 0),
                                             stop=(c == NC_D - 1))
                        C.tag(i, "m")
                    else:
                        C.inc("m")
                    C.wait("vector", "m")
                    if C.on("vector"):
                        i = C.eng.tensor_scalar(UF[:, di, t0:t0 + 512],
                                                psU[:, :],
                                                ABC[:, 2 * di + 1:2 * di + 2],
                                                None, OP.mult)
                    else:
                        i = None
                    C.tag_fenced("vector", i, "v")
                    C.wait("tensor", "v")

            if C.on("vector"):
                i = C.eng.engine_nop()
            else:
                i = None
            C.fence("vector", i)
            if C.on("vector"):
                C.eng.tensor_tensor_scan(HF[:, 0, :], ABT[:, 0, :], UF[:, 0, :],
                                         0.0, OP.mult, OP.add)
                i = C.eng.tensor_tensor_scan(HF[:, 1, ::-1], ABT[:, 1, ::-1],
                                             UF[:, 1, ::-1], 0.0,
                                             OP.mult, OP.add)
                C.tag(i, "v")
            else:
                C.inc("v")
            C.wait("sync", "v")
            C.dma("sync", "st", lambda e: e.dma_start(
                out=hf_out, in_=HF[:, :, :].rearrange("p a t -> p (a t)")))
            C.wait("sync", "st")

        run_program(nc, ["ld", "lw", "v", "s", "m", "fv", "st"], prog)
    return nc



def build_l1a_B(dbg=False):
    nc = bass.Bass("TRN2", target_bir_lowering=False, debug=False,
                   detect_race_conditions=RACE_DETECT)
    hfin = nc.dram_tensor("hfin", [64, 2 * TOK], F32, kind="ExternalInput").ap()
    carry = nc.dram_tensor("carry", [64, 2], F32, kind="ExternalInput").ap()
    sspar = nc.dram_tensor("sspar", [64, 6], F32, kind="ExternalInput").ap()
    cpar = nc.dram_tensor("cpar", [64, 2], F32, kind="ExternalInput").ap()
    woutT = nc.dram_tensor("woutT", [64, 2 * D], F32, kind="ExternalInput").ap()
    outwT = nc.dram_tensor("outwT", [D, D], F32, kind="ExternalInput").ap()
    outb = nc.dram_tensor("outb", [128, NC_D], F32, kind="ExternalInput").ap()
    gin = nc.dram_tensor("gin", [D, TOK], BF16, kind="ExternalInput").ap()
    tin = nc.dram_tensor("tin", [D, TOK], F32, kind="ExternalInput").ap()
    xin = nc.dram_tensor("xin", [D, TOK], F32, kind="ExternalInput").ap()
    out = nc.dram_tensor("out", [D, TOK], F32, kind="ExternalOutput").ap()
    if dbg:
        d_pow = nc.dram_tensor("d_pow", [64, 2 * TOK], F32,
                               kind="ExternalOutput").ap()
        d_hf = nc.dram_tensor("d_hf", [64, 2 * TOK], F32,
                              kind="ExternalOutput").ap()
        d_y = nc.dram_tensor("d_y", [D, TOK], F32, kind="ExternalOutput").ap()
    TG = 512

    es = ExitStack()
    with es:
        HF = es.enter_context(nc.sbuf_tensor("HF", [64, 2, TOK], F32))
        HTB = es.enter_context(nc.sbuf_tensor("HTB", [64, 2, TOK], BF16))
        IOT = es.enter_context(nc.sbuf_tensor("IOT", [64, 2, TOK], F32))
        POW = es.enter_context(nc.sbuf_tensor("POW", [64, TOK], F32))
        CAR = es.enter_context(nc.sbuf_tensor("CAR", [64, 2], F32))
        SSP = es.enter_context(nc.sbuf_tensor("SSP", [64, 6], F32))
        CPA = es.enter_context(nc.sbuf_tensor("CPA", [64, 2], F32))
        PRM = es.enter_context(nc.sbuf_tensor("PRM", [64, 4], F32))
        WOT = es.enter_context(nc.sbuf_tensor("WOT", [64, 2, D], BF16))
        OWT = es.enter_context(nc.sbuf_tensor("OWT", [128, NC_D, D], BF16))
        OBC = es.enter_context(nc.sbuf_tensor("OBC", [128, NC_D], F32))
        GATE = es.enter_context(nc.sbuf_tensor("GATE", [128, NC_D, TG], BF16))
        TIN = es.enter_context(nc.sbuf_tensor("TIN", [128, NC_D, TG], F32))
        XIN = es.enter_context(nc.sbuf_tensor("XIN", [128, NC_D, TG], F32))
        Y = es.enter_context(nc.sbuf_tensor("Y", [128, NC_D, TG], F32))
        YGB = es.enter_context(nc.sbuf_tensor("YGB", [128, NC_D, TG], BF16))
        OB = es.enter_context(nc.sbuf_tensor("OB", [128, NC_D, 512], F32))
        lnt = alloc_ln_scratch(es, nc, NC_D, TG)
        psY = es.enter_context(nc.psum_tensor("psY", [128, 512], F32))
        psO = es.enter_context(nc.psum_tensor("psO", [128, 512], F32))

        def prog(C):
            gir = gin.rearrange("(c p) t -> p c t", p=128)
            tir = tin.rearrange("(c p) t -> p c t", p=128)
            xir = xin.rearrange("(c p) t -> p c t", p=128)
            outr = out.rearrange("(c p) t -> p c t", p=128)
            C.dma("sync", "ld", lambda e: e.dma_start(
                out=HF[:, :, :], in_=hfin.rearrange("p (a t) -> p a t", a=2)))
            C.dma("gpsimd", "lw", lambda e: e.dma_start(out=CAR[:, :], in_=carry))
            C.dma("gpsimd", "lw", lambda e: e.dma_start(out=SSP[:, :], in_=sspar))
            C.dma("gpsimd", "lw", lambda e: e.dma_start(out=CPA[:, :], in_=cpar))
            C.dma("gpsimd", "lw", lambda e: e.dma_start(
                out=WOT[:, :, :], in_=woutT.rearrange("p (a d) -> p a d", a=2)))
            C.dma("gpsimd", "lw", lambda e: e.dma_start(
                out=OWT[:, :, :], in_=outwT.rearrange("(c p) o -> p c o", p=128)))
            C.dma("gpsimd", "lw", lambda e: e.dma_start(out=OBC[:, :], in_=outb))
            emit_ln_const_init(C, lnt)
            C.op("gpsimd", "v", lambda e: e.iota(
                IOT[:, 0, :], pattern=[[-1, TOK]], base=-1, channel_multiplier=0,
                allow_small_or_imprecise_dtypes=True))
            C.op("gpsimd", "v", lambda e: e.iota(
                IOT[:, 1, :], pattern=[[1, TOK]], base=-TOK,
                channel_multiplier=0, allow_small_or_imprecise_dtypes=True))
            C.wait("scalar", "lw")
            C.wait("vector", "lw")
            C.wait("vector", "ld")
            for di in range(2):
                la, ldt = 3 * di, 3 * di + 2
                C.op("scalar", "s", lambda e, la=la: e.activation(
                    out=PRM[:, 0:1], in_=SSP[:, la:la + 1], func=AF.Exp))
                C.op("scalar", "s", lambda e, ldt=ldt: e.activation(
                    out=PRM[:, 1:2], in_=SSP[:, ldt:ldt + 1], func=AF.Exp))
                C.wait("vector", "s")
                if C.on("vector"):
                    i = C.eng.tensor_tensor(PRM[:, 2:3], PRM[:, 0:1],
                                            PRM[:, 1:2], OP.mult)
                else:
                    i = None
                C.tag_fenced("vector", i, "v")
                C.wait("scalar", "v")
                ram = IOT[:, di, :]
                C.op("scalar", "s", lambda e, ram=ram: e.activation(
                    out=POW[:, :], in_=ram, func=AF.Exp, scale=PRM[:, 2:3]))
                C.wait("vector", "s")
                if C.on("vector"):
                    i = C.eng.scalar_tensor_tensor(HF[:, di, :], POW[:, :],
                                                   CAR[:, di:di + 1],
                                                   HF[:, di, :], OP.mult, OP.add)
                else:
                    i = None
                C.tag_fenced("vector", i, "v")
                if C.on("vector"):
                    i = C.eng.tensor_scalar(HTB[:, di, :], HF[:, di, :],
                                            CPA[:, di:di + 1], None, OP.mult)
                    C.tag(i, "v")
                else:
                    C.inc("v")
                if dbg:
                    C.wait("sync", "v")
                    C.dma("sync", "st", lambda e, di=di: e.dma_start(
                        out=d_pow.rearrange("p (a t) -> p a t", a=2)[:, di, :],
                        in_=POW[:, :]))
                    C.dma("sync", "st", lambda e, di=di: e.dma_start(
                        out=d_hf.rearrange("p (a t) -> p a t", a=2)[:, di, :],
                        in_=HF[:, di, :]))
            C.wait("tensor", "lw")
            C.wait("tensor", "v")
            for g in range(TOK // TG):
                gsl = slice(g * TG, (g + 1) * TG)
                C.wait("sync", "s")
                C.wait("sync", "m")
                C.wait("sync", "v")
                C.dma("sync", "ld", lambda e: e.dma_start(
                    out=GATE[:, :, :], in_=gir[:, :, gsl]))
                C.dma("sync", "ld", lambda e: e.dma_start(
                    out=TIN[:, :, :], in_=tir[:, :, gsl]))
                C.dma("sync", "ld", lambda e: e.dma_start(
                    out=XIN[:, :, :], in_=xir[:, :, gsl]))
                C.wait("tensor", "ld")
                C.wait("vector", "ld")
                for oc in range(NC_D):
                    for (tt0, tw) in _tiles(TG):
                        if C.on("tensor"):
                            C.eng.matmul(psY[:, 0:tw],
                                         WOT[:, 0, oc * 128:(oc + 1) * 128],
                                         HTB[:, 0, g * TG + tt0:g * TG + tt0 + tw],
                                         start=True, stop=False)
                            i = C.eng.matmul(psY[:, 0:tw],
                                             WOT[:, 1, oc * 128:(oc + 1) * 128],
                                             HTB[:, 1, g * TG + tt0:g * TG + tt0 + tw],
                                             start=False, stop=True)
                            C.tag(i, "m")
                        else:
                            C.inc("m")
                        C.wait("vector", "m")
                        if C.on("vector"):
                            i = C.eng.tensor_tensor(
                                YGB[:, oc, tt0:tt0 + tw], psY[:, 0:tw],
                                GATE[:, oc, tt0:tt0 + tw], OP.mult)
                            C.tag(i, "v")
                        else:
                            C.inc("v")
                        C.wait("tensor", "v")
                for oc in range(NC_D):
                    for (tt0, tw) in _tiles(TG):
                        if C.on("tensor"):
                            for c in range(NC_D):
                                i = C.eng.matmul(psO[:, 0:tw],
                                                 OWT[:, c, oc * 128:(oc + 1) * 128],
                                                 YGB[:, c, tt0:tt0 + tw],
                                                 start=(c == 0),
                                                 stop=(c == NC_D - 1))
                            C.tag(i, "m")
                        else:
                            C.inc("m")
                        C.wait("vector", "m")
                        if C.on("vector"):
                            i = C.eng.scalar_tensor_tensor(
                                Y[:, oc, tt0:tt0 + tw], psO[:, 0:tw],
                                OBC[:, oc:oc + 1], TIN[:, oc, tt0:tt0 + tw],
                                OP.add, OP.add)
                            C.tag(i, "v")
                        else:
                            C.inc("v")
                        C.wait("tensor", "v")
                        if dbg:
                            C.wait("sync", "v")
                            C.dma("sync", "st", lambda e, oc=oc, tt0=tt0, tw=tw:
                                  e.dma_start(
                                      out=out.rearrange(
                                          "(c p) t -> p c t", p=128)[
                                          :, oc, gsl][:, tt0:tt0 + tw]
                                      if False else
                                      d_y.rearrange("(c p) t -> p c t", p=128)[
                                          :, oc, gsl][:, tt0:tt0 + tw],
                                      in_=Y[:, oc, tt0:tt0 + tw]))
                C.wait("scalar", "v")

                def apply(C, ti):
                    sl = slice(ti * 512, (ti + 1) * 512)
                    if C.on("vector"):
                        C.eng.tensor_tensor(OB[:, :, :], Y[:, :, sl],
                                            bcN(lnt["ps_R"], NC_D), OP.mult)
                        C.eng.tensor_tensor(OB[:, :, :], OB[:, :, :],
                                            bcN(lnt["ps_M"], NC_D), OP.subtract)
                        i = C.eng.tensor_tensor(OB[:, :, :], OB[:, :, :],
                                                XIN[:, :, sl], OP.add)
                        C.tag(i, "v")
                    else:
                        C.inc("v")
                    C.wait("sync", "v")
                    C.dma("sync", "st", lambda e: e.dma_start(
                        out=outr[:, :, gsl][:, :, sl], in_=OB[:, :, :]))
                    C.wait("vector", "st")

                emit_ln_fm(C, lnt, Y, TG, apply)
            C.wait("sync", "st")

        run_program(nc, ["ld", "lw", "v", "s", "m", "fv", "st"], prog)
    return nc



DH1 = 4096
NCH = DH1 // 128


def build_l1f_a():
    nc = bass.Bass("TRN2", target_bir_lowering=False, debug=False,
                   detect_race_conditions=RACE_DETECT)
    xin = nc.dram_tensor("xin", [D, TOK], F32, kind="ExternalInput").ap()
    w1T = nc.dram_tensor("w1T", [D, DH1], F32, kind="ExternalInput").ap()
    w2T = nc.dram_tensor("w2T", [D, DH1], F32, kind="ExternalInput").ap()
    s1c = nc.dram_tensor("s1c", [128, NCH], F32, kind="ExternalInput").ap()
    s2c = nc.dram_tensor("s2c", [128, NCH], F32, kind="ExternalInput").ap()
    y1o = nc.dram_tensor("y1o", [DH1, TOK], BF16, kind="ExternalOutput").ap()
    TG = 512
    es = ExitStack()
    with es:
        XIN = es.enter_context(nc.sbuf_tensor("XIN", [128, NC_D, TG], F32))
        ZB = es.enter_context(nc.sbuf_tensor("ZB", [128, NC_D, TOK], BF16))
        WA = es.enter_context(nc.sbuf_tensor("WA", [128, 4, NCH, 128], BF16))
        SIL = es.enter_context(nc.sbuf_tensor("SIL", [128, 2, 512], BF16))
        OUTB = es.enter_context(nc.sbuf_tensor("OUTB", [128, 2, 512], BF16))
        S1 = es.enter_context(nc.sbuf_tensor("S1", [128, NCH], F32))
        S2 = es.enter_context(nc.sbuf_tensor("S2", [128, NCH], F32))
        lnt = alloc_ln_scratch(es, nc, NC_D, TG)
        psA = es.enter_context(nc.psum_tensor("psA", [128, 512], F32))
        psG = es.enter_context(nc.psum_tensor("psG", [128, 512], F32))
        psA2 = es.enter_context(nc.psum_tensor("psA2", [128, 512], F32))
        psG2 = es.enter_context(nc.psum_tensor("psG2", [128, 512], F32))

        def prog(C):
            xir = xin.rearrange("(c p) t -> p c t", p=128)
            y1r = y1o.rearrange("(c p) t -> p c t", p=128)
            for dst, srcap in ((S1, s1c), (S2, s2c)):
                C.dma("gpsimd", "lw",
                      lambda e, dst=dst, srcap=srcap: e.dma_start(
                          out=dst[:, :], in_=srcap))
            emit_ln_const_init(C, lnt)
            C.wait("scalar", "lw")
            C.wait("vector", "lw")
            for g in range(TOK // TG):
                gsl = slice(g * TG, (g + 1) * TG)
                C.wait("sync", "v")
                C.wait("sync", "m")
                C.wait("sync", "s")
                C.dma("sync", "ld", lambda e: e.dma_start(
                    out=XIN[:, :, :], in_=xir[:, :, gsl]))
                C.wait("tensor", "ld")
                C.wait("scalar", "ld")
                C.wait("vector", "ld")

                def applyZ(C, ti):
                    if C.on("vector"):
                        C.eng.tensor_tensor(lnt["x2t"][:, :, :], XIN[:, :, :],
                                            bcN(lnt["ps_R"], NC_D), OP.mult)
                        i = C.eng.tensor_tensor(ZB[:, :, gsl],
                                                lnt["x2t"][:, :, :],
                                                bcN(lnt["ps_M"], NC_D),
                                                OP.subtract)
                        C.tag(i, "v")
                    else:
                        C.inc("v")
                    C.wait("tensor", "v")

                emit_ln_fm(C, lnt, XIN, TG, applyZ)

            mm_done = {}
            vev = {}

            def prefetch(oc):
                p = oc % 2
                if oc >= 2:
                    C.wait("gpsimd", "m", mm_done[oc - 2])
                C.dma("gpsimd", f"lw{p}", lambda e, oc=oc, p=p: e.dma_start(
                    out=WA[:, 2 * p, :, :][:, 0:NC_D, :],
                    in_=w1T.rearrange("(c p) o -> p c o", p=128)[
                        :, :, oc * 128:(oc + 1) * 128]))
                C.dma("gpsimd", f"lw{p}", lambda e, oc=oc, p=p: e.dma_start(
                    out=WA[:, 2 * p + 1, :, :][:, 0:NC_D, :],
                    in_=w2T.rearrange("(c p) o -> p c o", p=128)[
                        :, :, oc * 128:(oc + 1) * 128]))

            nst = []
            prefetch(0)
            prefetch(1)
            pA = [psA, psA2]
            pG = [psG, psG2]
            for oc in range(NCH):
                p = oc % 2
                C.wait("tensor", f"lw{p}", 32 * (oc // 2 + 1))
                for tj, (tt0, tw) in enumerate(_tiles(TOK)):
                    gi = oc * len(_tiles(TOK)) + tj
                    ob = gi % 2
                    if len(nst) >= 2:
                        C.wait("vector", f"st{ob}", nst[-2])
                    C.wait("tensor", "v", vev.get(gi - 2, 0))
                    if C.on("tensor"):
                        for c in range(NC_D):
                            C.eng.matmul(pA[ob][:, 0:tw], WA[:, 2 * p, c, :],
                                         ZB[:, c, tt0:tt0 + tw], start=(c == 0),
                                         stop=(c == NC_D - 1))
                        for c in range(NC_D):
                            i = C.eng.matmul(pG[ob][:, 0:tw],
                                             WA[:, 2 * p + 1, c, :],
                                             ZB[:, c, tt0:tt0 + tw],
                                             start=(c == 0),
                                             stop=(c == NC_D - 1))
                        C.tag(i, "m")
                    else:
                        C.inc("m")
                    mm_done[oc] = C.cnt["m"]
                    C.wait("scalar", "m")
                    C.op("scalar", "s", lambda e, oc=oc, tw=tw, ob=ob:
                         e.activation(out=SIL[:, ob, 0:tw], in_=pA[ob][:, 0:tw],
                                      func=AF.Silu, scale=S1[:, oc:oc + 1]))
                    C.wait("vector", "s")
                    if C.on("vector"):
                        C.eng.tensor_scalar(pG[ob][:, 0:tw], pG[ob][:, 0:tw],
                                            S2[:, oc:oc + 1], None, OP.mult)
                        i = C.eng.tensor_tensor(OUTB[:, ob, 0:tw],
                                                pG[ob][:, 0:tw],
                                                SIL[:, ob, 0:tw], OP.mult)
                        C.tag(i, "v")
                    else:
                        C.inc("v")
                    vev[gi] = C.cnt["v"]
                    C.wait("sync", "v")
                    C.dma("sync", f"st{ob}", lambda e, oc=oc, tt0=tt0, tw=tw,
                          ob=ob: e.dma_start(out=y1r[:, oc, tt0:tt0 + tw],
                                             in_=OUTB[:, ob, 0:tw]))
                    nst.append(C.cnt[f"st{ob}"])
                if oc + 2 < NCH:
                    prefetch(oc + 2)
            C.wait("sync", "st0")
            C.wait("sync", "st1")

        run_program(nc, ["ld", "lw", "lw0", "lw1", "v", "s", "m", "fv",
                         "st0", "st1"], prog)
    return nc


def build_l1f_b():
    nc = bass.Bass("TRN2", target_bir_lowering=False, debug=False,
                   detect_race_conditions=RACE_DETECT)
    y1i = nc.dram_tensor("y1i", [DH1, TOK], BF16, kind="ExternalInput").ap()
    h2T = nc.dram_tensor("h2T", [DH1, DH1], F32, kind="ExternalInput").ap()
    s4c = nc.dram_tensor("s4c", [128, NCH], F32, kind="ExternalInput").ap()
    y2o = nc.dram_tensor("y2o", [DH1, TOK], BF16, kind="ExternalOutput").ap()
    es = ExitStack()
    with es:
        HB = es.enter_context(nc.sbuf_tensor("HB", [128, NCH, TOK], BF16))
        WA = es.enter_context(nc.sbuf_tensor("WA", [128, 2, NCH, 128], BF16))
        OUTB = es.enter_context(nc.sbuf_tensor("OUTB", [128, 2, 512], BF16))
        S4 = es.enter_context(nc.sbuf_tensor("S4", [128, NCH], F32))
        psA = es.enter_context(nc.psum_tensor("psA", [128, 512], F32))
        psB2 = es.enter_context(nc.psum_tensor("psB2", [128, 512], F32))

        def prog(C):
            y1r = y1i.rearrange("(c p) t -> p c t", p=128)
            y2r = y2o.rearrange("(c p) t -> p c t", p=128)
            for q4 in range(4):
                C.dma("sync", "ld", lambda e, q4=q4: e.dma_start(
                    out=HB[:, q4 * 8:(q4 + 1) * 8, :],
                    in_=y1r[:, q4 * 8:(q4 + 1) * 8, :]))
            C.dma("gpsimd", "lw", lambda e: e.dma_start(out=S4[:, :], in_=s4c))
            C.wait("scalar", "lw")
            C.wait("tensor", "ld")
            nst = []
            mm_done = {}
            sevac = {}
            psb = [psA[:, :], psB2[:, :]]

            def prefetch(oc):
                if oc >= 2:
                    C.wait("gpsimd", "m", mm_done[oc - 2])
                C.dma("gpsimd", f"lw{oc % 2}", lambda e, oc=oc: e.dma_start(
                    out=WA[:, oc % 2, :, :],
                    in_=h2T.rearrange("(c p) o -> p c o", p=128)[
                        :, :, oc * 128:(oc + 1) * 128]))

            prefetch(0)
            prefetch(1)
            for oc in range(NCH):
                buf = oc % 2
                C.wait("tensor", f"lw{buf}", 16 * (oc // 2 + 1))
                for tj, (tt0, tw) in enumerate(_tiles(TOK)):
                    ob = (oc * len(_tiles(TOK)) + tj) % 2
                    pb = psb[ob]
                    if len(nst) >= 2:
                        C.wait("scalar", f"st{ob}", nst[-2])
                    if oc >= 1 or tj >= 2:
                        C.wait("tensor", "s", sevac.get((oc, tj - 2),
                                                        sevac.get((oc - 1,
                                                                   tj + 2),
                                                                  0)))
                    if C.on("tensor"):
                        for c in range(NCH):
                            i = C.eng.matmul(pb[:, 0:tw], WA[:, buf, c, :],
                                             HB[:, c, tt0:tt0 + tw],
                                             start=(c == 0), stop=(c == NCH - 1))
                        C.tag(i, "m")
                    else:
                        C.inc("m")
                    mm_done[oc] = C.cnt["m"]
                    C.wait("scalar", "m")
                    C.op("scalar", "s", lambda e, tw=tw, oc=oc, ob=ob, pb=pb:
                         e.activation(out=OUTB[:, ob, 0:tw], in_=pb[:, 0:tw],
                                      func=AF.Gelu, scale=S4[:, oc:oc + 1]))
                    sevac[(oc, tj)] = C.cnt["s"]
                    C.wait("sync", "s")
                    C.dma("sync", f"st{ob}", lambda e, oc=oc, tt0=tt0, tw=tw,
                          ob=ob: e.dma_start(
                              out=y2r[:, oc, tt0:tt0 + tw],
                              in_=OUTB[:, ob, 0:tw]))
                    nst.append(C.cnt[f"st{ob}"])
                if oc + 2 < NCH:
                    prefetch(oc + 2)
            C.wait("sync", "st0")
            C.wait("sync", "st1")

        run_program(nc, ["ld", "lw", "lw0", "lw1", "v", "s", "m", "fv",
                         "st0", "st1"], prog)
    return nc


def build_l1f_c():
    nc = bass.Bass("TRN2", target_bir_lowering=False, debug=False,
                   detect_race_conditions=RACE_DETECT)
    y2i = nc.dram_tensor("y2i", [DH1, TOK], BF16, kind="ExternalInput").ap()
    w3T = nc.dram_tensor("w3T", [DH1, D], F32, kind="ExternalInput").ap()
    s5c = nc.dram_tensor("s5c", [128, NC_D], F32, kind="ExternalInput").ap()
    xin = nc.dram_tensor("xin", [D, TOK], F32, kind="ExternalInput").ap()
    out = nc.dram_tensor("out", [D, TOK], F32, kind="ExternalOutput").ap()
    TG = 512
    es = ExitStack()
    with es:
        HB = es.enter_context(nc.sbuf_tensor("HB", [128, NCH, TOK], BF16))
        WA = es.enter_context(nc.sbuf_tensor("WA", [128, 2, NCH, 128], BF16))
        S5 = es.enter_context(nc.sbuf_tensor("S5", [128, NC_D], F32))
        Y = es.enter_context(nc.sbuf_tensor("Y", [128, NC_D, TG], F32))
        XIN = es.enter_context(nc.sbuf_tensor("XIN", [128, NC_D, TG], F32))
        lnt = alloc_ln_scratch(es, nc, NC_D, TG)
        psA = es.enter_context(nc.psum_tensor("psA", [128, 512], F32))
        psA2c = es.enter_context(nc.psum_tensor("psA2c", [128, 512], F32))

        def prog(C):
            PAB = [psA, psA2c]
            MMD, VEV = {}, {}
            y2r = y2i.rearrange("(c p) t -> p c t", p=128)
            xir = xin.rearrange("(c p) t -> p c t", p=128)
            outr = out.rearrange("(c p) t -> p c t", p=128)
            for q4 in range(4):
                C.dma("sync", "ld", lambda e, q4=q4: e.dma_start(
                    out=HB[:, q4 * 8:(q4 + 1) * 8, :],
                    in_=y2r[:, q4 * 8:(q4 + 1) * 8, :]))
            C.dma("gpsimd", "lw", lambda e: e.dma_start(out=S5[:, :], in_=s5c))
            emit_ln_const_init(C, lnt)
            C.wait("vector", "lw")
            C.wait("tensor", "ld")
            for g in range(TOK // TG):
                gsl = slice(g * TG, (g + 1) * TG)
                C.wait("sync", "v")
                C.wait("sync", "m")
                C.wait("sync", "s")
                C.dma("sync", "ld", lambda e: e.dma_start(
                    out=XIN[:, :, :], in_=xir[:, :, gsl]))
                C.wait("vector", "ld")
                for oc in range(NC_D):
                    gi = g * NC_D + oc
                    buf = gi % 2
                    if gi == 0:
                        for pre in (0, 1):
                            C.dma("gpsimd", f"lw{pre % 2}",
                                  lambda e, pre=pre: e.dma_start(
                                      out=WA[:, pre % 2, :, :],
                                      in_=w3T.rearrange("(c p) o -> p c o",
                                                        p=128)[
                                          :, :, pre * 128:(pre + 1) * 128]))
                    C.wait("tensor", f"lw{buf}", 16 * (gi // 2 + 1))
                    for (tt0, tw) in _tiles(TG):
                        C.wait("tensor", "v", VEV.get(gi - 2, 0))
                        if C.on("tensor"):
                            for c in range(NCH):
                                i = C.eng.matmul(
                                    PAB[buf][:, 0:tw], WA[:, buf, c, :],
                                    HB[:, c, g * TG + tt0:g * TG + tt0 + tw],
                                    start=(c == 0), stop=(c == NCH - 1))
                            C.tag(i, "m")
                        else:
                            C.inc("m")
                        MMD[gi] = C.cnt["m"]
                        C.wait("vector", "m")
                        if C.on("vector"):
                            i = C.eng.tensor_scalar(Y[:, oc, tt0:tt0 + tw],
                                                    PAB[buf][:, 0:tw],
                                                    S5[:, oc:oc + 1], None,
                                                    OP.mult)
                            C.tag(i, "v")
                        else:
                            C.inc("v")
                        VEV[gi] = C.cnt["v"]
                    ni = gi + 2
                    if ni < 4 * NC_D:
                        noc = ni % NC_D
                        C.wait("gpsimd", "m", MMD[gi])
                        C.dma("gpsimd", f"lw{ni % 2}",
                              lambda e, noc=noc: e.dma_start(
                                  out=WA[:, noc % 2, :, :],
                                  in_=w3T.rearrange("(c p) o -> p c o", p=128)[
                                      :, :, noc * 128:(noc + 1) * 128]))

                def apply(C, ti):
                    sl = slice(ti * 512, (ti + 1) * 512)
                    if C.on("vector"):
                        C.eng.tensor_tensor(Y[:, :, sl], Y[:, :, sl],
                                            bcN(lnt["ps_R"], NC_D), OP.mult)
                        C.eng.tensor_tensor(Y[:, :, sl], Y[:, :, sl],
                                            bcN(lnt["ps_M"], NC_D), OP.subtract)
                        i = C.eng.tensor_tensor(Y[:, :, sl], Y[:, :, sl],
                                                XIN[:, :, sl], OP.add)
                        C.tag(i, "v")
                    else:
                        C.inc("v")
                    C.wait("sync", "v")
                    C.dma("sync", "st", lambda e: e.dma_start(
                        out=outr[:, :, gsl][:, :, sl], in_=Y[:, :, sl]))
                    C.wait("vector", "st")

                emit_ln_fm(C, lnt, Y, TG, apply)
            C.wait("sync", "st")

        run_program(nc, ["ld", "lw", "lw0", "lw1", "v", "s", "m", "fv", "st"],
                    prog)
    return nc



_PROG_CACHE = {}


def _get(name, builder):
    if name not in _PROG_CACHE:
        _PROG_CACHE[name] = builder()
    return _PROG_CACHE[name]


def _bf16(x):
    import ml_dtypes
    return np.asarray(x).astype(ml_dtypes.bfloat16)


def _tri_mask():
    return _bf16(np.tril(np.ones((128, 128), np.float32)).T)


def l0_attn_forward(x, Wq, bq, Wk, bk, Wv, bv, Wo, bo):
    import ml_dtypes
    xT = [np.ascontiguousarray(x[b].T) for b in range(B)]
    wcat = np.concatenate([
        np.concatenate([np.asarray(Wq).T, np.asarray(Wk).T, np.asarray(Wv).T],
                       axis=1),
        np.concatenate([bq, bk, bv])[None, :]], axis=0)
    wcat = np.ascontiguousarray(wcat, np.float32)

    nc1 = _get("l0a_proj", build_l0a_proj)
    in_maps = []
    for c in range(NCORES):
        b, h = divmod(c, 2)
        in_maps.append({
            "xo": np.ascontiguousarray(xT[b][:, h * TOK:(h + 1) * TOK]),
            "wc": wcat,
        })
    res = run_bass_kernel_spmd(nc1, in_maps, core_ids=list(range(NCORES)))
    qkv = [np.concatenate([res.results[2 * b]["qkv"], res.results[2 * b + 1]["qkv"]],
                          axis=0) for b in range(B)]

    nc2 = _get("l0a_attn", build_l0a_attn)
    tri = _tri_mask()
    in_maps = []
    for c in range(NCORES):
        b, hh = divmod(c, 2)
        hs = slice(hh * 8 * DH, (hh + 1) * 8 * DH)
        q = qkv[b][:, 0 * D:1 * D][:, hs].reshape(S, 8, DH)
        k = qkv[b][:, 1 * D:2 * D][:, hs].reshape(S, 8, DH)
        v = qkv[b][:, 2 * D:3 * D][:, hs].reshape(S, 8, DH)
        ones = np.ones((8, S, 1), ml_dtypes.bfloat16)
        in_maps.append({
            "qf": np.ascontiguousarray(q.transpose(1, 2, 0)),
            "kf": np.ascontiguousarray(k.transpose(1, 2, 0)),
            "kt": np.ascontiguousarray(k.transpose(1, 0, 2)),
            "vt": np.ascontiguousarray(
                np.concatenate([v.transpose(1, 0, 2), ones], axis=2)),
            "tri": tri})
    res = run_bass_kernel_spmd(nc2, in_maps, core_ids=list(range(NCORES)))
    at = []
    for b in range(B):
        ab = np.concatenate([res.results[2 * b]["at"], res.results[2 * b + 1]["at"]],
                            axis=0)
        at.append(np.ascontiguousarray(ab.transpose(1, 0, 2).reshape(S, D)))

    nc3 = _get("l0a_out", build_l0a_out)
    woT = np.ascontiguousarray(np.asarray(Wo).T, np.float32)
    bo_r = np.ascontiguousarray(np.asarray(bo).reshape(NC_D, 128).T, np.float32)
    in_maps = []
    for c in range(NCORES):
        b, h = divmod(c, 2)
        in_maps.append({
            "af": np.ascontiguousarray(at[b][h * TOK:(h + 1) * TOK, :].T),
            "wo": woT,
            "bo": bo_r,
            "xo": np.ascontiguousarray(xT[b][:, h * TOK:(h + 1) * TOK]),
        })
    res = run_bass_kernel_spmd(nc3, in_maps, core_ids=list(range(NCORES)))
    out = np.empty((B, S, D), np.float32)
    for c in range(NCORES):
        b, h = divmod(c, 2)
        out[b, h * TOK:(h + 1) * TOK, :] = res.results[c]["out"].T
    return out


INNER_W = 256


def _l0f_tables(o0):
    d = np.arange(128) % 64
    half = (d % 32).astype(np.float64)
    inv_freq = 1.0 / (10000.0 ** (2 * half / 64.0))
    base_scale = (2 * half + 0.4 * 64) / (1.4 * 64)
    g = np.arange(o0 - 384, o0 + TOK, dtype=np.float64)
    freqs = g[None, :] * inv_freq[:, None]
    power = (g - S // 2) / (INNER_W // 2)
    scale = base_scale[:, None] ** power[None, :]
    cos, sin = np.cos(freqs), np.sin(freqs)
    return (cos * scale).astype(np.float32), (sin * scale).astype(np.float32), \
           (cos / scale).astype(np.float32), (sin / scale).astype(np.float32)


def _rho_sigma():
    d = np.arange(128) % 64
    rho = np.where(d < 32, np.arange(128) + 32, np.arange(128) - 32)
    sigma = np.where(d < 32, -1.0, 1.0)
    return rho, sigma


def _l0f_attn_consts():
    rho, sigma = _rho_sigma()
    R = np.zeros((128, 128), np.float32)
    for op in range(128):
        R[rho[op], op] = sigma[op]
    blk = np.zeros((128, 128), np.float32)
    blk[:64, :64] = 1.0
    blk[64:, 64:] = 1.0
    idm = np.eye(128, dtype=np.float32)
    return R, blk, idm


def _l0f_masks(h):
    NEG = -1e38
    il = np.arange(128)[:, None]
    jl = np.arange(384)[None, :] % 128
    blkid = np.arange(384)[None, :] // 128
    interior = np.zeros((128, 384), np.float32)
    interior[(blkid == 0) & (jl < il)] = NEG
    interior[(blkid == 2) & (jl > il)] = NEG
    m3 = interior.copy()
    m4 = interior.copy()
    if h == 0:
        m3[blkid.repeat(128, 0) <= 1] = NEG
        m4[blkid.repeat(128, 0) == 0] = NEG
    return np.stack([m3, m4, interior])


def l0_ffn_forward(x, Wqkv, q_scale, k_scale, Wo, ln_g, ln_b, W1, b1, W2, b2, nlayers=6):
    nc_a = _get("l0f_attn", build_l0f_attn)
    nc_f = _get("l0f_ffn", build_l0f_ffn)
    rho, _ = _rho_sigma()
    R, blkm, idm = _l0f_attn_consts()
    xT = [np.ascontiguousarray(x[b].T, np.float32) for b in range(B)]
    DIM_INNER = 2730
    for li in range(nlayers):
        qs = np.asarray(q_scale[li], np.float32)
        ks = np.asarray(k_scale[li], np.float32)
        d64 = np.arange(128) % 64
        qs_c = qs[d64]
        qs_r = qs[d64[rho]]
        ks_c = ks[d64]
        ks_r = ks[d64[rho]]
        wq = np.ascontiguousarray(Wqkv[li], np.float32)
        woL = np.ascontiguousarray(
            np.asarray(Wo[li], np.float32).reshape(NC_D, 128, NC_D, 128)
            .transpose(2, 1, 0, 3).reshape(NC_D, 128, NC_D * 128))
        in_maps = []
        for c in range(NCORES):
            b, h = divmod(c, 2)
            o0 = h * TOK
            ext = np.zeros((D, EXT), np.float32)
            lo = max(0, o0 - 384)
            ext[:, 384 - (o0 - lo):384] = xT[b][:, lo:o0]
            ext[:, 384:] = xT[b][:, o0:o0 + TOK]
            c1, s1, c2, s2 = _l0f_tables(o0)
            in_maps.append({
                "xin": ext, "wqkv": wq, "wo": woL,
                "tq1": np.ascontiguousarray(8.0 * c1 * qs_c[:, None]),
                "tq2": np.ascontiguousarray(8.0 * s1 * qs_r[:, None]),
                "tk1": np.ascontiguousarray(c2 * ks_c[:, None]),
                "tk2": np.ascontiguousarray(s2 * ks_r[:, None]),
                "rotm": R, "blk": blkm, "idm": idm,
                "mneg": _l0f_masks(h)})
        res = run_bass_kernel_spmd(nc_a, in_maps, core_ids=list(range(NCORES)))
        for c in range(NCORES):
            b, h = divmod(c, 2)
            xT[b][:, h * TOK:(h + 1) * TOK] = res.results[c]["xout"]
        W1a = np.zeros((D, NHC * 128), np.float32)
        W1g = np.zeros((D, NHC * 128), np.float32)
        W1a[:, :DIM_INNER] = W1[li][:, :DIM_INNER]
        W1g[:, :DIM_INNER] = W1[li][:, DIM_INNER:]
        w1iL = np.zeros((D, NHC * 256), np.float32)
        for hc in range(NHC):
            w1iL[:, hc * 256:hc * 256 + 128] = W1a[:, hc * 128:(hc + 1) * 128]
            w1iL[:, hc * 256 + 128:hc * 256 + 256] = W1g[:, hc * 128:(hc + 1) * 128]
        b1aP = np.zeros(NHC * 128, np.float32)
        b1gP = np.zeros(NHC * 128, np.float32)
        b1aP[:DIM_INNER] = b1[li][:DIM_INNER]
        b1gP[:DIM_INNER] = b1[li][DIM_INNER:]
        b1cL = np.zeros((128, NHC, 2), np.float32)
        for hc in range(NHC):
            b1cL[:, hc, 0] = b1aP[hc * 128:(hc + 1) * 128]
            b1cL[:, hc, 1] = b1gP[hc * 128:(hc + 1) * 128]
        b1cL = np.ascontiguousarray(b1cL.reshape(128, NHC * 2))
        w2P = np.zeros((NHC * 128, D), np.float32)
        w2P[:DIM_INNER] = W2[li]
        w2P = np.ascontiguousarray(
            w2P.reshape(NHC, 128, NC_D, 128).transpose(2, 1, 0, 3)
            .reshape(NC_D, 128, NHC * 128))
        b2cL = np.ascontiguousarray(np.asarray(b2[li], np.float32).reshape(NC_D, 128).T)
        glnL = np.ascontiguousarray(np.asarray(ln_g[li], np.float32).reshape(NC_D, 128).T)
        blnL = np.ascontiguousarray(np.asarray(ln_b[li], np.float32).reshape(NC_D, 128).T)
        in_maps = []
        for c in range(NCORES):
            b, h = divmod(c, 2)
            o0 = h * TOK
            ext = np.zeros((D, TOK + 1), np.float32)
            if o0 > 0:
                ext[:, 0] = xT[b][:, o0 - 1]
            ext[:, 1:] = xT[b][:, o0:o0 + TOK]
            in_maps.append({
                "xin": ext, "w1i": w1iL, "b1c": b1cL, "w2": w2P, "b2c": b2cL,
                "gln": glnL, "bln": blnL})
        res = run_bass_kernel_spmd(nc_f, in_maps, core_ids=list(range(NCORES)))
        for c in range(NCORES):
            b, h = divmod(c, 2)
            xT[b][:, h * TOK:(h + 1) * TOK] = res.results[c]["xout"]
    out = np.stack([xT[b].T for b in range(B)])
    return np.ascontiguousarray(out)


def sandwich_pre(xT):
    nc = _get("pre_ln", build_pre_ln)
    in_maps = []
    for c in range(NCORES):
        b, h = divmod(c, 2)
        in_maps.append({"xo": np.ascontiguousarray(
            xT[b][:, h * TOK:(h + 1) * TOK])})
    res = run_bass_kernel_spmd(nc, in_maps, core_ids=list(range(NCORES)))
    return [np.concatenate([res.results[2 * b]["out"],
                            res.results[2 * b + 1]["out"]], axis=1)
            for b in range(B)]


def sandwich_post(yT, xT):
    nc = _get("post_ln", build_post_ln)
    in_maps = []
    for c in range(NCORES):
        b, h = divmod(c, 2)
        sl = slice(h * TOK, (h + 1) * TOK)
        in_maps.append({"yo": np.ascontiguousarray(yT[b][:, sl]),
                        "xo": np.ascontiguousarray(xT[b][:, sl])})
    res = run_bass_kernel_spmd(nc, in_maps, core_ids=list(range(NCORES)))
    return [np.concatenate([res.results[2 * b]["out"],
                            res.results[2 * b + 1]["out"]], axis=1)
            for b in range(B)]


def _cols(v, n):
    return np.ascontiguousarray(np.asarray(v, np.float32).reshape(n, 128).T)


def l1_attn_forward(xT, inp):
    nc_a = _get("l1a_A", build_l1a_A)
    nc_b = _get("l1a_B", build_l1a_B)
    nsc = _cols(inp["L1_attn_norm_scale"], NC_D)
    inwT = np.ascontiguousarray(np.asarray(inp["L1_attn_in_w"], np.float32).T)
    inb = _cols(inp["L1_attn_in_b"], 16)
    dww = np.ascontiguousarray(np.asarray(inp["L1_attn_dw_w"], np.float32)
                               .reshape(NC_D, 128, 3).transpose(1, 0, 2)
                               .reshape(128, NC_D * 3))
    dwb = _cols(inp["L1_attn_dw_b"], NC_D)
    pwwT = np.ascontiguousarray(np.asarray(inp["L1_attn_pw_w"], np.float32).T)
    pwb = _cols(inp["L1_attn_pw_b"], NC_D)
    winf = np.ascontiguousarray(np.asarray(inp["L1_attn_f_win"], np.float32).T)
    winb = np.ascontiguousarray(np.asarray(inp["L1_attn_b_win"], np.float32).T)
    sspar = np.stack([
        np.asarray(inp["L1_attn_f_logA"], np.float32),
        np.asarray(inp["L1_attn_f_B"], np.float32),
        np.full(64, float(inp["L1_attn_f_logdt"]), np.float32),
        np.asarray(inp["L1_attn_b_logA"], np.float32),
        np.asarray(inp["L1_attn_b_B"], np.float32),
        np.full(64, float(inp["L1_attn_b_logdt"]), np.float32)], axis=1)
    sspar = np.ascontiguousarray(sspar)
    in_maps = []
    for c in range(NCORES):
        b, h = divmod(c, 2)
        o0 = h * TOK
        ext = np.zeros((D, TOK + 2), np.float32)
        if o0 > 0:
            ext[:, 0] = xT[b][:, o0 - 1]
        ext[:, 1:TOK + 1] = xT[b][:, o0:o0 + TOK]
        if o0 + TOK < S:
            ext[:, TOK + 1] = xT[b][:, o0 + TOK]
        edge = np.ones((128, 2), np.float32)
        if h == 0:
            edge[:, 0] = 0.0
        if h == 1:
            edge[:, 1] = 0.0
        in_maps.append({"xo": ext, "nsc": nsc, "inw": inwT, "inb": inb,
                        "dww": dww, "dwb": dwb, "pww": pwwT, "pwb": pwb,
                        "winf": winf, "winb": winb, "sspar": sspar,
                        "edge": np.ascontiguousarray(edge)})
    res = run_bass_kernel_spmd(nc_a, in_maps, core_ids=list(range(NCORES)))
    rA = [res.results[c] for c in range(NCORES)]

    cpar = np.ascontiguousarray(np.stack(
        [np.asarray(inp["L1_attn_f_C"], np.float32),
         np.asarray(inp["L1_attn_b_C"], np.float32)], axis=1))
    woutT = np.ascontiguousarray(np.concatenate(
        [np.asarray(inp["L1_attn_f_wout"], np.float32).T,
         np.asarray(inp["L1_attn_b_wout"], np.float32).T], axis=1))
    outwT = np.ascontiguousarray(np.asarray(inp["L1_attn_out_w"], np.float32).T)
    outb = _cols(inp["L1_attn_out_b"], NC_D)
    in_maps = []
    for c in range(NCORES):
        b, h = divmod(c, 2)
        carry = np.zeros((64, 2), np.float32)
        if h == 1:
            carry[:, 0] = rA[2 * b]["hf_out"][:, TOK - 1]
        if h == 0:
            carry[:, 1] = rA[2 * b + 1]["hf_out"][:, TOK]
        in_maps.append({
            "hfin": rA[c]["hf_out"], "carry": np.ascontiguousarray(carry),
            "sspar": sspar, "cpar": cpar, "woutT": woutT, "outwT": outwT,
            "outb": outb, "gin": rA[c]["g_out"], "tin": rA[c]["t_out"],
            "xin": np.ascontiguousarray(xT[b][:, h * TOK:(h + 1) * TOK])})
    res = run_bass_kernel_spmd(nc_b, in_maps, core_ids=list(range(NCORES)))
    return [np.concatenate([res.results[2 * b]["out"],
                            res.results[2 * b + 1]["out"]], axis=1)
            for b in range(B)]


def l1_ffn_forward(xT, inp):
    nc_a = _get("l1f_a", build_l1f_a)
    nc_b = _get("l1f_b", build_l1f_b)
    nc_c = _get("l1f_c", build_l1f_c)
    NCH_ = NCH
    w1T = np.ascontiguousarray(np.asarray(inp["L1_ffn_w1_w"], np.float32).T)
    w2T = np.ascontiguousarray(np.asarray(inp["L1_ffn_w2_w"], np.float32).T)
    h1T = np.ascontiguousarray(np.asarray(inp["L1_ffn_h1_w"], np.float32).T)
    h2T = np.ascontiguousarray(np.asarray(inp["L1_ffn_h2_w"], np.float32).T)
    w3T = np.ascontiguousarray(np.asarray(inp["L1_ffn_w3_w"], np.float32).T)
    s1 = _cols(inp["L1_ffn_w1_s"], NCH_)
    s2 = _cols(inp["L1_ffn_w2_s"], NCH_)
    s3 = _cols(inp["L1_ffn_h1_s"], NCH_)
    s4 = _cols(inp["L1_ffn_h2_s"], NCH_)
    s5 = _cols(inp["L1_ffn_w3_s"], NC_D)
    xs = [np.ascontiguousarray(xT[c // 2][:, (c % 2) * TOK:(c % 2 + 1) * TOK])
          for c in range(NCORES)]
    in_maps = [{"xin": xs[c], "w1T": w1T, "w2T": w2T, "s1c": s1, "s2c": s2}
               for c in range(NCORES)]
    res = run_bass_kernel_spmd(nc_a, in_maps, core_ids=list(range(NCORES)))
    y1 = [res.results[c]["y1o"] for c in range(NCORES)]
    in_maps = [{"y1i": y1[c], "h2T": h1T if False else h1T, "s4c": s3}
               for c in range(NCORES)]
    res = run_bass_kernel_spmd(nc_b, in_maps, core_ids=list(range(NCORES)))
    y2 = [res.results[c]["y2o"] for c in range(NCORES)]
    in_maps = [{"y1i": y2[c], "h2T": h2T, "s4c": s4} for c in range(NCORES)]
    res = run_bass_kernel_spmd(nc_b, in_maps, core_ids=list(range(NCORES)))
    y3 = [res.results[c]["y2o"] for c in range(NCORES)]
    in_maps = [{"y2i": y3[c], "w3T": w3T, "s5c": s5, "xin": xs[c]}
               for c in range(NCORES)]
    res = run_bass_kernel_spmd(nc_c, in_maps, core_ids=list(range(NCORES)))
    return [np.concatenate([res.results[2 * b]["out"],
                            res.results[2 * b + 1]["out"]], axis=1)
            for b in range(B)]


def kernel(**inputs):
    x = np.asarray(inputs["L0_attn_x"], np.float32)
    x = l0_attn_forward(x, inputs["L0_attn_Wq"], inputs["L0_attn_bq"],
                        inputs["L0_attn_Wk"], inputs["L0_attn_bk"],
                        inputs["L0_attn_Wv"], inputs["L0_attn_bv"],
                        inputs["L0_attn_Wo"], inputs["L0_attn_bo"])
    xT = [np.ascontiguousarray(x[b].T) for b in range(B)]
    zT = sandwich_pre(xT)
    z = np.stack([zT[b].T for b in range(B)])
    z = l0_ffn_forward(z, inputs["L0_ffn_Wqkv"], inputs["L0_ffn_q_scale"],
                       inputs["L0_ffn_k_scale"], inputs["L0_ffn_Wo"],
                       inputs["L0_ffn_ln_g"], inputs["L0_ffn_ln_b"],
                       inputs["L0_ffn_W1"], inputs["L0_ffn_b1"],
                       inputs["L0_ffn_W2"], inputs["L0_ffn_b2"])
    zT = [np.ascontiguousarray(z[b].T) for b in range(B)]
    xT = sandwich_post(zT, xT)
    xT = l1_attn_forward(xT, inputs)
    xT = l1_ffn_forward(xT, inputs)
    out = np.stack([xT[b].T for b in range(B)])
    return np.ascontiguousarray(out)
